# version 04
# baseline (speedup 1.0000x reference)
import numpy as np
import concourse.bass as bass
import concourse.mybir as mybir
from concourse.bass_utils import run_bass_kernel_spmd
from concourse.alu_op_type import AluOpType as OP

F32 = mybir.dt.float32
BF16 = mybir.dt.bfloat16
AF = mybir.ActivationFunctionType
AX = mybir.AxisListType.X

TPC = 2048
NT = 4
TS = 512
NBLK = 4
D = 1024
NCD = 8
MLP = 4096
NE = 8

VG1, VB1, VG2, VB2, VBQ, VBO, VB2b = 0, 8, 16, 24, 32, 40, 48
VC_LN1B, VC_LN1S, VC_E5, VC_E6 = 56, 57, 58, 59


class Prog:

    def __init__(self):
        self.segs = [dict()]

    def _add(self, eng, fn, dma):
        self.segs[-1].setdefault(eng, []).append((fn, dma))

    def t(self, fn):
        self._add("tensor", fn, False)

    def v(self, fn):
        self._add("vector", fn, False)

    def a(self, fn):
        self._add("scalar", fn, False)

    def g(self, fn):
        self._add("gpsimd", fn, False)

    def dS(self, fn):
        self._add("sync", fn, True)

    def dG(self, fn):
        self._add("gpsimd", fn, True)

    def vdrain(self):
        self._add("vector", lambda e: e.drain(), False)

    def bar(self):
        if self.segs[-1]:
            self.segs.append(dict())

    def emit(self, block, sems):
        segs = [s for s in self.segs if s]
        n = len(segs)
        cum = {k: [0] * (n + 1) for k in sems if k != "x_vector_d"}
        for i, seg in enumerate(segs):
            for k in cum:
                cum[k][i + 1] = cum[k][i]
            for eng, ops in seg.items():
                ndma = sum(1 for _, d in ops if d)
                nex = sum(1 for _, d in ops if not d)
                if eng in ("sync", "gpsimd") and ndma:
                    cum["dma_" + eng][i + 1] += 16 * ndma
                if nex and eng != "sync":
                    cum["x_" + eng][i + 1] += 1

        def emit_engine(eng_name, eng):
            last_wait = {k: 0 for k in cum}
            for i, seg in enumerate(segs):
                ops = seg.get(eng_name)
                if not ops:
                    continue
                for k in cum:
                    if k == "x_" + eng_name and eng_name != "gpsimd":
                        continue
                    tgt = cum[k][i]
                    if tgt > last_wait[k]:
                        eng.wait_ge(sems[k], tgt)
                        last_wait[k] = tgt
                last_exec = None
                for fn, dma_ in ops:
                    instr = fn(eng)
                    if dma_:
                        instr.then_inc(sems["dma_" + eng_name], 16)
                    else:
                        last_exec = instr
                if last_exec is not None and eng_name != "sync":
                    last_exec.then_inc(sems["x_" + eng_name], 1)
            if eng_name == "sync":
                for k in cum:
                    if cum[k][n] > last_wait[k]:
                        eng.wait_ge(sems[k], cum[k][n])

        @block.sync
        def _(sync):
            emit_engine("sync", sync)

        @block.gpsimd
        def _(gpsimd):
            emit_engine("gpsimd", gpsimd)

        @block.vector
        def _(vector):
            emit_engine("vector", vector)

        @block.scalar
        def _(scalar):
            emit_engine("scalar", scalar)

        @block.tensor
        def _(tensor):
            emit_engine("tensor", tensor)


def rows3(t2d, r0, nblk, cols, c0=0):
    ap = t2d.ap()[r0:r0 + nblk * 128, :].rearrange("(a p) c -> p a c", p=128)
    if c0 != 0 or cols != ap.shape[2]:
        ap = ap[:, :, c0:c0 + cols]
    return ap


def sb3(sb_ap, nblk, cols):
    return sb_ap.rearrange("p (a c) -> p a c", c=cols)


def build_nc(debug=False):
    nc = bass.Bass("TRN2", target_bir_lowering=False)

    d_x = nc.dram_tensor("x", [TPC, D], F32, kind="ExternalInput")
    d_wq = nc.dram_tensor("wq", [NCD * 128, D], F32, kind="ExternalInput")
    d_wk = nc.dram_tensor("wk", [NCD * 128, D], F32, kind="ExternalInput")
    d_wv = nc.dram_tensor("wv", [NCD * 128, D], F32, kind="ExternalInput")
    d_wo = nc.dram_tensor("wo", [NCD * 128, D], F32, kind="ExternalInput")
    d_w1 = nc.dram_tensor("w1t", [8 * 8 * 128, 512], F32, kind="ExternalInput")
    d_w2 = nc.dram_tensor("w2", [32 * 128, D], F32, kind="ExternalInput")
    d_e1 = nc.dram_tensor("e1", [NE * 8 * 128, 1024], F32, kind="ExternalInput")
    d_e2 = nc.dram_tensor("e2", [NE * 8 * 128, 1024], F32, kind="ExternalInput")
    d_sgu = nc.dram_tensor("sgu", [8 * 128, 2048], F32, kind="ExternalInput")
    d_sdn = nc.dram_tensor("sdn", [8 * 128, 1024], F32, kind="ExternalInput")
    d_rw = nc.dram_tensor("rw", [128, 64], F32, kind="ExternalInput")
    d_sgw = nc.dram_tensor("sgw", [128, 8], F32, kind="ExternalInput")
    d_vecs = nc.dram_tensor("vecs", [128, 64], F32, kind="ExternalInput")
    d_b1c = nc.dram_tensor("b1c", [128, 32], F32, kind="ExternalInput")
    d_rowpack = nc.dram_tensor("rowpack", [1, 2176], F32, kind="ExternalInput")
    d_id = nc.dram_tensor("ident", [128, 128], F32, kind="ExternalInput")
    d_sels = nc.dram_tensor("sels", [16, 2304], F32, kind="ExternalInput")
    d_onesel = nc.dram_tensor("onesel", [128, 32], F32, kind="ExternalInput")

    d_out = nc.dram_tensor("out", [TPC, D], F32, kind="ExternalOutput")

    d_ccin = nc.dram_tensor("ccin", [128, 520], F32)
    d_ccout = nc.dram_tensor("ccout", [128, 520], F32)

    def cache(name, rows, cols):
        return (nc.dram_tensor(name + "_h", [rows, cols], BF16),
                nc.dram_tensor(name + "_l", [rows, cols], BF16))

    c_wq = cache("c_wq", NCD * 128, D)
    c_wk = cache("c_wk", NCD * 128, D)
    c_wv = cache("c_wv", NCD * 128, D)
    c_wo = cache("c_wo", NCD * 128, D)
    c_w1 = cache("c_w1", 8 * 8 * 128, 512)
    c_w2 = cache("c_w2", 32 * 128, D)
    c_e1 = cache("c_e1", NE * 8 * 128, 1024)
    c_e2 = cache("c_e2", NE * 8 * 128, 1024)
    c_sgu = cache("c_sgu", 8 * 128, 2048)
    c_sdn = cache("c_sdn", 8 * 128, 1024)

    d_pqh = nc.dram_tensor("pqh", [NT * NCD * 128, TS], BF16)
    d_pql = nc.dram_tensor("pql", [NT * NCD * 128, TS], BF16)
    d_rows = nc.dram_tensor("rows", [2, TPC], F32)

    X0 = [nc.alloc_sbuf_tensor(f"x0_{c}", [128, TPC], F32) for c in range(NCD)]
    kvacc = nc.alloc_sbuf_tensor("kvacc", [128, 520], F32)
    kvh = nc.alloc_sbuf_tensor("kvh", [128, 520], BF16)
    kvl = nc.alloc_sbuf_tensor("kvl", [128, 520], BF16)
    kselh = nc.alloc_sbuf_tensor("kselh", [128, 128], BF16)
    ksell = nc.alloc_sbuf_tensor("ksell", [128, 128], BF16)
    sb_id = nc.alloc_sbuf_tensor("sb_id", [128, 128], F32)
    sb_sels = nc.alloc_sbuf_tensor("sb_sels", [16, 2304], F32)
    sb_onesel = nc.alloc_sbuf_tensor("sb_onesel", [128, 32], F32)
    sb_rowpack = nc.alloc_sbuf_tensor("sb_rowpack", [1, 2176], F32)
    sb_vecs = nc.alloc_sbuf_tensor("sb_vecs", [128, 64], F32)
    sb_b1c = nc.alloc_sbuf_tensor("sb_b1c", [128, 32], F32)
    sb_rwh = nc.alloc_sbuf_tensor("sb_rwh", [128, 64], BF16)
    sb_rwl = nc.alloc_sbuf_tensor("sb_rwl", [128, 64], BF16)
    sb_sgwh = nc.alloc_sbuf_tensor("sb_sgwh", [128, 8], BF16)
    sb_sgwl = nc.alloc_sbuf_tensor("sb_sgwl", [128, 8], BF16)
    P1h = nc.alloc_sbuf_tensor("P1h", [128, 4096], BF16)
    P1l = nc.alloc_sbuf_tensor("P1l", [128, 4096], BF16)
    P2h = nc.alloc_sbuf_tensor("P2h", [128, 4096], BF16)
    P2l = nc.alloc_sbuf_tensor("P2l", [128, 4096], BF16)
    R1 = nc.alloc_sbuf_tensor("R1", [128, 4096], F32)
    F1 = nc.alloc_sbuf_tensor("F1", [128, 2048], F32)
    F2 = nc.alloc_sbuf_tensor("F2", [128, 1024], F32)
    WAh = nc.alloc_sbuf_tensor("WAh", [128, 8192], BF16)
    WAl = nc.alloc_sbuf_tensor("WAl", [128, 8192], BF16)
    WBh = nc.alloc_sbuf_tensor("WBh", [128, 4096], BF16)
    WBl = nc.alloc_sbuf_tensor("WBl", [128, 4096], BF16)
    U1a = nc.alloc_sbuf_tensor("U1a", [128, 2112], BF16)
    U1b = nc.alloc_sbuf_tensor("U1b", [128, 2112], BF16)

    pb = [nc.alloc_psum_tensor(f"pb{i}", [128, 512], F32) for i in range(8)]

    p = Prog()

    def mm3(out_ap, lh, ll, rh, rl, first, last):
        p.t(lambda e, o=out_ap, a=lh, b=rh, f=first: e.matmul(o, a, b, start=f, stop=False))
        p.t(lambda e, o=out_ap, a=lh, b=rl: e.matmul(o, a, b, start=False, stop=False))
        p.t(lambda e, o=out_ap, a=ll, b=rh, s=last: e.matmul(o, a, b, start=False, stop=s))

    def dma(q, out_ap, in_ap):
        fn = lambda e, o=out_ap, i=in_ap: e.dma_start(out=o, in_=i)
        (p.dS if q == "sync" else p.dG)(fn)

    def dbg(name, ap):
        if not debug:
            return
        shape = list(ap.shape)
        t = nc.dram_tensor("dbg_" + name, shape, ap.dtype, kind="ExternalOutput")
        p.bar()
        dma("gpsimd", t.ap(), ap)
        p.bar()

    def ln_stats(src_aps, sqrt_scale_col, sqrt_bias_col, keep_ti=None):
        for half in (0, 1):
            for j in range(4):
                c = half * 4 + j
                p.a(lambda e, s=src_aps[c], d=F1[:, j * TS:(j + 1) * TS]:
                    e.activation(d, s, AF.Square))
            p.bar()
            for j in range(4):
                c = half * 4 + j
                p.t(lambda e, r=src_aps[c], f=(c == 0):
                    e.matmul(pb[6][0:16, 0:TS], sb_onesel[:, 0:16], r, start=f, stop=False))
                p.t(lambda e, r=F1[:, j * TS:(j + 1) * TS], s=(c == 7):
                    e.matmul(pb[6][0:16, 0:TS], sb_onesel[:, 16:32], r, start=False, stop=s))
            p.bar()
        p.v(lambda e: e.tensor_copy(F2[0:2, 0:TS], pb[6][0:2, 0:TS]))
        p.bar()
        if keep_ti is not None:
            dma("gpsimd", d_rows.ap()[0:2, keep_ti * TS:(keep_ti + 1) * TS],
                F2[0:2, 0:TS])
        p.t(lambda e: e.matmul(pb[6][:, 0:TS], sb_sels[:, 0:128], F2[0:16, 0:TS],
                               start=True, stop=True))
        p.t(lambda e: e.matmul(pb[7][:, 0:TS], sb_sels[:, 128:256], F2[0:16, 0:TS],
                               start=True, stop=True))
        p.bar()
        sm = F1[:, 0:TS]
        sqm = F1[:, TS:2 * TS]
        p.v(lambda e: e.tensor_scalar(sqm, pb[6][:, 0:TS], 1.0 / D, None, OP.mult))
        p.bar()
        p.v(lambda e: e.tensor_tensor(sm, sqm, sqm, OP.mult))
        p.bar()
        p.v(lambda e: e.scalar_tensor_tensor(sm, pb[7][:, 0:TS], 1.0 / D, sm,
                                             OP.mult, OP.subtract))
        p.bar()
        if sqrt_scale_col is None:
            p.a(lambda e, bc=sqrt_bias_col:
                e.activation(sm, sm, AF.Sqrt, bias=sb_vecs[:, bc:bc + 1]))
        else:
            p.a(lambda e, bc=sqrt_bias_col, sc=sqrt_scale_col:
                e.activation(sm, sm, AF.Sqrt, bias=sb_vecs[:, bc:bc + 1],
                             scale=sb_vecs[:, sc:sc + 1]))
        p.bar()
        p.v(lambda e: e.reciprocal(sm, sm))
        p.bar()
        p.v(lambda e: e.tensor_tensor(sqm, sqm, sm, OP.mult))
        p.bar()

    dma("sync", sb_id[:], d_id.ap())
    dma("sync", sb_sels[:], d_sels.ap())
    dma("sync", sb_onesel[:], d_onesel.ap())
    dma("sync", sb_vecs[:], d_vecs.ap())
    dma("sync", sb_b1c[:], d_b1c.ap())
    dma("sync", sb_rowpack[:], d_rowpack.ap())
    p.v(lambda e: e.memset(kvacc[:], 0.0))
    p.v(lambda e: e.memset(F2[:], 0.0))
    p.bar()

    for blk in range(TPC // 128):
        dma("sync", F2[:, 0:1024], d_x.ap()[blk * 128:(blk + 1) * 128, :])
        p.bar()
        for c in range(NCD):
            bank = pb[c // 4]
            p.t(lambda e, c=c, b=bank:
                e.transpose(b[:, (c % 4) * 128:(c % 4) * 128 + 128],
                            F2[:, c * 128:(c + 1) * 128], sb_id[:]))
        p.bar()
        for c in range(NCD):
            bank = pb[c // 4]
            p.v(lambda e, c=c, b=bank, blk=blk:
                e.tensor_copy(X0[c][:, blk * 128:(blk + 1) * 128],
                              b[:, (c % 4) * 128:(c % 4) * 128 + 128]))
        p.bar()

    dbg("x0c0", X0[0][:, 0:512])

    slabs = []

    def add_slabs(src, ch, cl, nrb, cols):
        g = max(1, 2048 // cols)
        r = 0
        while r < nrb:
            gg = min(g, nrb - r)
            slabs.append((src, ch, cl, r * 128, gg, cols))
            r += gg

    add_slabs(d_wq, *c_wq, NCD, D)
    add_slabs(d_wk, *c_wk, NCD, D)
    add_slabs(d_wv, *c_wv, NCD, D)
    add_slabs(d_wo, *c_wo, NCD, D)
    add_slabs(d_w1, *c_w1, 64, 512)
    add_slabs(d_w2, *c_w2, 32, D)
    add_slabs(d_e1, *c_e1, 64, 1024)
    add_slabs(d_e2, *c_e2, 64, 1024)
    add_slabs(d_sgu, *c_sgu, 8, 2048)
    add_slabs(d_sdn, *c_sdn, 8, 1024)

    nsl = len(slabs)

    def p0in(k):
        s = k % 3
        return (R1[:, 0:2048] if s == 0 else
                R1[:, 2048:4096] if s == 1 else F1[:, 0:2048])

    for k in range(nsl + 3):
        if k < nsl:
            src_, ch, cl, r0, nb_, cols = slabs[k]
            w = nb_ * cols
            dma("sync", sb3(p0in(k)[:, 0:w], nb_, cols), rows3(src_, r0, nb_, cols))
        if 1 <= k <= nsl:
            src_, ch, cl, r0, nb_, cols = slabs[k - 1]
            w = nb_ * cols
            half = ((k - 1) % 2) * 2048
            p.a(lambda e, w=w, h=half, kk=k - 1:
                e.copy(P1h[:, h:h + w], p0in(kk)[:, 0:w]))
        if 2 <= k <= nsl + 1:
            src_, ch, cl, r0, nb_, cols = slabs[k - 2]
            w = nb_ * cols
            half = ((k - 2) % 2) * 2048
            p.v(lambda e, w=w, h=half, kk=k - 2:
                e.tensor_tensor(P1l[:, h:h + w], p0in(kk)[:, 0:w],
                                P1h[:, h:h + w], OP.subtract))
            dma("gpsimd", rows3(ch, r0, nb_, cols), sb3(P1h[:, half:half + w], nb_, cols))
        if 3 <= k:
            src_, ch, cl, r0, nb_, cols = slabs[k - 3]
            w = nb_ * cols
            half = ((k - 3) % 2) * 2048
            dma("gpsimd", rows3(cl, r0, nb_, cols), sb3(P1l[:, half:half + w], nb_, cols))
        p.bar()

    dma("sync", F1[:, 0:64], d_rw.ap())
    dma("sync", F1[:, 64:72], d_sgw.ap())
    p.bar()
    p.v(lambda e: e.tensor_copy(sb_rwh[:], F1[:, 0:64]))
    p.v(lambda e: e.tensor_copy(sb_sgwh[:], F1[:, 64:72]))
    p.bar()
    p.v(lambda e: e.tensor_tensor(sb_rwl[:], F1[:, 0:64], sb_rwh[:], OP.subtract))
    p.v(lambda e: e.tensor_tensor(sb_sgwl[:], F1[:, 64:72], sb_sgwh[:], OP.subtract))
    p.bar()

    for ti in range(NT):
        tcol = slice(ti * TS, (ti + 1) * TS)
        xc = [X0[c][:, tcol] for c in range(NCD)]

        ln_stats(xc, VC_LN1S, VC_LN1B, keep_ti=ti)
        if ti == 0:
            dbg("ln1U", F1[:, 0:TS])
            dbg("ln1Q", F1[:, TS:2 * TS])
        UB = F1[:, 0:TS]
        QB = F1[:, TS:2 * TS]
        for c in range(NCD):
            t2 = F1[:, 2 * TS:3 * TS]
            af = F1[:, 3 * TS:4 * TS]
            p.v(lambda e, xcc=xc[c], t2=t2: e.tensor_tensor(t2, xcc, UB, OP.mult))
            p.bar()
            if ti == 0 and c == 0:
                dbg("t2a", t2)
            p.v(lambda e, t2=t2: e.tensor_tensor(t2, t2, QB, OP.subtract))
            p.bar()
            if ti == 0 and c == 0:
                dbg("t2b", t2)
            p.a(lambda e, c=c, t2=t2, af=af:
                e.activation(af, t2, AF.Identity,
                             bias=sb_vecs[:, VB1 + c:VB1 + c + 1],
                             scale=sb_vecs[:, VG1 + c:VG1 + c + 1]))
            p.bar()
            if ti == 0 and c == 0:
                dbg("af0", af)
            p.v(lambda e, c=c, af=af: e.tensor_copy(P1h[:, c * TS:(c + 1) * TS], af))
            p.bar()
            p.v(lambda e, c=c, af=af:
                e.tensor_tensor(P1l[:, c * TS:(c + 1) * TS], af,
                                P1h[:, c * TS:(c + 1) * TS], OP.subtract))
            p.bar()

        if ti == 0:
            dbg("ah0", P1h[:, 0:TS])
            dbg("al0", P1l[:, 0:TS])
        dma("sync", sb3(WAh[:, 0:8192], 8, D), rows3(c_wq[0], 0, 8, D))
        dma("sync", sb3(WAl[:, 0:8192], 8, D), rows3(c_wq[1], 0, 8, D))
        p.bar()
        for oc in range(NCD):
            bank = pb[oc % 2]
            for kc in range(NCD):
                lh = WAh[:, kc * D + oc * 128: kc * D + oc * 128 + 128]
                ll = WAl[:, kc * D + oc * 128: kc * D + oc * 128 + 128]
                rh = P1h[:, kc * TS:(kc + 1) * TS]
                rl = P1l[:, kc * TS:(kc + 1) * TS]
                mm3(bank[:, 0:TS], lh, ll, rh, rl, kc == 0, kc == NCD - 1)
            p.bar()
            fq = F1[:, 2 * TS:3 * TS]
            p.a(lambda e, b=bank, fq=fq, oc=oc:
                e.activation(fq, b[:, 0:TS], AF.Relu,
                             bias=sb_vecs[:, VBQ + oc:VBQ + oc + 1]))
            p.bar()
            p.v(lambda e, fq=fq, oc=oc:
                e.tensor_scalar(P2h[:, oc * TS:(oc + 1) * TS], fq, 0.001, None, OP.add))
            p.bar()
            p.v(lambda e, fq=fq, oc=oc:
                e.scalar_tensor_tensor(P2l[:, oc * TS:(oc + 1) * TS], fq, 0.001,
                                       P2h[:, oc * TS:(oc + 1) * TS], OP.add, OP.subtract))
            p.bar()
        if ti == 0:
            dbg("pqh0", P2h[:, 0:TS])
        dma("gpsimd", rows3(d_pqh, ti * NCD * 128, 8, TS), sb3(P2h[:, 0:4096], 8, TS))
        dma("gpsimd", rows3(d_pql, ti * NCD * 128, 8, TS), sb3(P2l[:, 0:4096], 8, TS))
        p.bar()

        dma("sync", sb3(WAh[:, 0:8192], 8, D), rows3(c_wk[0], 0, 8, D))
        dma("sync", sb3(WAl[:, 0:8192], 8, D), rows3(c_wk[1], 0, 8, D))
        dma("sync", sb3(WBh[:, 0:4096], 4, D), rows3(c_wv[0], 0, 4, D))
        dma("sync", sb3(WBl[:, 0:4096], 4, D), rows3(c_wv[0], 4 * 128, 4, D))
        dma("sync", sb3(P2h[:, 0:4096], 4, D), rows3(c_wv[1], 0, 4, D))
        dma("sync", sb3(P2l[:, 0:4096], 4, D), rows3(c_wv[1], 4 * 128, 4, D))
        p.bar()

        def wvh(kc, hf):
            idx = kc * D + hf * 512
            t = WBh if idx < 4096 else WBl
            return t[:, idx % 4096: idx % 4096 + 512]

        def wvl(kc, hf):
            idx = kc * D + hf * 512
            t = P2h if idx < 4096 else P2l
            return t[:, idx % 4096: idx % 4096 + 512]

        for blk in range(NBLK):
            for hf in range(2):
                bank = pb[2 + hf]
                for kc in range(NCD):
                    ah = P1h[:, kc * TS + blk * 128: kc * TS + blk * 128 + 128]
                    al = P1l[:, kc * TS + blk * 128: kc * TS + blk * 128 + 128]
                    rh = WAh[:, kc * D + hf * 512: kc * D + hf * 512 + 512]
                    rl = WAl[:, kc * D + hf * 512: kc * D + hf * 512 + 512]
                    mm3(bank[:, 0:512], ah, al, rh, rl, kc == 0, False)
                p.t(lambda e, b=bank, hf=hf:
                    e.matmul(b[:, 0:512], sb_rowpack[0:1, 2048:2176],
                             sb_rowpack[0:1, hf * 512:hf * 512 + 512],
                             start=False, stop=True))
            p.bar()
            for hf in range(2):
                bank = pb[2 + hf]
                fk = F1[:, (2 + hf) * TS:(3 + hf) * TS]
                p.a(lambda e, b=bank, fk=fk: e.activation(fk, b[:, 0:512], AF.Relu))
            p.bar()
            for hf in range(2):
                fk = F1[:, (2 + hf) * TS:(3 + hf) * TS]
                p.v(lambda e, fk=fk, hf=hf:
                    e.tensor_scalar(U1a[:, hf * 512:hf * 512 + 512], fk, 0.001, None, OP.add))
                p.v(lambda e, fk=fk, hf=hf:
                    e.scalar_tensor_tensor(U1b[:, hf * 512:hf * 512 + 512], fk, 0.001,
                                           U1a[:, hf * 512:hf * 512 + 512], OP.add, OP.subtract))
            p.bar()
            for hf in range(2):
                bank = pb[2 + hf]
                for kc in range(NCD):
                    ah = P1h[:, kc * TS + blk * 128: kc * TS + blk * 128 + 128]
                    al = P1l[:, kc * TS + blk * 128: kc * TS + blk * 128 + 128]
                    mm3(bank[:, 0:512], ah, al, wvh(kc, hf), wvl(kc, hf), kc == 0, False)
                p.t(lambda e, b=bank, hf=hf:
                    e.matmul(b[:, 0:512], sb_rowpack[0:1, 2048:2176],
                             sb_rowpack[0:1, 1024 + hf * 512:1024 + hf * 512 + 512],
                             start=False, stop=True))
            p.bar()
            for hf in range(2):
                bank = pb[2 + hf]
                src = bass.AP(bank, 0, [[512, 128], [64, 8], [1, 64]])
                dsth = bass.AP(U1a, 1024 + hf * 8 * 65, [[2112, 128], [65, 8], [1, 64]])
                dstl = bass.AP(U1b, 1024 + hf * 8 * 65, [[2112, 128], [65, 8], [1, 64]])
                p.v(lambda e, s=src, d=dsth: e.tensor_copy(d, s))
                p.v(lambda e, s=src, dh=dsth, dl=dstl:
                    e.scalar_tensor_tensor(dl, s, 0.0, dh, OP.add, OP.subtract))
            onesh = bass.AP(U1a, 1024 + 64, [[2112, 128], [65, 16], [1, 1]])
            onesl = bass.AP(U1b, 1024 + 64, [[2112, 128], [65, 16], [1, 1]])
            p.v(lambda e, o=onesh: e.memset(o, 1.0))
            p.v(lambda e, o=onesl: e.memset(o, 0.0))
            p.bar()
            if ti == 0 and blk == 0:
                dbg("u1a0", U1a[:, 0:2112])
                dbg("u1b0", U1b[:, 0:2112])
            for rnd in range(2):
                for j in range(4):
                    pair = rnd * 4 + j
                    bank = pb[4 + j]
                    for sub in range(2):
                        h_ = pair * 2 + sub
                        lh = U1a[:, h_ * 64:(h_ + 1) * 64]
                        ll = U1b[:, h_ * 64:(h_ + 1) * 64]
                        rh = U1a[:, 1024 + h_ * 65:1024 + (h_ + 1) * 65]
                        rl = U1b[:, 1024 + h_ * 65:1024 + (h_ + 1) * 65]
                        mm3(bank[sub * 64:(sub + 1) * 64, 0:65], lh, ll, rh, rl, True, True)
                p.bar()
                if debug and ti == 0 and blk == 0 and rnd == 0:
                    p.v(lambda e: e.tensor_copy(F1[:, 0:65], pb[4][:, 0:65]))
                    dbg("kvps0", F1[:, 0:65])
                for j in range(4):
                    pair = rnd * 4 + j
                    bank = pb[4 + j]
                    p.v(lambda e, b=bank, pr=pair:
                        e.tensor_tensor(kvacc[:, pr * 65:(pr + 1) * 65],
                                        kvacc[:, pr * 65:(pr + 1) * 65],
                                        b[:, 0:65], OP.add))
                p.bar()
                if debug and ti == 0 and blk == 0 and rnd == 0:
                    dbg("kvacc0", kvacc[:])

    dbg("kvpart", kvacc[:])
    dma("gpsimd", d_ccin.ap(), kvacc[:])
    p.bar()
    dbg("ccin", d_ccin.ap())
    p.g(lambda e: e.collective_compute(
        "AllReduce", mybir.AluOpType.add,
        replica_groups=[[0, 1, 2, 3], [4, 5, 6, 7]],
        ins=[d_ccin.ap()], outs=[d_ccout.ap()]))
    p.bar()
    dbg("ccout", d_ccout.ap())
    dma("gpsimd", kvacc[:], d_ccout.ap())
    p.bar()
    dbg("kvback", kvacc[:])
    p.v(lambda e: e.tensor_copy(kvh[:], kvacc[:]))
    p.bar()
    p.v(lambda e: e.tensor_tensor(kvl[:], kvacc[:], kvh[:], OP.subtract))
    p.v(lambda e: e.memset(kselh[:], 0.0))
    p.v(lambda e: e.memset(ksell[:], 0.0))
    p.bar()
    dbg("kvfull", kvacc[:])
    p.vdrain()
    for c in range(NCD):
        p.v(lambda e, c=c: e.tensor_copy(kselh[0:64, c * 16 + 2 * c:c * 16 + 2 * c + 1],
                                         kvh[0:64, c * 65 + 64:c * 65 + 65]))
        p.v(lambda e, c=c: e.tensor_copy(kselh[64:128, c * 16 + 2 * c + 1:c * 16 + 2 * c + 2],
                                         kvh[64:128, c * 65 + 64:c * 65 + 65]))
        p.v(lambda e, c=c: e.tensor_copy(ksell[0:64, c * 16 + 2 * c:c * 16 + 2 * c + 1],
                                         kvl[0:64, c * 65 + 64:c * 65 + 65]))
        p.v(lambda e, c=c: e.tensor_copy(ksell[64:128, c * 16 + 2 * c + 1:c * 16 + 2 * c + 2],
                                         kvl[64:128, c * 65 + 64:c * 65 + 65]))
    p.vdrain()
    p.bar()

    for ti in range(NT):
        tcol = slice(ti * TS, (ti + 1) * TS)
        xc = [X0[c][:, tcol] for c in range(NCD)]
        rc = [R1[:, c * TS:(c + 1) * TS] for c in range(NCD)]

        dma("sync", sb3(P1h[:, 0:4096], 8, TS), rows3(d_pqh, ti * NCD * 128, 8, TS))
        dma("sync", sb3(P1l[:, 0:4096], 8, TS), rows3(d_pql, ti * NCD * 128, 8, TS))
        dma("sync", sb3(WAh[:, 0:8192], 8, D), rows3(c_wo[0], 0, 8, D))
        dma("sync", sb3(WAl[:, 0:8192], 8, D), rows3(c_wo[1], 0, 8, D))
        p.bar()

        for kc in range(NCD):
            rh = P1h[:, kc * TS:(kc + 1) * TS]
            rl = P1l[:, kc * TS:(kc + 1) * TS]
            mm3(pb[2][0:16, 0:TS], kselh[:, kc * 16:(kc + 1) * 16],
                ksell[:, kc * 16:(kc + 1) * 16], rh, rl, kc == 0, kc == NCD - 1)
        p.bar()
        p.v(lambda e: e.reciprocal(F2[0:16, 0:512], pb[2][0:16, 0:TS]))
        p.bar()
        if ti == 0:
            dbg("recip0", F2[0:16, 0:512])

        for c in range(NCD):
            nb_ = pb[c % 2]
            bb_ = pb[3 + c % 2]
            for sub in range(2):
                lh = kvh[sub * 64:(sub + 1) * 64, c * 65:c * 65 + 64]
                ll = kvl[sub * 64:(sub + 1) * 64, c * 65:c * 65 + 64]
                rh = P1h[sub * 64:(sub + 1) * 64, c * TS:(c + 1) * TS]
                rl = P1l[sub * 64:(sub + 1) * 64, c * TS:(c + 1) * TS]
                mm3(nb_[sub * 64:(sub + 1) * 64, 0:TS], lh, ll, rh, rl, True, True)
            p.t(lambda e, c=c, b=bb_:
                e.matmul(b[:, 0:TS], sb_sels[:, 256 + c * 128:256 + (c + 1) * 128],
                         F2[0:16, 0:512], start=True, stop=True))
            p.bar()
            rb = F2[:, 512:1024]
            fa = F1[:, 2 * TS:3 * TS]
            p.v(lambda e, b=bb_, rb=rb: e.tensor_copy(rb, b[:, 0:TS]))
            p.bar()
            p.v(lambda e, b=nb_, rb=rb, fa=fa: e.tensor_tensor(fa, b[:, 0:TS], rb, OP.mult))
            p.bar()
            p.v(lambda e, c=c, fa=fa: e.tensor_copy(P2h[:, c * TS:(c + 1) * TS], fa))
            p.bar()
            p.v(lambda e, c=c, fa=fa:
                e.tensor_tensor(P2l[:, c * TS:(c + 1) * TS], fa,
                                P2h[:, c * TS:(c + 1) * TS], OP.subtract))
            p.bar()

        dma("sync", F2[0:2, 0:512], d_rows.ap()[0:2, tcol])
        p.bar()
        p.t(lambda e: e.matmul(pb[6][:, 0:TS], sb_sels[:, 0:128],
                               F2[0:16, 0:512], start=True, stop=True))
        p.t(lambda e: e.matmul(pb[7][:, 0:TS], sb_sels[:, 128:256],
                               F2[0:16, 0:512], start=True, stop=True))
        p.bar()
        S1B = F1[:, 0:TS]
        MS1B = F1[:, TS:2 * TS]
        p.v(lambda e: e.tensor_scalar(MS1B, pb[6][:, 0:TS], 1.0 / D, None, OP.mult))
        p.bar()
        p.v(lambda e: e.tensor_tensor(S1B, MS1B, MS1B, OP.mult))
        p.bar()
        p.v(lambda e: e.scalar_tensor_tensor(S1B, pb[7][:, 0:TS], 1.0 / D, S1B,
                                             OP.mult, OP.subtract))
        p.bar()
        p.a(lambda e: e.activation(S1B, S1B, AF.Sqrt, bias=sb_vecs[:, VC_E5:VC_E5 + 1]))
        p.bar()
        p.v(lambda e: e.reciprocal(S1B, S1B))
        p.bar()
        p.v(lambda e: e.tensor_tensor(MS1B, MS1B, S1B, OP.mult))
        p.bar()

        for oc in range(NCD):
            bank = pb[oc]
            for kc in range(NCD):
                lh = WAh[:, kc * D + oc * 128: kc * D + oc * 128 + 128]
                ll = WAl[:, kc * D + oc * 128: kc * D + oc * 128 + 128]
                rh = P2h[:, kc * TS:(kc + 1) * TS]
                rl = P2l[:, kc * TS:(kc + 1) * TS]
                mm3(bank[:, 0:TS], lh, ll, rh, rl, kc == 0, kc == NCD - 1)
        p.bar()
        for oc in range(NCD):
            bank = pb[oc]
            t3 = F1[:, 2 * TS:3 * TS]
            p.v(lambda e, t3=t3, xop=X0[oc][:, tcol]:
                e.tensor_tensor(t3, xop, S1B, OP.mult))
            p.bar()
            p.v(lambda e, t3=t3: e.tensor_tensor(t3, t3, MS1B, OP.subtract))
            p.bar()
            p.v(lambda e, oc=oc, b=bank, t3=t3:
                e.scalar_tensor_tensor(R1[:, oc * TS:(oc + 1) * TS],
                                       b[:, 0:TS], sb_vecs[:, VBO + oc:VBO + oc + 1],
                                       t3, OP.add, OP.add))
            p.bar()

        ln_stats(rc, None, VC_E6)
        UB = F1[:, 0:TS]
        QB = F1[:, TS:2 * TS]
        for c in range(NCD):
            t2 = F1[:, 2 * TS:3 * TS]
            yf = F1[:, 3 * TS:4 * TS]
            p.v(lambda e, rcc=rc[c], t2=t2: e.tensor_tensor(t2, rcc, UB, OP.mult))
            p.bar()
            p.v(lambda e, t2=t2: e.tensor_tensor(t2, t2, QB, OP.subtract))
            p.bar()
            p.a(lambda e, c=c, t2=t2, yf=yf:
                e.activation(yf, t2, AF.Identity,
                             bias=sb_vecs[:, VB2 + c:VB2 + c + 1],
                             scale=sb_vecs[:, VG2 + c:VG2 + c + 1]))
            p.bar()
            p.v(lambda e, c=c, yf=yf: e.tensor_copy(P1h[:, c * TS:(c + 1) * TS], yf))
            p.bar()
            p.v(lambda e, c=c, yf=yf:
                e.tensor_tensor(P1l[:, c * TS:(c + 1) * TS], yf,
                                P1h[:, c * TS:(c + 1) * TS], OP.subtract))
            p.bar()

        dma("sync", sb3(WAh[:, 0:4096], 8, 512), rows3(c_w1[0], 0, 8, 512))
        dma("sync", sb3(WAl[:, 0:4096], 8, 512), rows3(c_w1[1], 0, 8, 512))
        dma("sync", sb3(WBh[:, 0:4096], 4, D), rows3(c_w2[0], 0, 4, D))
        dma("sync", sb3(WBl[:, 0:4096], 4, D), rows3(c_w2[1], 0, 4, D))
        p.bar()
        for ft in range(8):
            ha = (ft % 2) * 4096
            for j in range(4):
                bank = pb[j]
                for kc in range(NCD):
                    lh = WAh[:, ha + kc * 512 + j * 128: ha + kc * 512 + j * 128 + 128]
                    ll = WAl[:, ha + kc * 512 + j * 128: ha + kc * 512 + j * 128 + 128]
                    rh = P1h[:, kc * TS:(kc + 1) * TS]
                    rl = P1l[:, kc * TS:(kc + 1) * TS]
                    mm3(bank[:, 0:TS], lh, ll, rh, rl, kc == 0, kc == NCD - 1)
            p.bar()
            for j in range(4):
                gf = F1[:, j * TS:(j + 1) * TS]
                p.a(lambda e, b=pb[j], gf=gf, ft=ft, j=j:
                    e.activation(gf, b[:, 0:TS], AF.Gelu_apprx_tanh,
                                 bias=sb_b1c[:, ft * 4 + j: ft * 4 + j + 1]))
            p.bar()
            for j in range(4):
                gf = F1[:, j * TS:(j + 1) * TS]
                p.v(lambda e, j=j, gf=gf: e.tensor_copy(U1a[:, j * TS:(j + 1) * TS], gf))
                p.v(lambda e, j=j, gf=gf:
                    e.scalar_tensor_tensor(U1b[:, j * TS:(j + 1) * TS], gf, 0.0,
                                           U1a[:, j * TS:(j + 1) * TS], OP.add, OP.subtract))
            p.bar()
            if ft < 7:
                ha2 = ((ft + 1) % 2) * 4096
                dma("sync", sb3(WAh[:, ha2:ha2 + 4096], 8, 512),
                    rows3(c_w1[0], (ft + 1) * 1024, 8, 512))
                dma("sync", sb3(WAl[:, ha2:ha2 + 4096], 8, 512),
                    rows3(c_w1[1], (ft + 1) * 1024, 8, 512))
            for dc in range(NCD):
                bank = pb[dc]
                for j in range(4):
                    lh = WBh[:, j * D + dc * 128: j * D + dc * 128 + 128]
                    ll = WBl[:, j * D + dc * 128: j * D + dc * 128 + 128]
                    rh = U1a[:, j * TS:(j + 1) * TS]
                    rl = U1b[:, j * TS:(j + 1) * TS]
                    mm3(bank[:, 0:TS], lh, ll, rh, rl, j == 0, j == 3)
            p.bar()
            if ft < 7:
                dma("sync", sb3(WBh[:, 0:4096], 4, D), rows3(c_w2[0], (ft + 1) * 4 * 128, 4, D))
                dma("sync", sb3(WBl[:, 0:4096], 4, D), rows3(c_w2[1], (ft + 1) * 4 * 128, 4, D))
            for dc in range(NCD):
                bank = pb[dc]
                rap = R1[:, dc * TS:(dc + 1) * TS]
                if ft == 7:
                    p.v(lambda e, dc=dc, b=bank, rap=rap:
                        e.scalar_tensor_tensor(rap, b[:, 0:TS],
                                               sb_vecs[:, VB2b + dc:VB2b + dc + 1],
                                               rap, OP.add, OP.add))
                else:
                    p.v(lambda e, b=bank, rap=rap:
                        e.tensor_tensor(rap, rap, b[:, 0:TS], OP.add))
            p.bar()

        ln_stats(rc, None, VC_E5)
        SB = F1[:, 0:TS]
        QB2 = F1[:, TS:2 * TS]
        for c in range(NCD):
            t2 = F1[:, (2 + c % 2) * TS:(3 + c % 2) * TS]
            em = p.v if c % 2 == 0 else p.g
            em(lambda e, rcc=rc[c], t2=t2: e.tensor_tensor(t2, rcc, SB, OP.mult))
            em(lambda e, t2=t2: e.tensor_tensor(t2, t2, QB2, OP.subtract))
            em(lambda e, xcc=xc[c], t2=t2: e.tensor_tensor(xcc, xcc, t2, OP.add))
            if c % 2 == 1:
                p.bar()

        if ti == 0:
            dbg("x1_0", X0[0][:, 0:TS])
        ln_stats(xc, None, VC_E5)
        SB = F1[:, 0:TS]
        QB2 = F1[:, TS:2 * TS]
        for c in range(NCD):
            t2 = F1[:, 2 * TS:3 * TS]
            wf = F1[:, 3 * TS:4 * TS]
            p.v(lambda e, xcc=xc[c], t2=t2: e.tensor_tensor(t2, xcc, SB, OP.mult))
            p.bar()
            p.v(lambda e, t2=t2, wf=wf: e.tensor_tensor(wf, t2, QB2, OP.subtract))
            p.bar()
            p.v(lambda e, c=c, wf=wf: e.tensor_copy(P1h[:, c * TS:(c + 1) * TS], wf))
            p.bar()
            p.v(lambda e, c=c, wf=wf:
                e.tensor_tensor(P1l[:, c * TS:(c + 1) * TS], wf,
                                P1h[:, c * TS:(c + 1) * TS], OP.subtract))
            p.bar()

        for kc in range(NCD):
            rh = P1h[:, kc * TS:(kc + 1) * TS]
            rl = P1l[:, kc * TS:(kc + 1) * TS]
            mm3(pb[2][0:8, 0:TS], sb_rwh[:, kc * 8:(kc + 1) * 8],
                sb_rwl[:, kc * 8:(kc + 1) * 8], rh, rl, kc == 0, kc == NCD - 1)
        p.bar()
        p.v(lambda e: e.tensor_copy(F2[0:8, 0:512], pb[2][0:8, 0:TS]))
        p.bar()
        if ti == 0:
            dbg("rlog", F2[0:8, 0:512])
        for j in range(4):
            p.t(lambda e, j=j: e.transpose(pb[3][:, j * 8:(j + 1) * 8],
                                           F2[0:8, j * 128:(j + 1) * 128],
                                           sb_id[0:8, 0:8]))
        p.bar()
        sc_ = F2[:, 512:1024]
        p.v(lambda e: e.tensor_copy(sc_[:, 0:32], pb[3][:, 0:32]))
        p.bar()
        for g in range(4):
            p.v(lambda e, g=g: e.reduce_max(sc_[:, 32 + g:33 + g],
                                            sc_[:, g * 8:(g + 1) * 8], axis=AX))
        p.bar()
        p.vdrain()
        for g in range(4):
            p.v(lambda e, g=g: e.tensor_scalar(sc_[:, 40 + g * 8:48 + g * 8],
                                               sc_[:, g * 8:(g + 1) * 8],
                                               sc_[:, 32 + g:33 + g], None, OP.subtract))
        p.bar()
        p.a(lambda e: e.activation(sc_[:, 40:72], sc_[:, 40:72], AF.Exp))
        p.bar()
        for g in range(4):
            p.v(lambda e, g=g: e.reduce_sum(sc_[:, 72 + g:73 + g],
                                            sc_[:, 40 + g * 8:48 + g * 8], axis=AX))
        p.bar()
        p.vdrain()
        p.v(lambda e: e.reciprocal(sc_[:, 76:80], sc_[:, 72:76]))
        p.bar()
        p.vdrain()
        for g in range(4):
            p.v(lambda e, g=g: e.tensor_scalar(sc_[:, 80 + g * 8:88 + g * 8],
                                               sc_[:, 40 + g * 8:48 + g * 8],
                                               sc_[:, 76 + g:77 + g], None, OP.mult))
        p.bar()
        p.vdrain()
        for g in range(4):
            p.v(lambda e, g=g: e.reduce_max(sc_[:, 112 + g:113 + g],
                                            sc_[:, 80 + g * 8:88 + g * 8], axis=AX))
        p.bar()
        p.vdrain()
        for g in range(4):
            p.v(lambda e, g=g: e.tensor_scalar(sc_[:, 120 + g * 8:128 + g * 8],
                                               sc_[:, 80 + g * 8:88 + g * 8],
                                               sc_[:, 112 + g:113 + g], None, OP.is_ge))
        p.bar()
        p.vdrain()
        p.v(lambda e: e.tensor_tensor(sc_[:, 120:152], sc_[:, 80:112],
                                      sc_[:, 120:152], OP.mult))
        p.bar()
        p.vdrain()
        p.v(lambda e: e.tensor_tensor(sc_[:, 152:184], sc_[:, 80:112],
                                      sc_[:, 120:152], OP.subtract))
        p.bar()
        p.vdrain()
        for g in range(4):
            p.v(lambda e, g=g: e.reduce_max(sc_[:, 184 + g:185 + g],
                                            sc_[:, 152 + g * 8:160 + g * 8], axis=AX))
        p.bar()
        p.vdrain()
        for g in range(4):
            p.v(lambda e, g=g: e.tensor_scalar(sc_[:, 192 + g * 8:200 + g * 8],
                                               sc_[:, 80 + g * 8:88 + g * 8],
                                               sc_[:, 184 + g:185 + g], None, OP.is_ge))
        p.bar()
        p.vdrain()
        p.v(lambda e: e.tensor_tensor(sc_[:, 224:256], sc_[:, 80:112],
                                      sc_[:, 192:224], OP.mult))
        p.bar()
        if ti == 0:
            dbg("smsc", sc_[:, 0:256])
        for j in range(4):
            p.t(lambda e, j=j: e.transpose(pb[2][0:8, j * 128:(j + 1) * 128],
                                           sc_[:, 224 + j * 8:224 + (j + 1) * 8],
                                           sb_id[:]))
        p.bar()
        p.v(lambda e: e.tensor_copy(F2[0:8, 0:512], pb[2][0:8, 0:512]))
        p.bar()

        if ti == 0:
            dbg("wh0", P1h[:, 0:TS])
            dbg("comb0", F2[0:8, 0:512])
        def eslot(s, kc, off, width):
            c = kc * 1024 + off
            if s == 0:
                return WAh[:, c:c + width]
            if s == 1:
                return WAl[:, c:c + width]
            t = WBh if c < 4096 else WBl
            return t[:, c % 4096: c % 4096 + width]

        def eload(s, cache2d, row0):
            if s == 0:
                dma("sync", sb3(WAh[:, 0:8192], 8, D), rows3(cache2d, row0, 8, D))
            elif s == 1:
                dma("sync", sb3(WAl[:, 0:8192], 8, D), rows3(cache2d, row0, 8, D))
            else:
                dma("sync", sb3(WBh[:, 0:4096], 4, D), rows3(cache2d, row0, 4, D))
                dma("sync", sb3(WBl[:, 0:4096], 4, D), rows3(cache2d, row0 + 4 * 128, 4, D))

        eload(0, c_e1[0], 0)
        eload(1, c_e1[1], 0)
        p.bar()
        for ei in range(NE):
            g = ei * 4
            s_e1h, s_e1l, s_e2h, s_e2l = g % 3, (g + 1) % 3, (g + 2) % 3, g % 3
            p.t(lambda e, ei=ei:
                e.matmul(pb[0][:, 0:TS],
                         sb_sels[0:8, 1280 + ei * 128:1280 + (ei + 1) * 128],
                         F2[0:8, 0:512], start=True, stop=True))
            p.bar()
            p.v(lambda e: e.tensor_copy(F2[:, 512:1024], pb[0][:, 0:TS]))
            p.bar()
            for kc in range(NCD):
                eload(s_e2h, c_e2[0], ei * 8 * 128) if kc == 0 else None
                for oc in range(NCD):
                    lh = eslot(s_e1h, kc, oc * 128, 128)
                    ll = eslot(s_e1l, kc, oc * 128, 128)
                    rh = P1h[:, kc * TS:(kc + 1) * TS]
                    rl = P1l[:, kc * TS:(kc + 1) * TS]
                    mm3(pb[oc][:, 0:TS], lh, ll, rh, rl, kc == 0, kc == NCD - 1)
            p.bar()
            for half in range(2):
                for j in range(4):
                    oc = half * 4 + j
                    gf = F1[:, j * TS:(j + 1) * TS]
                    p.a(lambda e, b=pb[oc], gf=gf: e.activation(gf, b[:, 0:TS], AF.Silu))
                if half == 0:
                    eload(s_e2l, c_e2[1], ei * 8 * 128)
                p.bar()
                for j in range(4):
                    oc = half * 4 + j
                    gf = F1[:, j * TS:(j + 1) * TS]
                    p.v(lambda e, gf=gf: e.tensor_tensor(gf, gf, F2[:, 512:1024], OP.mult))
                p.bar()
                for j in range(4):
                    oc = half * 4 + j
                    gf = F1[:, j * TS:(j + 1) * TS]
                    p.v(lambda e, oc=oc, gf=gf:
                        e.tensor_copy(P2h[:, oc * TS:(oc + 1) * TS], gf))
                    p.v(lambda e, oc=oc, gf=gf:
                        e.scalar_tensor_tensor(P2l[:, oc * TS:(oc + 1) * TS], gf, 0.0,
                                               P2h[:, oc * TS:(oc + 1) * TS],
                                               OP.add, OP.subtract))
                p.bar()
            if ei < NE - 1:
                eload((g + 4) % 3, c_e1[0], (ei + 1) * 8 * 128)
            for dc in range(NCD):
                for fc in range(NCD):
                    lh = eslot(s_e2h, fc, dc * 128, 128)
                    ll = eslot(s_e2l, fc, dc * 128, 128)
                    rh = P2h[:, fc * TS:(fc + 1) * TS]
                    rl = P2l[:, fc * TS:(fc + 1) * TS]
                    mm3(pb[dc][:, 0:TS], lh, ll, rh, rl, fc == 0, fc == NCD - 1)
            p.bar()
            if ei < NE - 1:
                eload((g + 5) % 3, c_e1[1], (ei + 1) * 8 * 128)
            for dc in range(NCD):
                rap = R1[:, dc * TS:(dc + 1) * TS]
                if ei == 0:
                    p.v(lambda e, b=pb[dc], rap=rap: e.tensor_copy(rap, b[:, 0:TS]))
                else:
                    p.v(lambda e, b=pb[dc], rap=rap:
                        e.tensor_tensor(rap, rap, b[:, 0:TS], OP.add))
            p.bar()

        for kc in range(NCD):
            rh = P1h[:, kc * TS:(kc + 1) * TS]
            rl = P1l[:, kc * TS:(kc + 1) * TS]
            mm3(pb[2][0:1, 0:TS], sb_sgwh[:, kc:kc + 1], sb_sgwl[:, kc:kc + 1],
                rh, rl, kc == 0, kc == NCD - 1)
        dma("sync", sb3(WAh[:, 0:8192], 8, 1024), rows3(c_sgu[0], 0, 8, 1024, c0=0))
        dma("sync", sb3(WAl[:, 0:8192], 8, 1024), rows3(c_sgu[1], 0, 8, 1024, c0=0))
        p.bar()
        p.a(lambda e: e.activation(F2[0:1, 512:1024], pb[2][0:1, 0:TS], AF.Sigmoid))
        p.bar()
        p.t(lambda e: e.matmul(pb[3][:, 0:TS], sb_sels[:, 0:128], F2[0:16, 512:1024],
                               start=True, stop=True))
        p.bar()
        p.v(lambda e: e.tensor_copy(F2[:, 512:1024], pb[3][:, 0:TS]))
        dma("sync", sb3(WBh[:, 0:4096], 8, 512), rows3(c_sgu[0], 0, 8, 512, c0=1024))
        dma("sync", sb3(WBl[:, 0:4096], 8, 512), rows3(c_sgu[1], 0, 8, 512, c0=1024))
        p.bar()
        for half in range(2):
            if half == 1:
                dma("sync", sb3(WBh[:, 0:4096], 8, 512), rows3(c_sgu[0], 0, 8, 512, c0=1536))
                dma("sync", sb3(WBl[:, 0:4096], 8, 512), rows3(c_sgu[1], 0, 8, 512, c0=1536))
                p.bar()
            for q in range(4):
                oc = half * 4 + q
                for kc in range(NCD):
                    lh = WAh[:, kc * 1024 + oc * 128: kc * 1024 + oc * 128 + 128]
                    ll = WAl[:, kc * 1024 + oc * 128: kc * 1024 + oc * 128 + 128]
                    rh = P1h[:, kc * TS:(kc + 1) * TS]
                    rl = P1l[:, kc * TS:(kc + 1) * TS]
                    mm3(pb[0][:, 0:TS], lh, ll, rh, rl, kc == 0, kc == NCD - 1)
                for kc in range(NCD):
                    lh = WBh[:, kc * 512 + q * 128: kc * 512 + q * 128 + 128]
                    ll = WBl[:, kc * 512 + q * 128: kc * 512 + q * 128 + 128]
                    rh = P1h[:, kc * TS:(kc + 1) * TS]
                    rl = P1l[:, kc * TS:(kc + 1) * TS]
                    mm3(pb[1][:, 0:TS], lh, ll, rh, rl, kc == 0, kc == NCD - 1)
                p.bar()
                s3s = F1[:, 2 * TS:3 * TS]
                p.a(lambda e, s3s=s3s: e.activation(s3s, pb[0][:, 0:TS], AF.Silu))
                p.bar()
                p.v(lambda e, s3s=s3s: e.tensor_tensor(s3s, s3s, pb[1][:, 0:TS], OP.mult))
                p.bar()
                p.v(lambda e, s3s=s3s: e.tensor_tensor(s3s, s3s, F2[:, 512:1024], OP.mult))
                p.bar()
                p.v(lambda e, q=q, s3s=s3s:
                    e.tensor_copy(U1a[:, q * 512:(q + 1) * 512], s3s))
                p.v(lambda e, q=q, s3s=s3s:
                    e.scalar_tensor_tensor(U1b[:, q * 512:(q + 1) * 512], s3s, 0.0,
                                           U1a[:, q * 512:(q + 1) * 512], OP.add, OP.subtract))
                p.bar()
            dma("sync", sb3(P2h[:, 0:4096], 4, D), rows3(c_sdn[0], half * 4 * 128, 4, D))
            dma("sync", sb3(P2l[:, 0:4096], 4, D), rows3(c_sdn[1], half * 4 * 128, 4, D))
            p.bar()
            for dc in range(NCD):
                for q in range(4):
                    lh = P2h[:, q * D + dc * 128: q * D + dc * 128 + 128]
                    ll = P2l[:, q * D + dc * 128: q * D + dc * 128 + 128]
                    rh = U1a[:, q * 512 + 0: q * 512 + 512]
                    rl = U1b[:, q * 512 + 0: q * 512 + 512]
                    mm3(pb[dc][:, 0:TS], lh, ll, rh, rl, q == 0, q == 3)
            p.bar()
            for dc in range(NCD):
                rap = R1[:, dc * TS:(dc + 1) * TS]
                p.v(lambda e, b=pb[dc], rap=rap:
                    e.tensor_tensor(rap, rap, b[:, 0:TS], OP.add))
            p.bar()

        ln_stats(rc, None, VC_E5)
        SB = F1[:, 0:TS]
        QB2 = F1[:, TS:2 * TS]
        for c in range(NCD):
            t2 = F1[:, (2 + c % 2) * TS:(3 + c % 2) * TS]
            em = p.v if c % 2 == 0 else p.g
            em(lambda e, rcc=rc[c], t2=t2: e.tensor_tensor(t2, rcc, SB, OP.mult))
            em(lambda e, t2=t2: e.tensor_tensor(t2, t2, QB2, OP.subtract))
            em(lambda e, xcc=xc[c], t2=t2: e.tensor_tensor(xcc, xcc, t2, OP.add))
            if c % 2 == 1:
                p.bar()

        for blk in range(NBLK):
            for c in range(NCD):
                bank = pb[c // 4]
                p.t(lambda e, c=c, b=bank,
                    xap=X0[c][:, ti * TS + blk * 128: ti * TS + blk * 128 + 128]:
                    e.transpose(b[:, (c % 4) * 128:(c % 4) * 128 + 128], xap, sb_id[:]))
            p.bar()
            p.v(lambda e: e.tensor_copy(F2[:, 0:512], pb[0][:, 0:512]))
            p.v(lambda e: e.tensor_copy(F2[:, 512:1024], pb[1][:, 0:512]))
            p.bar()
            dma("gpsimd", d_out.ap()[ti * TS + blk * 128: ti * TS + (blk + 1) * 128, :],
                F2[:, 0:1024])
            p.bar()

    with (
        nc.semaphore("s_dma_sync") as s1,
        nc.semaphore("s_dma_gpsimd") as s2,
        nc.semaphore("s_x_gpsimd") as s3,
        nc.semaphore("s_x_vector") as s4,
        nc.semaphore("s_x_scalar") as s5,
        nc.semaphore("s_x_tensor") as s6,
        nc.semaphore("s_x_vector_d") as s7,
        nc.Block() as block,
    ):
        sems = {"dma_sync": s1, "dma_gpsimd": s2, "x_gpsimd": s3,
                "x_vector": s4, "x_scalar": s5, "x_tensor": s6,
                "x_vector_d": s7}
        p.emit(block, sems)

    return nc


def host_prep(inp):
    f32 = np.float32
    x = np.ascontiguousarray(np.asarray(inp["L0_attn_inputs"], f32)).reshape(2, 8192, D)
    wq = np.ascontiguousarray(np.asarray(inp["L0_attn_Wq"], f32).reshape(D, D))
    wk = np.ascontiguousarray(np.asarray(inp["L0_attn_Wk"], f32).reshape(D, D))
    wv = np.ascontiguousarray(np.asarray(inp["L0_attn_Wv"], f32).reshape(D, D))
    wo = np.ascontiguousarray(np.asarray(inp["L0_attn_Wo"], f32).reshape(D, D))
    w1 = np.asarray(inp["L0_attn_W1"], f32)
    w1t = np.ascontiguousarray(
        w1.reshape(8, 128, 8, 512).transpose(2, 0, 1, 3).reshape(8 * 8 * 128, 512))
    w2 = np.ascontiguousarray(np.asarray(inp["L0_attn_W2"], f32))
    e1 = np.ascontiguousarray(np.asarray(inp["L0_ffn_w1"], f32).reshape(NE * D, 1024))
    e2 = np.ascontiguousarray(np.asarray(inp["L0_ffn_w2"], f32).reshape(NE * D, 1024))
    sgu = np.ascontiguousarray(np.asarray(inp["L0_ffn_shared_gate_up"], f32))
    sdn = np.ascontiguousarray(np.asarray(inp["L0_ffn_shared_down"], f32))
    rw = np.ascontiguousarray(np.asarray(inp["L0_ffn_router_w"], f32)
                              .reshape(8, 128, 8).transpose(1, 0, 2).reshape(128, 64))
    sgw = np.ascontiguousarray(np.asarray(inp["L0_ffn_shared_gate_w"], f32)
                               .reshape(8, 128).T)

    def col8(v):
        return np.ascontiguousarray(np.asarray(v, f32).reshape(8, 128).T)

    vecs = np.zeros((128, 64), f32)
    vecs[:, VG1:VG1 + 8] = col8(inp["L0_attn_ln1_scale"])
    vecs[:, VB1:VB1 + 8] = col8(inp["L0_attn_ln1_bias"])
    vecs[:, VG2:VG2 + 8] = col8(inp["L0_attn_ln2_scale"])
    vecs[:, VB2:VB2 + 8] = col8(inp["L0_attn_ln2_bias"])
    vecs[:, VBQ:VBQ + 8] = col8(np.asarray(inp["L0_attn_bq"], f32).reshape(D))
    vecs[:, VBO:VBO + 8] = col8(inp["L0_attn_bo"])
    vecs[:, VB2b:VB2b + 8] = col8(inp["L0_attn_b2"])
    vecs[:, VC_LN1B] = 1e-11
    vecs[:, VC_LN1S] = 1.0 + 1e-6
    vecs[:, VC_E5] = 1e-5
    vecs[:, VC_E6] = 1e-6
    b1c = np.ascontiguousarray(np.asarray(inp["L0_attn_b1"], f32).reshape(32, 128).T)
    rowpack = np.zeros((1, 2176), f32)
    rowpack[0, 0:D] = np.asarray(inp["L0_attn_bk"], f32).reshape(D)
    rowpack[0, D:2 * D] = np.asarray(inp["L0_attn_bv"], f32).reshape(D)
    rowpack[0, 2048:2176] = 1.0

    ident = np.eye(128, dtype=f32)
    sels = np.zeros((16, 2304), f32)
    for j in range(2):
        sels[j, j * 128:(j + 1) * 128] = 1.0
    for c in range(8):
        sels[2 * c, 256 + c * 128:256 + c * 128 + 64] = 1.0
        sels[2 * c + 1, 256 + c * 128 + 64:256 + (c + 1) * 128] = 1.0
    for e_ in range(8):
        sels[e_, 1280 + e_ * 128:1280 + (e_ + 1) * 128] = 1.0
    onesel = np.zeros((128, 32), f32)
    onesel[:, 0] = 1.0
    onesel[:, 17] = 1.0

    shared = dict(wq=wq, wk=wk, wv=wv, wo=wo, w1t=w1t, w2=w2, e1=e1, e2=e2,
                  sgu=sgu, sdn=sdn, rw=rw, sgw=sgw, vecs=vecs, b1c=b1c,
                  rowpack=rowpack, ident=ident, sels=sels, onesel=onesel)
    in_maps = []
    for core in range(8):
        b = core // 4
        s0 = (core % 4) * TPC
        m = dict(shared)
        m["x"] = np.ascontiguousarray(x[b, s0:s0 + TPC])
        in_maps.append(m)
    return in_maps


def kernel(**inp):
    in_maps = host_prep(inp)
    nc = build_nc()
    res = run_bass_kernel_spmd(nc, in_maps, core_ids=list(range(8)))
    out = np.empty((2, 8192, D), np.float32)
    for core in range(8):
        b = core // 4
        s0 = (core % 4) * TPC
        out[b, s0:s0 + TPC] = np.asarray(res.results[core]["out"], np.float32)
    return out


# version 05
# speedup vs baseline: 1.0520x; 1.0520x over previous
import numpy as np
import concourse.bass as bass
import concourse.mybir as mybir
from concourse.bass_utils import run_bass_kernel_spmd
from concourse.alu_op_type import AluOpType as OP

F32 = mybir.dt.float32
BF16 = mybir.dt.bfloat16
AF = mybir.ActivationFunctionType
AX = mybir.AxisListType.X

TPC = 2048
NT = 4
TS = 512
NBLK = 4
D = 1024
NCD = 8
MLP = 4096
NE = 8

VG1, VB1, VG2, VB2, VBQ, VBO, VB2b = 0, 8, 16, 24, 32, 40, 48
VC_LN1B, VC_LN1S, VC_E5, VC_E6 = 56, 57, 58, 59


class Prog:

    def __init__(self):
        self.segs = [dict()]

    def _add(self, eng, fn, dma):
        self.segs[-1].setdefault(eng, []).append((fn, dma))

    def t(self, fn):
        self._add("tensor", fn, False)

    def v(self, fn):
        self._add("vector", fn, False)

    def a(self, fn):
        self._add("scalar", fn, False)

    def g(self, fn):
        self._add("gpsimd", fn, False)

    def dS(self, fn):
        self._add("sync", fn, True)

    def dG(self, fn):
        self._add("gpsimd", fn, True)

    def vdrain(self):
        self._add("vector", lambda e: e.drain(), False)

    def bar(self):
        if self.segs[-1]:
            self.segs.append(dict())

    def emit(self, block, sems):
        segs = [s for s in self.segs if s]
        n = len(segs)
        cum = {k: [0] * (n + 1) for k in sems if k != "x_vector_d"}
        for i, seg in enumerate(segs):
            for k in cum:
                cum[k][i + 1] = cum[k][i]
            for eng, ops in seg.items():
                ndma = sum(1 for _, d in ops if d)
                nex = sum(1 for _, d in ops if not d)
                if eng in ("sync", "gpsimd") and ndma:
                    cum["dma_" + eng][i + 1] += 16 * ndma
                if nex and eng != "sync":
                    cum["x_" + eng][i + 1] += 1

        def emit_engine(eng_name, eng):
            last_wait = {k: 0 for k in cum}
            for i, seg in enumerate(segs):
                ops = seg.get(eng_name)
                if not ops:
                    continue
                for k in cum:
                    if k == "x_" + eng_name and eng_name != "gpsimd":
                        continue
                    tgt = cum[k][i]
                    if tgt > last_wait[k]:
                        eng.wait_ge(sems[k], tgt)
                        last_wait[k] = tgt
                last_exec = None
                for fn, dma_ in ops:
                    instr = fn(eng)
                    if dma_:
                        instr.then_inc(sems["dma_" + eng_name], 16)
                    else:
                        last_exec = instr
                if last_exec is not None and eng_name != "sync":
                    last_exec.then_inc(sems["x_" + eng_name], 1)
            if eng_name == "sync":
                for k in cum:
                    if cum[k][n] > last_wait[k]:
                        eng.wait_ge(sems[k], cum[k][n])

        @block.sync
        def _(sync):
            emit_engine("sync", sync)

        @block.gpsimd
        def _(gpsimd):
            emit_engine("gpsimd", gpsimd)

        @block.vector
        def _(vector):
            emit_engine("vector", vector)

        @block.scalar
        def _(scalar):
            emit_engine("scalar", scalar)

        @block.tensor
        def _(tensor):
            emit_engine("tensor", tensor)


def rows3(t2d, r0, nblk, cols, c0=0):
    ap = t2d.ap()[r0:r0 + nblk * 128, :].rearrange("(a p) c -> p a c", p=128)
    if c0 != 0 or cols != ap.shape[2]:
        ap = ap[:, :, c0:c0 + cols]
    return ap


def sb3(sb_ap, nblk, cols):
    return sb_ap.rearrange("p (a c) -> p a c", c=cols)


def build_nc(debug=False):
    nc = bass.Bass("TRN2", target_bir_lowering=False)

    d_x = nc.dram_tensor("x", [TPC, D], F32, kind="ExternalInput")
    d_wq = nc.dram_tensor("wq", [NCD * 128, D], F32, kind="ExternalInput")
    d_wk = nc.dram_tensor("wk", [NCD * 128, D], F32, kind="ExternalInput")
    d_wv = nc.dram_tensor("wv", [NCD * 128, D], F32, kind="ExternalInput")
    d_wo = nc.dram_tensor("wo", [NCD * 128, D], F32, kind="ExternalInput")
    d_w1 = nc.dram_tensor("w1t", [8 * 8 * 128, 512], F32, kind="ExternalInput")
    d_w2 = nc.dram_tensor("w2", [32 * 128, D], F32, kind="ExternalInput")
    d_e1 = nc.dram_tensor("e1", [NE * 8 * 128, 1024], F32, kind="ExternalInput")
    d_e2 = nc.dram_tensor("e2", [NE * 8 * 128, 1024], F32, kind="ExternalInput")
    d_sgu = nc.dram_tensor("sgu", [8 * 128, 2048], F32, kind="ExternalInput")
    d_sdn = nc.dram_tensor("sdn", [8 * 128, 1024], F32, kind="ExternalInput")
    d_rw = nc.dram_tensor("rw", [128, 64], F32, kind="ExternalInput")
    d_sgw = nc.dram_tensor("sgw", [128, 8], F32, kind="ExternalInput")
    d_vecs = nc.dram_tensor("vecs", [128, 64], F32, kind="ExternalInput")
    d_b1c = nc.dram_tensor("b1c", [128, 32], F32, kind="ExternalInput")
    d_rowpack = nc.dram_tensor("rowpack", [1, 2176], F32, kind="ExternalInput")
    d_id = nc.dram_tensor("ident", [128, 128], F32, kind="ExternalInput")
    d_sels = nc.dram_tensor("sels", [16, 2304], F32, kind="ExternalInput")
    d_onesel = nc.dram_tensor("onesel", [128, 32], F32, kind="ExternalInput")

    d_out = nc.dram_tensor("out", [TPC, D], F32, kind="ExternalOutput")

    d_ccin = nc.dram_tensor("ccin", [128, 520], F32)
    d_ccout = nc.dram_tensor("ccout", [128, 520], F32)

    def cache(name, rows, cols):
        return (nc.dram_tensor(name + "_h", [rows, cols], BF16),
                nc.dram_tensor(name + "_l", [rows, cols], BF16))

    c_wq = cache("c_wq", NCD * 128, D)
    c_wk = cache("c_wk", NCD * 128, D)
    c_wv = cache("c_wv", NCD * 128, D)
    c_wo = cache("c_wo", NCD * 128, D)
    c_w1 = cache("c_w1", 8 * 8 * 128, 512)
    c_w2 = cache("c_w2", 32 * 128, D)
    c_e1 = cache("c_e1", NE * 8 * 128, 1024)
    c_e2 = cache("c_e2", NE * 8 * 128, 1024)
    c_sgu = cache("c_sgu", 8 * 128, 2048)
    c_sdn = cache("c_sdn", 8 * 128, 1024)

    d_pqh = nc.dram_tensor("pqh", [NT * NCD * 128, TS], BF16)
    d_pql = nc.dram_tensor("pql", [NT * NCD * 128, TS], BF16)
    d_rows = nc.dram_tensor("rows", [2, TPC], F32)

    X0 = [nc.alloc_sbuf_tensor(f"x0_{c}", [128, TPC], F32) for c in range(NCD)]
    kvacc = nc.alloc_sbuf_tensor("kvacc", [128, 520], F32)
    kvh = nc.alloc_sbuf_tensor("kvh", [128, 520], BF16)
    kvl = nc.alloc_sbuf_tensor("kvl", [128, 520], BF16)
    kselh = nc.alloc_sbuf_tensor("kselh", [128, 128], BF16)
    ksell = nc.alloc_sbuf_tensor("ksell", [128, 128], BF16)
    sb_id = nc.alloc_sbuf_tensor("sb_id", [128, 128], F32)
    sb_sels = nc.alloc_sbuf_tensor("sb_sels", [16, 2304], F32)
    sb_onesel = nc.alloc_sbuf_tensor("sb_onesel", [128, 32], F32)
    sb_rowpack = nc.alloc_sbuf_tensor("sb_rowpack", [1, 2176], F32)
    sb_vecs = nc.alloc_sbuf_tensor("sb_vecs", [128, 64], F32)
    sb_b1c = nc.alloc_sbuf_tensor("sb_b1c", [128, 32], F32)
    sb_rwh = nc.alloc_sbuf_tensor("sb_rwh", [128, 64], BF16)
    sb_rwl = nc.alloc_sbuf_tensor("sb_rwl", [128, 64], BF16)
    sb_sgwh = nc.alloc_sbuf_tensor("sb_sgwh", [128, 8], BF16)
    sb_sgwl = nc.alloc_sbuf_tensor("sb_sgwl", [128, 8], BF16)
    P1h = nc.alloc_sbuf_tensor("P1h", [128, 4096], BF16)
    P1l = nc.alloc_sbuf_tensor("P1l", [128, 4096], BF16)
    P2h = nc.alloc_sbuf_tensor("P2h", [128, 4096], BF16)
    P2l = nc.alloc_sbuf_tensor("P2l", [128, 4096], BF16)
    R1 = nc.alloc_sbuf_tensor("R1", [128, 4096], F32)
    F1 = nc.alloc_sbuf_tensor("F1", [128, 2048], F32)
    F2 = nc.alloc_sbuf_tensor("F2", [128, 1024], F32)
    WAh = nc.alloc_sbuf_tensor("WAh", [128, 8192], BF16)
    WAl = nc.alloc_sbuf_tensor("WAl", [128, 8192], BF16)
    WBh = nc.alloc_sbuf_tensor("WBh", [128, 4096], BF16)
    WBl = nc.alloc_sbuf_tensor("WBl", [128, 4096], BF16)
    U1a = nc.alloc_sbuf_tensor("U1a", [128, 2112], BF16)
    U1b = nc.alloc_sbuf_tensor("U1b", [128, 2112], BF16)

    pb = [nc.alloc_psum_tensor(f"pb{i}", [128, 512], F32) for i in range(8)]

    p = Prog()

    def mm3(out_ap, lh, ll, rh, rl, first, last):
        p.t(lambda e, o=out_ap, a=lh, b=rh, f=first: e.matmul(o, a, b, start=f, stop=False))
        p.t(lambda e, o=out_ap, a=lh, b=rl: e.matmul(o, a, b, start=False, stop=False))
        p.t(lambda e, o=out_ap, a=ll, b=rh, s=last: e.matmul(o, a, b, start=False, stop=s))

    def dma(q, out_ap, in_ap):
        fn = lambda e, o=out_ap, i=in_ap: e.dma_start(out=o, in_=i)
        (p.dS if q == "sync" else p.dG)(fn)

    def dbg(name, ap):
        if not debug:
            return
        shape = list(ap.shape)
        t = nc.dram_tensor("dbg_" + name, shape, ap.dtype, kind="ExternalOutput")
        p.bar()
        dma("gpsimd", t.ap(), ap)
        p.bar()

    def ln_stats(src_aps, sqrt_scale_col, sqrt_bias_col, keep_ti=None):
        for half in (0, 1):
            for j in range(4):
                c = half * 4 + j
                p.a(lambda e, s=src_aps[c], d=F1[:, j * TS:(j + 1) * TS]:
                    e.activation(d, s, AF.Square))
            p.bar()
            for j in range(4):
                c = half * 4 + j
                p.t(lambda e, r=src_aps[c], f=(c == 0):
                    e.matmul(pb[6][0:16, 0:TS], sb_onesel[:, 0:16], r, start=f, stop=False))
                p.t(lambda e, r=F1[:, j * TS:(j + 1) * TS], s=(c == 7):
                    e.matmul(pb[6][0:16, 0:TS], sb_onesel[:, 16:32], r, start=False, stop=s))
            p.bar()
        p.v(lambda e: e.tensor_copy(F2[0:2, 0:TS], pb[6][0:2, 0:TS]))
        p.bar()
        if keep_ti is not None:
            dma("gpsimd", d_rows.ap()[0:2, keep_ti * TS:(keep_ti + 1) * TS],
                F2[0:2, 0:TS])
        p.t(lambda e: e.matmul(pb[6][:, 0:TS], sb_sels[:, 0:128], F2[0:16, 0:TS],
                               start=True, stop=True))
        p.t(lambda e: e.matmul(pb[7][:, 0:TS], sb_sels[:, 128:256], F2[0:16, 0:TS],
                               start=True, stop=True))
        p.bar()
        sm = F1[:, 0:TS]
        sqm = F1[:, TS:2 * TS]
        p.v(lambda e: e.tensor_scalar(sqm, pb[6][:, 0:TS], 1.0 / D, None, OP.mult))
        p.bar()
        p.v(lambda e: e.tensor_tensor(sm, sqm, sqm, OP.mult))
        p.bar()
        p.v(lambda e: e.scalar_tensor_tensor(sm, pb[7][:, 0:TS], 1.0 / D, sm,
                                             OP.mult, OP.subtract))
        p.bar()
        if sqrt_scale_col is None:
            p.a(lambda e, bc=sqrt_bias_col:
                e.activation(sm, sm, AF.Sqrt, bias=sb_vecs[:, bc:bc + 1]))
        else:
            p.a(lambda e, bc=sqrt_bias_col, sc=sqrt_scale_col:
                e.activation(sm, sm, AF.Sqrt, bias=sb_vecs[:, bc:bc + 1],
                             scale=sb_vecs[:, sc:sc + 1]))
        p.bar()
        p.v(lambda e: e.reciprocal(sm, sm))
        p.bar()
        p.v(lambda e: e.tensor_tensor(sqm, sqm, sm, OP.mult))
        p.bar()

    dma("sync", sb_id[:], d_id.ap())
    dma("sync", sb_sels[:], d_sels.ap())
    dma("sync", sb_onesel[:], d_onesel.ap())
    dma("sync", sb_vecs[:], d_vecs.ap())
    dma("sync", sb_b1c[:], d_b1c.ap())
    dma("sync", sb_rowpack[:], d_rowpack.ap())
    p.v(lambda e: e.memset(kvacc[:], 0.0))
    p.v(lambda e: e.memset(F2[:], 0.0))
    p.bar()

    for blk in range(TPC // 128):
        dma("sync", F2[:, 0:1024], d_x.ap()[blk * 128:(blk + 1) * 128, :])
        p.bar()
        for c in range(NCD):
            bank = pb[c // 4]
            p.t(lambda e, c=c, b=bank:
                e.transpose(b[:, (c % 4) * 128:(c % 4) * 128 + 128],
                            F2[:, c * 128:(c + 1) * 128], sb_id[:]))
        p.bar()
        for c in range(NCD):
            bank = pb[c // 4]
            p.v(lambda e, c=c, b=bank, blk=blk:
                e.tensor_copy(X0[c][:, blk * 128:(blk + 1) * 128],
                              b[:, (c % 4) * 128:(c % 4) * 128 + 128]))
        p.bar()

    dbg("x0c0", X0[0][:, 0:512])

    slabs = []

    def add_slabs(src, ch, cl, nrb, cols):
        g = max(1, 2048 // cols)
        r = 0
        while r < nrb:
            gg = min(g, nrb - r)
            slabs.append((src, ch, cl, r * 128, gg, cols))
            r += gg

    add_slabs(d_wq, *c_wq, NCD, D)
    add_slabs(d_wk, *c_wk, NCD, D)
    add_slabs(d_wv, *c_wv, NCD, D)
    add_slabs(d_wo, *c_wo, NCD, D)
    add_slabs(d_w1, *c_w1, 64, 512)
    add_slabs(d_w2, *c_w2, 32, D)
    add_slabs(d_e1, *c_e1, 64, 1024)
    add_slabs(d_e2, *c_e2, 64, 1024)
    add_slabs(d_sgu, *c_sgu, 8, 2048)
    add_slabs(d_sdn, *c_sdn, 8, 1024)

    nsl = len(slabs)

    def p0in(k):
        s = k % 3
        return (R1[:, 0:2048] if s == 0 else
                R1[:, 2048:4096] if s == 1 else F1[:, 0:2048])

    for k in range(nsl + 3):
        if k < nsl:
            src_, ch, cl, r0, nb_, cols = slabs[k]
            w = nb_ * cols
            dma("sync", sb3(p0in(k)[:, 0:w], nb_, cols), rows3(src_, r0, nb_, cols))
        if 1 <= k <= nsl:
            src_, ch, cl, r0, nb_, cols = slabs[k - 1]
            w = nb_ * cols
            half = ((k - 1) % 2) * 2048
            p.a(lambda e, w=w, h=half, kk=k - 1:
                e.copy(P1h[:, h:h + w], p0in(kk)[:, 0:w]))
        if 2 <= k <= nsl + 1:
            src_, ch, cl, r0, nb_, cols = slabs[k - 2]
            w = nb_ * cols
            half = ((k - 2) % 2) * 2048
            p.v(lambda e, w=w, h=half, kk=k - 2:
                e.tensor_tensor(P1l[:, h:h + w], p0in(kk)[:, 0:w],
                                P1h[:, h:h + w], OP.subtract))
            dma("gpsimd", rows3(ch, r0, nb_, cols), sb3(P1h[:, half:half + w], nb_, cols))
        if 3 <= k:
            src_, ch, cl, r0, nb_, cols = slabs[k - 3]
            w = nb_ * cols
            half = ((k - 3) % 2) * 2048
            dma("gpsimd", rows3(cl, r0, nb_, cols), sb3(P1l[:, half:half + w], nb_, cols))
        p.bar()

    dma("sync", F1[:, 0:64], d_rw.ap())
    dma("sync", F1[:, 64:72], d_sgw.ap())
    p.bar()
    p.v(lambda e: e.tensor_copy(sb_rwh[:], F1[:, 0:64]))
    p.v(lambda e: e.tensor_copy(sb_sgwh[:], F1[:, 64:72]))
    p.bar()
    p.v(lambda e: e.tensor_tensor(sb_rwl[:], F1[:, 0:64], sb_rwh[:], OP.subtract))
    p.v(lambda e: e.tensor_tensor(sb_sgwl[:], F1[:, 64:72], sb_sgwh[:], OP.subtract))
    p.bar()

    for ti in range(NT):
        tcol = slice(ti * TS, (ti + 1) * TS)
        xc = [X0[c][:, tcol] for c in range(NCD)]

        ln_stats(xc, VC_LN1S, VC_LN1B, keep_ti=ti)
        if ti == 0:
            dbg("ln1U", F1[:, 0:TS])
            dbg("ln1Q", F1[:, TS:2 * TS])
        UB = F1[:, 0:TS]
        QB = F1[:, TS:2 * TS]
        for c in range(NCD):
            t2 = F1[:, 2 * TS:3 * TS]
            af = F1[:, 3 * TS:4 * TS]
            p.v(lambda e, xcc=xc[c], t2=t2: e.tensor_tensor(t2, xcc, UB, OP.mult))
            p.bar()
            if ti == 0 and c == 0:
                dbg("t2a", t2)
            p.v(lambda e, t2=t2: e.tensor_tensor(t2, t2, QB, OP.subtract))
            p.bar()
            if ti == 0 and c == 0:
                dbg("t2b", t2)
            p.a(lambda e, c=c, t2=t2, af=af:
                e.activation(af, t2, AF.Identity,
                             bias=sb_vecs[:, VB1 + c:VB1 + c + 1],
                             scale=sb_vecs[:, VG1 + c:VG1 + c + 1]))
            p.bar()
            if ti == 0 and c == 0:
                dbg("af0", af)
            p.v(lambda e, c=c, af=af: e.tensor_copy(P1h[:, c * TS:(c + 1) * TS], af))
            p.bar()
            p.v(lambda e, c=c, af=af:
                e.tensor_tensor(P1l[:, c * TS:(c + 1) * TS], af,
                                P1h[:, c * TS:(c + 1) * TS], OP.subtract))
            p.bar()

        if ti == 0:
            dbg("ah0", P1h[:, 0:TS])
            dbg("al0", P1l[:, 0:TS])
        dma("sync", sb3(WAh[:, 0:8192], 8, D), rows3(c_wq[0], 0, 8, D))
        dma("sync", sb3(WAl[:, 0:8192], 8, D), rows3(c_wq[1], 0, 8, D))
        p.bar()
        def q_relu(oc):
            fq = F1[:, (2 + oc % 2) * TS:(3 + oc % 2) * TS]
            p.a(lambda e, b=pb[oc % 2], fq=fq, oc=oc:
                e.activation(fq, b[:, 0:TS], AF.Relu,
                             bias=sb_vecs[:, VBQ + oc:VBQ + oc + 1]))

        def q_split(oc):
            fq = F1[:, (2 + oc % 2) * TS:(3 + oc % 2) * TS]
            p.v(lambda e, fq=fq, oc=oc:
                e.tensor_scalar(P2h[:, oc * TS:(oc + 1) * TS], fq, 0.001, None, OP.add))
            p.v(lambda e, fq=fq, oc=oc:
                e.scalar_tensor_tensor(P2l[:, oc * TS:(oc + 1) * TS], fq, 0.001,
                                       P2h[:, oc * TS:(oc + 1) * TS], OP.add, OP.subtract))

        for oc in range(NCD + 2):
            if oc < NCD:
                bank = pb[oc % 2]
                for kc in range(NCD):
                    lh = WAh[:, kc * D + oc * 128: kc * D + oc * 128 + 128]
                    ll = WAl[:, kc * D + oc * 128: kc * D + oc * 128 + 128]
                    rh = P1h[:, kc * TS:(kc + 1) * TS]
                    rl = P1l[:, kc * TS:(kc + 1) * TS]
                    mm3(bank[:, 0:TS], lh, ll, rh, rl, kc == 0, kc == NCD - 1)
            if oc >= 1 and oc - 1 < NCD:
                q_relu(oc - 1)
            if oc >= 2:
                q_split(oc - 2)
            p.bar()
        if ti == 0:
            dbg("pqh0", P2h[:, 0:TS])
        dma("gpsimd", rows3(d_pqh, ti * NCD * 128, 8, TS), sb3(P2h[:, 0:4096], 8, TS))
        dma("gpsimd", rows3(d_pql, ti * NCD * 128, 8, TS), sb3(P2l[:, 0:4096], 8, TS))
        p.bar()

        dma("sync", sb3(WAh[:, 0:8192], 8, D), rows3(c_wk[0], 0, 8, D))
        dma("sync", sb3(WAl[:, 0:8192], 8, D), rows3(c_wk[1], 0, 8, D))
        dma("sync", sb3(WBh[:, 0:4096], 4, D), rows3(c_wv[0], 0, 4, D))
        dma("sync", sb3(WBl[:, 0:4096], 4, D), rows3(c_wv[0], 4 * 128, 4, D))
        dma("sync", sb3(P2h[:, 0:4096], 4, D), rows3(c_wv[1], 0, 4, D))
        dma("sync", sb3(P2l[:, 0:4096], 4, D), rows3(c_wv[1], 4 * 128, 4, D))
        p.bar()

        def wvh(kc, hf):
            idx = kc * D + hf * 512
            t = WBh if idx < 4096 else WBl
            return t[:, idx % 4096: idx % 4096 + 512]

        def wvl(kc, hf):
            idx = kc * D + hf * 512
            t = P2h if idx < 4096 else P2l
            return t[:, idx % 4096: idx % 4096 + 512]

        for blk in range(NBLK):
            for hf in range(2):
                bank = pb[2 + hf]
                for kc in range(NCD):
                    ah = P1h[:, kc * TS + blk * 128: kc * TS + blk * 128 + 128]
                    al = P1l[:, kc * TS + blk * 128: kc * TS + blk * 128 + 128]
                    rh = WAh[:, kc * D + hf * 512: kc * D + hf * 512 + 512]
                    rl = WAl[:, kc * D + hf * 512: kc * D + hf * 512 + 512]
                    mm3(bank[:, 0:512], ah, al, rh, rl, kc == 0, False)
                p.t(lambda e, b=bank, hf=hf:
                    e.matmul(b[:, 0:512], sb_rowpack[0:1, 2048:2176],
                             sb_rowpack[0:1, hf * 512:hf * 512 + 512],
                             start=False, stop=True))
            p.bar()
            for hf in range(2):
                bank = pb[2 + hf]
                fk = F1[:, (2 + hf) * TS:(3 + hf) * TS]
                p.a(lambda e, b=bank, fk=fk: e.activation(fk, b[:, 0:512], AF.Relu))
            p.bar()
            for hf in range(2):
                fk = F1[:, (2 + hf) * TS:(3 + hf) * TS]
                p.v(lambda e, fk=fk, hf=hf:
                    e.tensor_scalar(U1a[:, hf * 512:hf * 512 + 512], fk, 0.001, None, OP.add))
                p.v(lambda e, fk=fk, hf=hf:
                    e.scalar_tensor_tensor(U1b[:, hf * 512:hf * 512 + 512], fk, 0.001,
                                           U1a[:, hf * 512:hf * 512 + 512], OP.add, OP.subtract))
            p.bar()
            for hf in range(2):
                bank = pb[2 + hf]
                for kc in range(NCD):
                    ah = P1h[:, kc * TS + blk * 128: kc * TS + blk * 128 + 128]
                    al = P1l[:, kc * TS + blk * 128: kc * TS + blk * 128 + 128]
                    mm3(bank[:, 0:512], ah, al, wvh(kc, hf), wvl(kc, hf), kc == 0, False)
                p.t(lambda e, b=bank, hf=hf:
                    e.matmul(b[:, 0:512], sb_rowpack[0:1, 2048:2176],
                             sb_rowpack[0:1, 1024 + hf * 512:1024 + hf * 512 + 512],
                             start=False, stop=True))
            p.bar()
            for hf in range(2):
                bank = pb[2 + hf]
                src = bass.AP(bank, 0, [[512, 128], [64, 8], [1, 64]])
                dsth = bass.AP(U1a, 1024 + hf * 8 * 65, [[2112, 128], [65, 8], [1, 64]])
                dstl = bass.AP(U1b, 1024 + hf * 8 * 65, [[2112, 128], [65, 8], [1, 64]])
                p.v(lambda e, s=src, d=dsth: e.tensor_copy(d, s))
                p.v(lambda e, s=src, dh=dsth, dl=dstl:
                    e.scalar_tensor_tensor(dl, s, 0.0, dh, OP.add, OP.subtract))
            onesh = bass.AP(U1a, 1024 + 64, [[2112, 128], [65, 16], [1, 1]])
            onesl = bass.AP(U1b, 1024 + 64, [[2112, 128], [65, 16], [1, 1]])
            p.v(lambda e, o=onesh: e.memset(o, 1.0))
            p.v(lambda e, o=onesl: e.memset(o, 0.0))
            p.bar()
            if ti == 0 and blk == 0:
                dbg("u1a0", U1a[:, 0:2112])
                dbg("u1b0", U1b[:, 0:2112])
            for rnd in range(2):
                for j in range(4):
                    pair = rnd * 4 + j
                    bank = pb[4 + j]
                    for sub in range(2):
                        h_ = pair * 2 + sub
                        lh = U1a[:, h_ * 64:(h_ + 1) * 64]
                        ll = U1b[:, h_ * 64:(h_ + 1) * 64]
                        rh = U1a[:, 1024 + h_ * 65:1024 + (h_ + 1) * 65]
                        rl = U1b[:, 1024 + h_ * 65:1024 + (h_ + 1) * 65]
                        mm3(bank[sub * 64:(sub + 1) * 64, 0:65], lh, ll, rh, rl, True, True)
                p.bar()
                if debug and ti == 0 and blk == 0 and rnd == 0:
                    p.v(lambda e: e.tensor_copy(F1[:, 0:65], pb[4][:, 0:65]))
                    dbg("kvps0", F1[:, 0:65])
                for j in range(4):
                    pair = rnd * 4 + j
                    bank = pb[4 + j]
                    p.v(lambda e, b=bank, pr=pair:
                        e.tensor_tensor(kvacc[:, pr * 65:(pr + 1) * 65],
                                        kvacc[:, pr * 65:(pr + 1) * 65],
                                        b[:, 0:65], OP.add))
                p.bar()
                if debug and ti == 0 and blk == 0 and rnd == 0:
                    dbg("kvacc0", kvacc[:])

    dbg("kvpart", kvacc[:])
    dma("gpsimd", d_ccin.ap(), kvacc[:])
    p.bar()
    dbg("ccin", d_ccin.ap())
    p.g(lambda e: e.collective_compute(
        "AllReduce", mybir.AluOpType.add,
        replica_groups=[[0, 1, 2, 3], [4, 5, 6, 7]],
        ins=[d_ccin.ap()], outs=[d_ccout.ap()]))
    p.bar()
    dbg("ccout", d_ccout.ap())
    dma("gpsimd", kvacc[:], d_ccout.ap())
    p.bar()
    dbg("kvback", kvacc[:])
    p.v(lambda e: e.tensor_copy(kvh[:], kvacc[:]))
    p.bar()
    p.v(lambda e: e.tensor_tensor(kvl[:], kvacc[:], kvh[:], OP.subtract))
    p.v(lambda e: e.memset(kselh[:], 0.0))
    p.v(lambda e: e.memset(ksell[:], 0.0))
    p.bar()
    dbg("kvfull", kvacc[:])
    p.vdrain()
    for c in range(NCD):
        p.v(lambda e, c=c: e.tensor_copy(kselh[0:64, c * 16 + 2 * c:c * 16 + 2 * c + 1],
                                         kvh[0:64, c * 65 + 64:c * 65 + 65]))
        p.v(lambda e, c=c: e.tensor_copy(kselh[64:128, c * 16 + 2 * c + 1:c * 16 + 2 * c + 2],
                                         kvh[64:128, c * 65 + 64:c * 65 + 65]))
        p.v(lambda e, c=c: e.tensor_copy(ksell[0:64, c * 16 + 2 * c:c * 16 + 2 * c + 1],
                                         kvl[0:64, c * 65 + 64:c * 65 + 65]))
        p.v(lambda e, c=c: e.tensor_copy(ksell[64:128, c * 16 + 2 * c + 1:c * 16 + 2 * c + 2],
                                         kvl[64:128, c * 65 + 64:c * 65 + 65]))
    p.vdrain()
    p.bar()

    for ti in range(NT):
        tcol = slice(ti * TS, (ti + 1) * TS)
        xc = [X0[c][:, tcol] for c in range(NCD)]
        rc = [R1[:, c * TS:(c + 1) * TS] for c in range(NCD)]

        if ti == 0:
            dma("sync", sb3(P1h[:, 0:4096], 8, TS), rows3(d_pqh, 0, 8, TS))
            dma("sync", sb3(P1l[:, 0:4096], 8, TS), rows3(d_pql, 0, 8, TS))
            dma("sync", sb3(WAh[:, 0:8192], 8, D), rows3(c_wo[0], 0, 8, D))
            dma("sync", sb3(WAl[:, 0:8192], 8, D), rows3(c_wo[1], 0, 8, D))
            p.bar()

        for kc in range(NCD):
            rh = P1h[:, kc * TS:(kc + 1) * TS]
            rl = P1l[:, kc * TS:(kc + 1) * TS]
            mm3(pb[2][0:16, 0:TS], kselh[:, kc * 16:(kc + 1) * 16],
                ksell[:, kc * 16:(kc + 1) * 16], rh, rl, kc == 0, kc == NCD - 1)
        p.bar()
        p.v(lambda e: e.reciprocal(F2[0:16, 0:512], pb[2][0:16, 0:TS]))
        p.bar()
        if ti == 0:
            dbg("recip0", F2[0:16, 0:512])

        def attn_evac(c):
            nb_ = pb[c % 2]
            bb_ = pb[3 + c % 2]
            rb = F2[:, 512:1024]
            fa = F1[:, (2 + c % 2) * TS:(3 + c % 2) * TS]
            p.v(lambda e, b=bb_, rb=rb: e.tensor_copy(rb, b[:, 0:TS]))
            p.v(lambda e, b=nb_, rb=rb, fa=fa: e.tensor_tensor(fa, b[:, 0:TS], rb, OP.mult))
            p.v(lambda e, c=c, fa=fa: e.tensor_copy(P2h[:, c * TS:(c + 1) * TS], fa))
            p.v(lambda e, c=c, fa=fa:
                e.tensor_tensor(P2l[:, c * TS:(c + 1) * TS], fa,
                                P2h[:, c * TS:(c + 1) * TS], OP.subtract))

        for c in range(NCD + 1):
            if c < NCD:
                nb_ = pb[c % 2]
                bb_ = pb[3 + c % 2]
                for sub in range(2):
                    lh = kvh[sub * 64:(sub + 1) * 64, c * 65:c * 65 + 64]
                    ll = kvl[sub * 64:(sub + 1) * 64, c * 65:c * 65 + 64]
                    rh = P1h[sub * 64:(sub + 1) * 64, c * TS:(c + 1) * TS]
                    rl = P1l[sub * 64:(sub + 1) * 64, c * TS:(c + 1) * TS]
                    mm3(nb_[sub * 64:(sub + 1) * 64, 0:TS], lh, ll, rh, rl, True, True)
                p.t(lambda e, c=c, b=bb_:
                    e.matmul(b[:, 0:TS], sb_sels[:, 256 + c * 128:256 + (c + 1) * 128],
                             F2[0:16, 0:512], start=True, stop=True))
            if c >= 1:
                attn_evac(c - 1)
            p.bar()

        dma("sync", F2[0:2, 0:512], d_rows.ap()[0:2, tcol])
        p.bar()
        p.t(lambda e: e.matmul(pb[6][:, 0:TS], sb_sels[:, 0:128],
                               F2[0:16, 0:512], start=True, stop=True))
        p.t(lambda e: e.matmul(pb[7][:, 0:TS], sb_sels[:, 128:256],
                               F2[0:16, 0:512], start=True, stop=True))
        p.bar()
        S1B = F1[:, 0:TS]
        MS1B = F1[:, TS:2 * TS]
        p.v(lambda e: e.tensor_scalar(MS1B, pb[6][:, 0:TS], 1.0 / D, None, OP.mult))
        p.bar()
        p.v(lambda e: e.tensor_tensor(S1B, MS1B, MS1B, OP.mult))
        p.bar()
        p.v(lambda e: e.scalar_tensor_tensor(S1B, pb[7][:, 0:TS], 1.0 / D, S1B,
                                             OP.mult, OP.subtract))
        p.bar()
        p.a(lambda e: e.activation(S1B, S1B, AF.Sqrt, bias=sb_vecs[:, VC_E5:VC_E5 + 1]))
        p.bar()
        p.v(lambda e: e.reciprocal(S1B, S1B))
        p.bar()
        p.v(lambda e: e.tensor_tensor(MS1B, MS1B, S1B, OP.mult))
        p.bar()

        for oc in range(NCD):
            bank = pb[oc]
            for kc in range(NCD):
                lh = WAh[:, kc * D + oc * 128: kc * D + oc * 128 + 128]
                ll = WAl[:, kc * D + oc * 128: kc * D + oc * 128 + 128]
                rh = P2h[:, kc * TS:(kc + 1) * TS]
                rl = P2l[:, kc * TS:(kc + 1) * TS]
                mm3(bank[:, 0:TS], lh, ll, rh, rl, kc == 0, kc == NCD - 1)
        p.bar()
        for oc in range(NCD):
            bank = pb[oc]
            t3 = F1[:, 2 * TS:3 * TS]
            p.v(lambda e, t3=t3, xop=X0[oc][:, tcol]:
                e.tensor_tensor(t3, xop, S1B, OP.mult))
            p.bar()
            p.v(lambda e, t3=t3: e.tensor_tensor(t3, t3, MS1B, OP.subtract))
            p.bar()
            p.v(lambda e, oc=oc, b=bank, t3=t3:
                e.scalar_tensor_tensor(R1[:, oc * TS:(oc + 1) * TS],
                                       b[:, 0:TS], sb_vecs[:, VBO + oc:VBO + oc + 1],
                                       t3, OP.add, OP.add))
            p.bar()

        ln_stats(rc, None, VC_E6)
        UB = F1[:, 0:TS]
        QB = F1[:, TS:2 * TS]
        for c in range(NCD):
            t2 = F1[:, 2 * TS:3 * TS]
            yf = F1[:, 3 * TS:4 * TS]
            p.v(lambda e, rcc=rc[c], t2=t2: e.tensor_tensor(t2, rcc, UB, OP.mult))
            p.bar()
            p.v(lambda e, t2=t2: e.tensor_tensor(t2, t2, QB, OP.subtract))
            p.bar()
            p.a(lambda e, c=c, t2=t2, yf=yf:
                e.activation(yf, t2, AF.Identity,
                             bias=sb_vecs[:, VB2 + c:VB2 + c + 1],
                             scale=sb_vecs[:, VG2 + c:VG2 + c + 1]))
            p.bar()
            p.v(lambda e, c=c, yf=yf: e.tensor_copy(P1h[:, c * TS:(c + 1) * TS], yf))
            p.bar()
            p.v(lambda e, c=c, yf=yf:
                e.tensor_tensor(P1l[:, c * TS:(c + 1) * TS], yf,
                                P1h[:, c * TS:(c + 1) * TS], OP.subtract))
            p.bar()

        dma("sync", sb3(WAh[:, 0:4096], 8, 512), rows3(c_w1[0], 0, 8, 512))
        dma("sync", sb3(WAl[:, 0:4096], 8, 512), rows3(c_w1[1], 0, 8, 512))
        dma("sync", sb3(WBh[:, 0:4096], 4, D), rows3(c_w2[0], 0, 4, D))
        dma("sync", sb3(WBl[:, 0:4096], 4, D), rows3(c_w2[1], 0, 4, D))
        p.bar()
        for ft in range(8):
            ha = (ft % 2) * 4096
            for j in range(4):
                bank = pb[j]
                for kc in range(NCD):
                    lh = WAh[:, ha + kc * 512 + j * 128: ha + kc * 512 + j * 128 + 128]
                    ll = WAl[:, ha + kc * 512 + j * 128: ha + kc * 512 + j * 128 + 128]
                    rh = P1h[:, kc * TS:(kc + 1) * TS]
                    rl = P1l[:, kc * TS:(kc + 1) * TS]
                    mm3(bank[:, 0:TS], lh, ll, rh, rl, kc == 0, kc == NCD - 1)
            p.bar()
            for j in range(4):
                gf = F1[:, j * TS:(j + 1) * TS]
                p.a(lambda e, b=pb[j], gf=gf, ft=ft, j=j:
                    e.activation(gf, b[:, 0:TS], AF.Gelu_apprx_tanh,
                                 bias=sb_b1c[:, ft * 4 + j: ft * 4 + j + 1]))
            p.bar()
            for j in range(4):
                gf = F1[:, j * TS:(j + 1) * TS]
                p.v(lambda e, j=j, gf=gf: e.tensor_copy(U1a[:, j * TS:(j + 1) * TS], gf))
                p.v(lambda e, j=j, gf=gf:
                    e.scalar_tensor_tensor(U1b[:, j * TS:(j + 1) * TS], gf, 0.0,
                                           U1a[:, j * TS:(j + 1) * TS], OP.add, OP.subtract))
            p.bar()
            if ft < 7:
                ha2 = ((ft + 1) % 2) * 4096
                dma("sync", sb3(WAh[:, ha2:ha2 + 4096], 8, 512),
                    rows3(c_w1[0], (ft + 1) * 1024, 8, 512))
                dma("sync", sb3(WAl[:, ha2:ha2 + 4096], 8, 512),
                    rows3(c_w1[1], (ft + 1) * 1024, 8, 512))
            for dc in range(NCD):
                bank = pb[dc]
                for j in range(4):
                    lh = WBh[:, j * D + dc * 128: j * D + dc * 128 + 128]
                    ll = WBl[:, j * D + dc * 128: j * D + dc * 128 + 128]
                    rh = U1a[:, j * TS:(j + 1) * TS]
                    rl = U1b[:, j * TS:(j + 1) * TS]
                    mm3(bank[:, 0:TS], lh, ll, rh, rl, j == 0, j == 3)
            p.bar()
            if ft < 7:
                dma("sync", sb3(WBh[:, 0:4096], 4, D), rows3(c_w2[0], (ft + 1) * 4 * 128, 4, D))
                dma("sync", sb3(WBl[:, 0:4096], 4, D), rows3(c_w2[1], (ft + 1) * 4 * 128, 4, D))
            for dc in range(NCD):
                bank = pb[dc]
                rap = R1[:, dc * TS:(dc + 1) * TS]
                if ft == 7:
                    p.v(lambda e, dc=dc, b=bank, rap=rap:
                        e.scalar_tensor_tensor(rap, b[:, 0:TS],
                                               sb_vecs[:, VB2b + dc:VB2b + dc + 1],
                                               rap, OP.add, OP.add))
                else:
                    p.v(lambda e, b=bank, rap=rap:
                        e.tensor_tensor(rap, rap, b[:, 0:TS], OP.add))
            p.bar()

        ln_stats(rc, None, VC_E5)
        SB = F1[:, 0:TS]
        QB2 = F1[:, TS:2 * TS]
        for c in range(NCD):
            t2 = F1[:, 2 * TS:3 * TS]
            p.v(lambda e, rcc=rc[c], t2=t2: e.tensor_tensor(t2, rcc, SB, OP.mult))
            p.v(lambda e, t2=t2: e.tensor_tensor(t2, t2, QB2, OP.subtract))
            p.v(lambda e, xcc=xc[c], t2=t2: e.tensor_tensor(xcc, xcc, t2, OP.add))
        p.bar()

        if ti == 0:
            dbg("x1_0", X0[0][:, 0:TS])
        ln_stats(xc, None, VC_E5)
        SB = F1[:, 0:TS]
        QB2 = F1[:, TS:2 * TS]
        for c in range(NCD):
            t2 = F1[:, 2 * TS:3 * TS]
            wf = F1[:, 3 * TS:4 * TS]
            p.v(lambda e, xcc=xc[c], t2=t2: e.tensor_tensor(t2, xcc, SB, OP.mult))
            p.bar()
            p.v(lambda e, t2=t2, wf=wf: e.tensor_tensor(wf, t2, QB2, OP.subtract))
            p.bar()
            p.v(lambda e, c=c, wf=wf: e.tensor_copy(P1h[:, c * TS:(c + 1) * TS], wf))
            p.bar()
            p.v(lambda e, c=c, wf=wf:
                e.tensor_tensor(P1l[:, c * TS:(c + 1) * TS], wf,
                                P1h[:, c * TS:(c + 1) * TS], OP.subtract))
            p.bar()

        for kc in range(NCD):
            rh = P1h[:, kc * TS:(kc + 1) * TS]
            rl = P1l[:, kc * TS:(kc + 1) * TS]
            mm3(pb[2][0:8, 0:TS], sb_rwh[:, kc * 8:(kc + 1) * 8],
                sb_rwl[:, kc * 8:(kc + 1) * 8], rh, rl, kc == 0, kc == NCD - 1)
        p.bar()
        p.v(lambda e: e.tensor_copy(F2[0:8, 0:512], pb[2][0:8, 0:TS]))
        p.bar()
        if ti == 0:
            dbg("rlog", F2[0:8, 0:512])
        for j in range(4):
            p.t(lambda e, j=j: e.transpose(pb[3][:, j * 8:(j + 1) * 8],
                                           F2[0:8, j * 128:(j + 1) * 128],
                                           sb_id[0:8, 0:8]))
        p.bar()
        sc_ = F2[:, 512:1024]
        p.v(lambda e: e.tensor_copy(sc_[:, 0:32], pb[3][:, 0:32]))
        p.bar()
        for g in range(4):
            p.v(lambda e, g=g: e.reduce_max(sc_[:, 32 + g:33 + g],
                                            sc_[:, g * 8:(g + 1) * 8], axis=AX))
        p.bar()
        p.vdrain()
        for g in range(4):
            p.v(lambda e, g=g: e.tensor_scalar(sc_[:, 40 + g * 8:48 + g * 8],
                                               sc_[:, g * 8:(g + 1) * 8],
                                               sc_[:, 32 + g:33 + g], None, OP.subtract))
        p.bar()
        p.a(lambda e: e.activation(sc_[:, 40:72], sc_[:, 40:72], AF.Exp))
        p.bar()
        for g in range(4):
            p.v(lambda e, g=g: e.reduce_sum(sc_[:, 72 + g:73 + g],
                                            sc_[:, 40 + g * 8:48 + g * 8], axis=AX))
        p.bar()
        p.vdrain()
        p.v(lambda e: e.reciprocal(sc_[:, 76:80], sc_[:, 72:76]))
        p.bar()
        p.vdrain()
        for g in range(4):
            p.v(lambda e, g=g: e.tensor_scalar(sc_[:, 80 + g * 8:88 + g * 8],
                                               sc_[:, 40 + g * 8:48 + g * 8],
                                               sc_[:, 76 + g:77 + g], None, OP.mult))
        p.bar()
        p.vdrain()
        for g in range(4):
            p.v(lambda e, g=g: e.reduce_max(sc_[:, 112 + g:113 + g],
                                            sc_[:, 80 + g * 8:88 + g * 8], axis=AX))
        p.bar()
        p.vdrain()
        for g in range(4):
            p.v(lambda e, g=g: e.tensor_scalar(sc_[:, 120 + g * 8:128 + g * 8],
                                               sc_[:, 80 + g * 8:88 + g * 8],
                                               sc_[:, 112 + g:113 + g], None, OP.is_ge))
        p.bar()
        p.vdrain()
        p.v(lambda e: e.tensor_tensor(sc_[:, 120:152], sc_[:, 80:112],
                                      sc_[:, 120:152], OP.mult))
        p.bar()
        p.vdrain()
        p.v(lambda e: e.tensor_tensor(sc_[:, 152:184], sc_[:, 80:112],
                                      sc_[:, 120:152], OP.subtract))
        p.bar()
        p.vdrain()
        for g in range(4):
            p.v(lambda e, g=g: e.reduce_max(sc_[:, 184 + g:185 + g],
                                            sc_[:, 152 + g * 8:160 + g * 8], axis=AX))
        p.bar()
        p.vdrain()
        for g in range(4):
            p.v(lambda e, g=g: e.tensor_scalar(sc_[:, 192 + g * 8:200 + g * 8],
                                               sc_[:, 80 + g * 8:88 + g * 8],
                                               sc_[:, 184 + g:185 + g], None, OP.is_ge))
        p.bar()
        p.vdrain()
        p.v(lambda e: e.tensor_tensor(sc_[:, 224:256], sc_[:, 80:112],
                                      sc_[:, 192:224], OP.mult))
        p.bar()
        if ti == 0:
            dbg("smsc", sc_[:, 0:256])
        for j in range(4):
            p.t(lambda e, j=j: e.transpose(pb[2][0:8, j * 128:(j + 1) * 128],
                                           sc_[:, 224 + j * 8:224 + (j + 1) * 8],
                                           sb_id[:]))
        p.bar()
        p.v(lambda e: e.tensor_copy(F2[0:8, 0:512], pb[2][0:8, 0:512]))
        p.bar()

        if ti == 0:
            dbg("wh0", P1h[:, 0:TS])
            dbg("comb0", F2[0:8, 0:512])
        def eslot(s, kc, off, width):
            c = kc * 1024 + off
            if s == 0:
                return WAh[:, c:c + width]
            if s == 1:
                return WAl[:, c:c + width]
            t = WBh if c < 4096 else WBl
            return t[:, c % 4096: c % 4096 + width]

        def eload(s, cache2d, row0):
            if s == 0:
                dma("sync", sb3(WAh[:, 0:8192], 8, D), rows3(cache2d, row0, 8, D))
            elif s == 1:
                dma("sync", sb3(WAl[:, 0:8192], 8, D), rows3(cache2d, row0, 8, D))
            else:
                dma("sync", sb3(WBh[:, 0:4096], 4, D), rows3(cache2d, row0, 4, D))
                dma("sync", sb3(WBl[:, 0:4096], 4, D), rows3(cache2d, row0 + 4 * 128, 4, D))

        eload(0, c_e1[0], 0)
        eload(1, c_e1[1], 0)
        p.bar()
        for ei in range(NE):
            g = ei * 4
            s_e1h, s_e1l, s_e2h, s_e2l = g % 3, (g + 1) % 3, (g + 2) % 3, g % 3
            p.t(lambda e, ei=ei:
                e.matmul(pb[0][:, 0:TS],
                         sb_sels[0:8, 1280 + ei * 128:1280 + (ei + 1) * 128],
                         F2[0:8, 0:512], start=True, stop=True))
            p.bar()
            p.v(lambda e: e.tensor_copy(F2[:, 512:1024], pb[0][:, 0:TS]))
            p.bar()
            for kc in range(NCD):
                eload(s_e2h, c_e2[0], ei * 8 * 128) if kc == 0 else None
                for oc in range(NCD):
                    lh = eslot(s_e1h, kc, oc * 128, 128)
                    ll = eslot(s_e1l, kc, oc * 128, 128)
                    rh = P1h[:, kc * TS:(kc + 1) * TS]
                    rl = P1l[:, kc * TS:(kc + 1) * TS]
                    mm3(pb[oc][:, 0:TS], lh, ll, rh, rl, kc == 0, kc == NCD - 1)
            p.bar()
            for half in range(2):
                for j in range(4):
                    oc = half * 4 + j
                    gf = F1[:, j * TS:(j + 1) * TS]
                    p.a(lambda e, b=pb[oc], gf=gf: e.activation(gf, b[:, 0:TS], AF.Silu))
                if half == 0:
                    eload(s_e2l, c_e2[1], ei * 8 * 128)
                p.bar()
                for j in range(4):
                    oc = half * 4 + j
                    gf = F1[:, j * TS:(j + 1) * TS]
                    p.v(lambda e, gf=gf: e.tensor_tensor(gf, gf, F2[:, 512:1024], OP.mult))
                p.bar()
                for j in range(4):
                    oc = half * 4 + j
                    gf = F1[:, j * TS:(j + 1) * TS]
                    p.v(lambda e, oc=oc, gf=gf:
                        e.tensor_copy(P2h[:, oc * TS:(oc + 1) * TS], gf))
                    p.v(lambda e, oc=oc, gf=gf:
                        e.scalar_tensor_tensor(P2l[:, oc * TS:(oc + 1) * TS], gf, 0.0,
                                               P2h[:, oc * TS:(oc + 1) * TS],
                                               OP.add, OP.subtract))
                p.bar()
            if ei < NE - 1:
                eload((g + 4) % 3, c_e1[0], (ei + 1) * 8 * 128)
            for dc in range(NCD):
                for fc in range(NCD):
                    lh = eslot(s_e2h, fc, dc * 128, 128)
                    ll = eslot(s_e2l, fc, dc * 128, 128)
                    rh = P2h[:, fc * TS:(fc + 1) * TS]
                    rl = P2l[:, fc * TS:(fc + 1) * TS]
                    mm3(pb[dc][:, 0:TS], lh, ll, rh, rl, fc == 0, fc == NCD - 1)
            p.bar()
            if ei < NE - 1:
                eload((g + 5) % 3, c_e1[1], (ei + 1) * 8 * 128)
            for dc in range(NCD):
                rap = R1[:, dc * TS:(dc + 1) * TS]
                if ei == 0:
                    p.v(lambda e, b=pb[dc], rap=rap: e.tensor_copy(rap, b[:, 0:TS]))
                else:
                    p.v(lambda e, b=pb[dc], rap=rap:
                        e.tensor_tensor(rap, rap, b[:, 0:TS], OP.add))
            p.bar()

        for kc in range(NCD):
            rh = P1h[:, kc * TS:(kc + 1) * TS]
            rl = P1l[:, kc * TS:(kc + 1) * TS]
            mm3(pb[2][0:1, 0:TS], sb_sgwh[:, kc:kc + 1], sb_sgwl[:, kc:kc + 1],
                rh, rl, kc == 0, kc == NCD - 1)
        dma("sync", sb3(WAh[:, 0:8192], 8, 1024), rows3(c_sgu[0], 0, 8, 1024, c0=0))
        dma("sync", sb3(WAl[:, 0:8192], 8, 1024), rows3(c_sgu[1], 0, 8, 1024, c0=0))
        p.bar()
        p.a(lambda e: e.activation(F2[0:1, 512:1024], pb[2][0:1, 0:TS], AF.Sigmoid))
        p.bar()
        p.t(lambda e: e.matmul(pb[3][:, 0:TS], sb_sels[:, 0:128], F2[0:16, 512:1024],
                               start=True, stop=True))
        p.bar()
        p.v(lambda e: e.tensor_copy(F2[:, 512:1024], pb[3][:, 0:TS]))
        dma("sync", sb3(WBh[:, 0:4096], 8, 512), rows3(c_sgu[0], 0, 8, 512, c0=1024))
        dma("sync", sb3(WBl[:, 0:4096], 8, 512), rows3(c_sgu[1], 0, 8, 512, c0=1024))
        p.bar()
        for half in range(2):
            if half == 1:
                dma("sync", sb3(WBh[:, 0:4096], 8, 512), rows3(c_sgu[0], 0, 8, 512, c0=1536))
                dma("sync", sb3(WBl[:, 0:4096], 8, 512), rows3(c_sgu[1], 0, 8, 512, c0=1536))
                p.bar()
            for q in range(4):
                oc = half * 4 + q
                for kc in range(NCD):
                    lh = WAh[:, kc * 1024 + oc * 128: kc * 1024 + oc * 128 + 128]
                    ll = WAl[:, kc * 1024 + oc * 128: kc * 1024 + oc * 128 + 128]
                    rh = P1h[:, kc * TS:(kc + 1) * TS]
                    rl = P1l[:, kc * TS:(kc + 1) * TS]
                    mm3(pb[0][:, 0:TS], lh, ll, rh, rl, kc == 0, kc == NCD - 1)
                for kc in range(NCD):
                    lh = WBh[:, kc * 512 + q * 128: kc * 512 + q * 128 + 128]
                    ll = WBl[:, kc * 512 + q * 128: kc * 512 + q * 128 + 128]
                    rh = P1h[:, kc * TS:(kc + 1) * TS]
                    rl = P1l[:, kc * TS:(kc + 1) * TS]
                    mm3(pb[1][:, 0:TS], lh, ll, rh, rl, kc == 0, kc == NCD - 1)
                p.bar()
                s3s = F1[:, 2 * TS:3 * TS]
                p.a(lambda e, s3s=s3s: e.activation(s3s, pb[0][:, 0:TS], AF.Silu))
                p.bar()
                p.v(lambda e, s3s=s3s: e.tensor_tensor(s3s, s3s, pb[1][:, 0:TS], OP.mult))
                p.bar()
                p.v(lambda e, s3s=s3s: e.tensor_tensor(s3s, s3s, F2[:, 512:1024], OP.mult))
                p.bar()
                p.v(lambda e, q=q, s3s=s3s:
                    e.tensor_copy(U1a[:, q * 512:(q + 1) * 512], s3s))
                p.v(lambda e, q=q, s3s=s3s:
                    e.scalar_tensor_tensor(U1b[:, q * 512:(q + 1) * 512], s3s, 0.0,
                                           U1a[:, q * 512:(q + 1) * 512], OP.add, OP.subtract))
                p.bar()
            dma("sync", sb3(P2h[:, 0:4096], 4, D), rows3(c_sdn[0], half * 4 * 128, 4, D))
            dma("sync", sb3(P2l[:, 0:4096], 4, D), rows3(c_sdn[1], half * 4 * 128, 4, D))
            p.bar()
            for dc in range(NCD):
                for q in range(4):
                    lh = P2h[:, q * D + dc * 128: q * D + dc * 128 + 128]
                    ll = P2l[:, q * D + dc * 128: q * D + dc * 128 + 128]
                    rh = U1a[:, q * 512 + 0: q * 512 + 512]
                    rl = U1b[:, q * 512 + 0: q * 512 + 512]
                    mm3(pb[dc][:, 0:TS], lh, ll, rh, rl, q == 0, q == 3)
            p.bar()
            for dc in range(NCD):
                rap = R1[:, dc * TS:(dc + 1) * TS]
                p.v(lambda e, b=pb[dc], rap=rap:
                    e.tensor_tensor(rap, rap, b[:, 0:TS], OP.add))
            p.bar()

        if ti < NT - 1:
            dma("sync", sb3(P1h[:, 0:4096], 8, TS), rows3(d_pqh, (ti + 1) * NCD * 128, 8, TS))
            dma("sync", sb3(P1l[:, 0:4096], 8, TS), rows3(d_pql, (ti + 1) * NCD * 128, 8, TS))
            dma("sync", sb3(WAh[:, 0:8192], 8, D), rows3(c_wo[0], 0, 8, D))
            dma("sync", sb3(WAl[:, 0:8192], 8, D), rows3(c_wo[1], 0, 8, D))
        ln_stats(rc, None, VC_E5)
        SB = F1[:, 0:TS]
        QB2 = F1[:, TS:2 * TS]
        for c in range(NCD):
            t2 = F1[:, 2 * TS:3 * TS]
            p.v(lambda e, rcc=rc[c], t2=t2: e.tensor_tensor(t2, rcc, SB, OP.mult))
            p.v(lambda e, t2=t2: e.tensor_tensor(t2, t2, QB2, OP.subtract))
            p.v(lambda e, xcc=xc[c], t2=t2: e.tensor_tensor(xcc, xcc, t2, OP.add))
        p.bar()

        for blk in range(NBLK):
            for c in range(NCD):
                bank = pb[c // 4]
                p.t(lambda e, c=c, b=bank,
                    xap=X0[c][:, ti * TS + blk * 128: ti * TS + blk * 128 + 128]:
                    e.transpose(b[:, (c % 4) * 128:(c % 4) * 128 + 128], xap, sb_id[:]))
            p.bar()
            p.v(lambda e: e.tensor_copy(F2[:, 0:512], pb[0][:, 0:512]))
            p.v(lambda e: e.tensor_copy(F2[:, 512:1024], pb[1][:, 0:512]))
            p.bar()
            dma("gpsimd", d_out.ap()[ti * TS + blk * 128: ti * TS + (blk + 1) * 128, :],
                F2[:, 0:1024])
            p.bar()

    with (
        nc.semaphore("s_dma_sync") as s1,
        nc.semaphore("s_dma_gpsimd") as s2,
        nc.semaphore("s_x_gpsimd") as s3,
        nc.semaphore("s_x_vector") as s4,
        nc.semaphore("s_x_scalar") as s5,
        nc.semaphore("s_x_tensor") as s6,
        nc.semaphore("s_x_vector_d") as s7,
        nc.Block() as block,
    ):
        sems = {"dma_sync": s1, "dma_gpsimd": s2, "x_gpsimd": s3,
                "x_vector": s4, "x_scalar": s5, "x_tensor": s6,
                "x_vector_d": s7}
        p.emit(block, sems)

    return nc


def host_prep(inp):
    f32 = np.float32
    x = np.ascontiguousarray(np.asarray(inp["L0_attn_inputs"], f32)).reshape(2, 8192, D)
    wq = np.ascontiguousarray(np.asarray(inp["L0_attn_Wq"], f32).reshape(D, D))
    wk = np.ascontiguousarray(np.asarray(inp["L0_attn_Wk"], f32).reshape(D, D))
    wv = np.ascontiguousarray(np.asarray(inp["L0_attn_Wv"], f32).reshape(D, D))
    wo = np.ascontiguousarray(np.asarray(inp["L0_attn_Wo"], f32).reshape(D, D))
    w1 = np.asarray(inp["L0_attn_W1"], f32)
    w1t = np.ascontiguousarray(
        w1.reshape(8, 128, 8, 512).transpose(2, 0, 1, 3).reshape(8 * 8 * 128, 512))
    w2 = np.ascontiguousarray(np.asarray(inp["L0_attn_W2"], f32))
    e1 = np.ascontiguousarray(np.asarray(inp["L0_ffn_w1"], f32).reshape(NE * D, 1024))
    e2 = np.ascontiguousarray(np.asarray(inp["L0_ffn_w2"], f32).reshape(NE * D, 1024))
    sgu = np.ascontiguousarray(np.asarray(inp["L0_ffn_shared_gate_up"], f32))
    sdn = np.ascontiguousarray(np.asarray(inp["L0_ffn_shared_down"], f32))
    rw = np.ascontiguousarray(np.asarray(inp["L0_ffn_router_w"], f32)
                              .reshape(8, 128, 8).transpose(1, 0, 2).reshape(128, 64))
    sgw = np.ascontiguousarray(np.asarray(inp["L0_ffn_shared_gate_w"], f32)
                               .reshape(8, 128).T)

    def col8(v):
        return np.ascontiguousarray(np.asarray(v, f32).reshape(8, 128).T)

    vecs = np.zeros((128, 64), f32)
    vecs[:, VG1:VG1 + 8] = col8(inp["L0_attn_ln1_scale"])
    vecs[:, VB1:VB1 + 8] = col8(inp["L0_attn_ln1_bias"])
    vecs[:, VG2:VG2 + 8] = col8(inp["L0_attn_ln2_scale"])
    vecs[:, VB2:VB2 + 8] = col8(inp["L0_attn_ln2_bias"])
    vecs[:, VBQ:VBQ + 8] = col8(np.asarray(inp["L0_attn_bq"], f32).reshape(D))
    vecs[:, VBO:VBO + 8] = col8(inp["L0_attn_bo"])
    vecs[:, VB2b:VB2b + 8] = col8(inp["L0_attn_b2"])
    vecs[:, VC_LN1B] = 1e-11
    vecs[:, VC_LN1S] = 1.0 + 1e-6
    vecs[:, VC_E5] = 1e-5
    vecs[:, VC_E6] = 1e-6
    b1c = np.ascontiguousarray(np.asarray(inp["L0_attn_b1"], f32).reshape(32, 128).T)
    rowpack = np.zeros((1, 2176), f32)
    rowpack[0, 0:D] = np.asarray(inp["L0_attn_bk"], f32).reshape(D)
    rowpack[0, D:2 * D] = np.asarray(inp["L0_attn_bv"], f32).reshape(D)
    rowpack[0, 2048:2176] = 1.0

    ident = np.eye(128, dtype=f32)
    sels = np.zeros((16, 2304), f32)
    for j in range(2):
        sels[j, j * 128:(j + 1) * 128] = 1.0
    for c in range(8):
        sels[2 * c, 256 + c * 128:256 + c * 128 + 64] = 1.0
        sels[2 * c + 1, 256 + c * 128 + 64:256 + (c + 1) * 128] = 1.0
    for e_ in range(8):
        sels[e_, 1280 + e_ * 128:1280 + (e_ + 1) * 128] = 1.0
    onesel = np.zeros((128, 32), f32)
    onesel[:, 0] = 1.0
    onesel[:, 17] = 1.0

    shared = dict(wq=wq, wk=wk, wv=wv, wo=wo, w1t=w1t, w2=w2, e1=e1, e2=e2,
                  sgu=sgu, sdn=sdn, rw=rw, sgw=sgw, vecs=vecs, b1c=b1c,
                  rowpack=rowpack, ident=ident, sels=sels, onesel=onesel)
    in_maps = []
    for core in range(8):
        b = core // 4
        s0 = (core % 4) * TPC
        m = dict(shared)
        m["x"] = np.ascontiguousarray(x[b, s0:s0 + TPC])
        in_maps.append(m)
    return in_maps


def kernel(**inp):
    in_maps = host_prep(inp)
    nc = build_nc()
    res = run_bass_kernel_spmd(nc, in_maps, core_ids=list(range(8)))
    out = np.empty((2, 8192, D), np.float32)
    for core in range(8):
        b = core // 4
        s0 = (core % 4) * TPC
        out[b, s0:s0 + TPC] = np.asarray(res.results[core]["out"], np.float32)
    return out


# version 06
# speedup vs baseline: 1.0866x; 1.0329x over previous
import numpy as np
import concourse.bass as bass
import concourse.mybir as mybir
from concourse.bass_utils import run_bass_kernel_spmd
from concourse.alu_op_type import AluOpType as OP

F32 = mybir.dt.float32
BF16 = mybir.dt.bfloat16
AF = mybir.ActivationFunctionType
AX = mybir.AxisListType.X

TPC = 2048
NT = 4
TS = 512
NBLK = 4
D = 1024
NCD = 8
MLP = 4096
NE = 8

VG1, VB1, VG2, VB2, VBQ, VBO, VB2b = 0, 8, 16, 24, 32, 40, 48
VC_LN1B, VC_LN1S, VC_E5, VC_E6 = 56, 57, 58, 59


class Prog:

    def __init__(self):
        self.segs = [dict()]

    def _add(self, eng, fn, dma):
        self.segs[-1].setdefault(eng, []).append((fn, dma))

    def t(self, fn):
        self._add("tensor", fn, False)

    def v(self, fn):
        self._add("vector", fn, False)

    def a(self, fn):
        self._add("scalar", fn, False)

    def g(self, fn):
        self._add("gpsimd", fn, False)

    def dS(self, fn):
        self._add("sync", fn, True)

    def dG(self, fn):
        self._add("gpsimd", fn, True)

    def vdrain(self):
        self._add("vector", lambda e: e.drain(), False)

    def bar(self):
        if self.segs[-1]:
            self.segs.append(dict())

    def emit(self, block, sems):
        segs = [s for s in self.segs if s]
        n = len(segs)
        cum = {k: [0] * (n + 1) for k in sems if k != "x_vector_d"}
        for i, seg in enumerate(segs):
            for k in cum:
                cum[k][i + 1] = cum[k][i]
            for eng, ops in seg.items():
                ndma = sum(1 for _, d in ops if d)
                nex = sum(1 for _, d in ops if not d)
                if eng in ("sync", "gpsimd") and ndma:
                    cum["dma_" + eng][i + 1] += 16 * ndma
                if nex and eng != "sync":
                    cum["x_" + eng][i + 1] += 1

        def emit_engine(eng_name, eng):
            last_wait = {k: 0 for k in cum}
            for i, seg in enumerate(segs):
                ops = seg.get(eng_name)
                if not ops:
                    continue
                for k in cum:
                    if k == "x_" + eng_name and eng_name != "gpsimd":
                        continue
                    tgt = cum[k][i]
                    if tgt > last_wait[k]:
                        eng.wait_ge(sems[k], tgt)
                        last_wait[k] = tgt
                last_exec = None
                for fn, dma_ in ops:
                    instr = fn(eng)
                    if dma_:
                        instr.then_inc(sems["dma_" + eng_name], 16)
                    else:
                        last_exec = instr
                if last_exec is not None and eng_name != "sync":
                    last_exec.then_inc(sems["x_" + eng_name], 1)
            if eng_name == "sync":
                for k in cum:
                    if cum[k][n] > last_wait[k]:
                        eng.wait_ge(sems[k], cum[k][n])

        @block.sync
        def _(sync):
            emit_engine("sync", sync)

        @block.gpsimd
        def _(gpsimd):
            emit_engine("gpsimd", gpsimd)

        @block.vector
        def _(vector):
            emit_engine("vector", vector)

        @block.scalar
        def _(scalar):
            emit_engine("scalar", scalar)

        @block.tensor
        def _(tensor):
            emit_engine("tensor", tensor)


def rows3(t2d, r0, nblk, cols, c0=0):
    ap = t2d.ap()[r0:r0 + nblk * 128, :].rearrange("(a p) c -> p a c", p=128)
    if c0 != 0 or cols != ap.shape[2]:
        ap = ap[:, :, c0:c0 + cols]
    return ap


def sb3(sb_ap, nblk, cols):
    return sb_ap.rearrange("p (a c) -> p a c", c=cols)


def build_nc(debug=False):
    nc = bass.Bass("TRN2", target_bir_lowering=False)

    d_x = nc.dram_tensor("x", [TPC, D], F32, kind="ExternalInput")
    d_wq = nc.dram_tensor("wq", [NCD * 128, D], F32, kind="ExternalInput")
    d_wk = nc.dram_tensor("wk", [NCD * 128, D], F32, kind="ExternalInput")
    d_wv = nc.dram_tensor("wv", [NCD * 128, D], F32, kind="ExternalInput")
    d_wo = nc.dram_tensor("wo", [NCD * 128, D], F32, kind="ExternalInput")
    d_w1 = nc.dram_tensor("w1t", [8 * 8 * 128, 512], F32, kind="ExternalInput")
    d_w2 = nc.dram_tensor("w2", [32 * 128, D], F32, kind="ExternalInput")
    d_e1 = nc.dram_tensor("e1", [NE * 8 * 128, 1024], F32, kind="ExternalInput")
    d_e2 = nc.dram_tensor("e2", [NE * 8 * 128, 1024], F32, kind="ExternalInput")
    d_sgu = nc.dram_tensor("sgu", [8 * 128, 2048], F32, kind="ExternalInput")
    d_sdn = nc.dram_tensor("sdn", [8 * 128, 1024], F32, kind="ExternalInput")
    d_rw = nc.dram_tensor("rw", [128, 64], F32, kind="ExternalInput")
    d_sgw = nc.dram_tensor("sgw", [128, 8], F32, kind="ExternalInput")
    d_vecs = nc.dram_tensor("vecs", [128, 64], F32, kind="ExternalInput")
    d_b1c = nc.dram_tensor("b1c", [128, 32], F32, kind="ExternalInput")
    d_rowpack = nc.dram_tensor("rowpack", [1, 2176], F32, kind="ExternalInput")
    d_id = nc.dram_tensor("ident", [128, 128], F32, kind="ExternalInput")
    d_sels = nc.dram_tensor("sels", [16, 2304], F32, kind="ExternalInput")
    d_onesel = nc.dram_tensor("onesel", [128, 32], F32, kind="ExternalInput")

    d_out = nc.dram_tensor("out", [TPC, D], F32, kind="ExternalOutput")

    d_ccin = nc.dram_tensor("ccin", [128, 520], F32)
    d_ccout = nc.dram_tensor("ccout", [128, 520], F32)

    def cache(name, rows, cols):
        return (nc.dram_tensor(name + "_h", [rows, cols], BF16),
                nc.dram_tensor(name + "_l", [rows, cols], BF16))

    c_wq = cache("c_wq", NCD * 128, D)
    c_wk = cache("c_wk", NCD * 128, D)
    c_wv = cache("c_wv", NCD * 128, D)
    c_wo = cache("c_wo", NCD * 128, D)
    c_w1 = cache("c_w1", 8 * 8 * 128, 512)
    c_w2 = cache("c_w2", 32 * 128, D)
    c_e1 = cache("c_e1", NE * 8 * 128, 1024)
    c_e2 = cache("c_e2", NE * 8 * 128, 1024)
    c_sgu = cache("c_sgu", 8 * 128, 2048)
    c_sdn = cache("c_sdn", 8 * 128, 1024)

    d_pqh = nc.dram_tensor("pqh", [NT * NCD * 128, TS], BF16)
    d_pql = nc.dram_tensor("pql", [NT * NCD * 128, TS], BF16)
    d_rows = nc.dram_tensor("rows", [2, TPC], F32)
    d_cb = nc.dram_tensor("cbt", [NE * 128, TS], F32)

    X0 = [nc.alloc_sbuf_tensor(f"x0_{c}", [128, TPC], F32) for c in range(NCD)]
    kvacc = nc.alloc_sbuf_tensor("kvacc", [128, 520], F32)
    kvh = nc.alloc_sbuf_tensor("kvh", [128, 520], BF16)
    kvl = nc.alloc_sbuf_tensor("kvl", [128, 520], BF16)
    kselh = nc.alloc_sbuf_tensor("kselh", [128, 128], BF16)
    ksell = nc.alloc_sbuf_tensor("ksell", [128, 128], BF16)
    sb_id = nc.alloc_sbuf_tensor("sb_id", [128, 128], F32)
    sb_sels = nc.alloc_sbuf_tensor("sb_sels", [16, 2304], F32)
    sb_onesel = nc.alloc_sbuf_tensor("sb_onesel", [128, 32], F32)
    sb_rowpack = nc.alloc_sbuf_tensor("sb_rowpack", [1, 2176], F32)
    sb_vecs = nc.alloc_sbuf_tensor("sb_vecs", [128, 64], F32)
    sb_b1c = nc.alloc_sbuf_tensor("sb_b1c", [128, 32], F32)
    sb_rwh = nc.alloc_sbuf_tensor("sb_rwh", [128, 64], BF16)
    sb_rwl = nc.alloc_sbuf_tensor("sb_rwl", [128, 64], BF16)
    sb_sgwh = nc.alloc_sbuf_tensor("sb_sgwh", [128, 8], BF16)
    sb_sgwl = nc.alloc_sbuf_tensor("sb_sgwl", [128, 8], BF16)
    P1h = nc.alloc_sbuf_tensor("P1h", [128, 4096], BF16)
    P1l = nc.alloc_sbuf_tensor("P1l", [128, 4096], BF16)
    P2h = nc.alloc_sbuf_tensor("P2h", [128, 4096], BF16)
    P2l = nc.alloc_sbuf_tensor("P2l", [128, 4096], BF16)
    R1 = nc.alloc_sbuf_tensor("R1", [128, 4096], F32)
    F1 = nc.alloc_sbuf_tensor("F1", [128, 2048], F32)
    F2 = nc.alloc_sbuf_tensor("F2", [128, 1024], F32)
    WAh = nc.alloc_sbuf_tensor("WAh", [128, 8192], BF16)
    WAl = nc.alloc_sbuf_tensor("WAl", [128, 8192], BF16)
    WBh = nc.alloc_sbuf_tensor("WBh", [128, 4096], BF16)
    WBl = nc.alloc_sbuf_tensor("WBl", [128, 4096], BF16)
    U1a = nc.alloc_sbuf_tensor("U1a", [128, 2112], BF16)
    U1b = nc.alloc_sbuf_tensor("U1b", [128, 2112], BF16)

    pb = [nc.alloc_psum_tensor(f"pb{i}", [128, 512], F32) for i in range(8)]

    p = Prog()

    def mm3(out_ap, lh, ll, rh, rl, first, last):
        p.t(lambda e, o=out_ap, a=lh, b=rh, f=first: e.matmul(o, a, b, start=f, stop=False))
        p.t(lambda e, o=out_ap, a=lh, b=rl: e.matmul(o, a, b, start=False, stop=False))
        p.t(lambda e, o=out_ap, a=ll, b=rh, s=last: e.matmul(o, a, b, start=False, stop=s))

    def dma(q, out_ap, in_ap):
        fn = lambda e, o=out_ap, i=in_ap: e.dma_start(out=o, in_=i)
        (p.dS if q == "sync" else p.dG)(fn)

    def dbg(name, ap):
        if not debug:
            return
        shape = list(ap.shape)
        t = nc.dram_tensor("dbg_" + name, shape, ap.dtype, kind="ExternalOutput")
        p.bar()
        dma("gpsimd", t.ap(), ap)
        p.bar()

    def ln_stats(src_aps, sqrt_scale_col, sqrt_bias_col, keep_ti=None):
        for half in (0, 1):
            for j in range(4):
                c = half * 4 + j
                p.a(lambda e, s=src_aps[c], d=F1[:, j * TS:(j + 1) * TS]:
                    e.activation(d, s, AF.Square))
            p.bar()
            for j in range(4):
                c = half * 4 + j
                p.t(lambda e, r=src_aps[c], f=(c == 0):
                    e.matmul(pb[6][0:16, 0:TS], sb_onesel[:, 0:16], r, start=f, stop=False))
                p.t(lambda e, r=F1[:, j * TS:(j + 1) * TS], s=(c == 7):
                    e.matmul(pb[6][0:16, 0:TS], sb_onesel[:, 16:32], r, start=False, stop=s))
            p.bar()
        p.v(lambda e: e.tensor_copy(F2[0:2, 0:TS], pb[6][0:2, 0:TS]))
        p.bar()
        if keep_ti is not None:
            dma("gpsimd", d_rows.ap()[0:2, keep_ti * TS:(keep_ti + 1) * TS],
                F2[0:2, 0:TS])
        p.t(lambda e: e.matmul(pb[6][:, 0:TS], sb_sels[:, 0:128], F2[0:16, 0:TS],
                               start=True, stop=True))
        p.t(lambda e: e.matmul(pb[7][:, 0:TS], sb_sels[:, 128:256], F2[0:16, 0:TS],
                               start=True, stop=True))
        p.bar()
        sm = F1[:, 0:TS]
        sqm = F1[:, TS:2 * TS]
        p.v(lambda e: e.tensor_scalar(sqm, pb[6][:, 0:TS], 1.0 / D, None, OP.mult))
        p.bar()
        p.v(lambda e: e.tensor_tensor(sm, sqm, sqm, OP.mult))
        p.bar()
        p.v(lambda e: e.scalar_tensor_tensor(sm, pb[7][:, 0:TS], 1.0 / D, sm,
                                             OP.mult, OP.subtract))
        p.bar()
        if sqrt_scale_col is None:
            p.a(lambda e, bc=sqrt_bias_col:
                e.activation(sm, sm, AF.Sqrt, bias=sb_vecs[:, bc:bc + 1]))
        else:
            p.a(lambda e, bc=sqrt_bias_col, sc=sqrt_scale_col:
                e.activation(sm, sm, AF.Sqrt, bias=sb_vecs[:, bc:bc + 1],
                             scale=sb_vecs[:, sc:sc + 1]))
        p.bar()
        p.v(lambda e: e.reciprocal(sm, sm))
        p.bar()
        p.v(lambda e: e.tensor_tensor(sqm, sqm, sm, OP.mult))
        p.bar()

    dma("sync", sb_id[:], d_id.ap())
    dma("sync", sb_sels[:], d_sels.ap())
    dma("sync", sb_onesel[:], d_onesel.ap())
    dma("sync", sb_vecs[:], d_vecs.ap())
    dma("sync", sb_b1c[:], d_b1c.ap())
    dma("sync", sb_rowpack[:], d_rowpack.ap())
    p.v(lambda e: e.memset(kvacc[:], 0.0))
    p.v(lambda e: e.memset(F2[:], 0.0))
    p.bar()

    for blk in range(TPC // 128):
        dma("sync", F2[:, 0:1024], d_x.ap()[blk * 128:(blk + 1) * 128, :])
        p.bar()
        for c in range(NCD):
            bank = pb[c // 4]
            p.t(lambda e, c=c, b=bank:
                e.transpose(b[:, (c % 4) * 128:(c % 4) * 128 + 128],
                            F2[:, c * 128:(c + 1) * 128], sb_id[:]))
        p.bar()
        for c in range(NCD):
            bank = pb[c // 4]
            p.v(lambda e, c=c, b=bank, blk=blk:
                e.tensor_copy(X0[c][:, blk * 128:(blk + 1) * 128],
                              b[:, (c % 4) * 128:(c % 4) * 128 + 128]))
        p.bar()

    dbg("x0c0", X0[0][:, 0:512])

    slabs = []

    def add_slabs(src, ch, cl, nrb, cols):
        g = max(1, 2048 // cols)
        r = 0
        while r < nrb:
            gg = min(g, nrb - r)
            slabs.append((src, ch, cl, r * 128, gg, cols))
            r += gg

    add_slabs(d_wq, *c_wq, NCD, D)
    add_slabs(d_wk, *c_wk, NCD, D)
    add_slabs(d_wv, *c_wv, NCD, D)
    add_slabs(d_wo, *c_wo, NCD, D)
    add_slabs(d_w1, *c_w1, 64, 512)
    add_slabs(d_w2, *c_w2, 32, D)
    add_slabs(d_e1, *c_e1, 64, 1024)
    add_slabs(d_e2, *c_e2, 64, 1024)
    add_slabs(d_sgu, *c_sgu, 8, 2048)
    add_slabs(d_sdn, *c_sdn, 8, 1024)

    nsl = len(slabs)

    def p0in(k):
        s = k % 3
        return (R1[:, 0:2048] if s == 0 else
                R1[:, 2048:4096] if s == 1 else F1[:, 0:2048])

    for k in range(nsl + 3):
        if k < nsl:
            src_, ch, cl, r0, nb_, cols = slabs[k]
            w = nb_ * cols
            dma("sync", sb3(p0in(k)[:, 0:w], nb_, cols), rows3(src_, r0, nb_, cols))
        if 1 <= k <= nsl:
            src_, ch, cl, r0, nb_, cols = slabs[k - 1]
            w = nb_ * cols
            half = ((k - 1) % 2) * 2048
            p.a(lambda e, w=w, h=half, kk=k - 1:
                e.copy(P1h[:, h:h + w], p0in(kk)[:, 0:w]))
        if 2 <= k <= nsl + 1:
            src_, ch, cl, r0, nb_, cols = slabs[k - 2]
            w = nb_ * cols
            half = ((k - 2) % 2) * 2048
            p.v(lambda e, w=w, h=half, kk=k - 2:
                e.tensor_tensor(P1l[:, h:h + w], p0in(kk)[:, 0:w],
                                P1h[:, h:h + w], OP.subtract))
            dma("gpsimd", rows3(ch, r0, nb_, cols), sb3(P1h[:, half:half + w], nb_, cols))
        if 3 <= k:
            src_, ch, cl, r0, nb_, cols = slabs[k - 3]
            w = nb_ * cols
            half = ((k - 3) % 2) * 2048
            dma("gpsimd", rows3(cl, r0, nb_, cols), sb3(P1l[:, half:half + w], nb_, cols))
        p.bar()

    dma("sync", F1[:, 0:64], d_rw.ap())
    dma("sync", F1[:, 64:72], d_sgw.ap())
    p.bar()
    p.v(lambda e: e.tensor_copy(sb_rwh[:], F1[:, 0:64]))
    p.v(lambda e: e.tensor_copy(sb_sgwh[:], F1[:, 64:72]))
    p.bar()
    p.v(lambda e: e.tensor_tensor(sb_rwl[:], F1[:, 0:64], sb_rwh[:], OP.subtract))
    p.v(lambda e: e.tensor_tensor(sb_sgwl[:], F1[:, 64:72], sb_sgwh[:], OP.subtract))
    p.bar()

    for ti in range(NT):
        tcol = slice(ti * TS, (ti + 1) * TS)
        xc = [X0[c][:, tcol] for c in range(NCD)]

        ln_stats(xc, VC_LN1S, VC_LN1B, keep_ti=ti)
        if ti == 0:
            dbg("ln1U", F1[:, 0:TS])
            dbg("ln1Q", F1[:, TS:2 * TS])
        UB = F1[:, 0:TS]
        QB = F1[:, TS:2 * TS]
        for c in range(NCD):
            t2 = F1[:, 2 * TS:3 * TS]
            af = F1[:, 3 * TS:4 * TS]
            p.v(lambda e, xcc=xc[c], t2=t2: e.tensor_tensor(t2, xcc, UB, OP.mult))
            p.bar()
            if ti == 0 and c == 0:
                dbg("t2a", t2)
            p.v(lambda e, t2=t2: e.tensor_tensor(t2, t2, QB, OP.subtract))
            p.bar()
            if ti == 0 and c == 0:
                dbg("t2b", t2)
            p.a(lambda e, c=c, t2=t2, af=af:
                e.activation(af, t2, AF.Identity,
                             bias=sb_vecs[:, VB1 + c:VB1 + c + 1],
                             scale=sb_vecs[:, VG1 + c:VG1 + c + 1]))
            p.bar()
            if ti == 0 and c == 0:
                dbg("af0", af)
            p.v(lambda e, c=c, af=af: e.tensor_copy(P1h[:, c * TS:(c + 1) * TS], af))
            p.bar()
            p.v(lambda e, c=c, af=af:
                e.tensor_tensor(P1l[:, c * TS:(c + 1) * TS], af,
                                P1h[:, c * TS:(c + 1) * TS], OP.subtract))
            p.bar()

        if ti == 0:
            dbg("ah0", P1h[:, 0:TS])
            dbg("al0", P1l[:, 0:TS])
        dma("sync", sb3(WAh[:, 0:8192], 8, D), rows3(c_wq[0], 0, 8, D))
        dma("sync", sb3(WAl[:, 0:8192], 8, D), rows3(c_wq[1], 0, 8, D))
        p.bar()
        def q_relu(oc):
            fq = F1[:, (2 + oc % 2) * TS:(3 + oc % 2) * TS]
            p.a(lambda e, b=pb[oc % 2], fq=fq, oc=oc:
                e.activation(fq, b[:, 0:TS], AF.Relu,
                             bias=sb_vecs[:, VBQ + oc:VBQ + oc + 1]))

        def q_split(oc):
            fq = F1[:, (2 + oc % 2) * TS:(3 + oc % 2) * TS]
            p.v(lambda e, fq=fq, oc=oc:
                e.tensor_scalar(P2h[:, oc * TS:(oc + 1) * TS], fq, 0.001, None, OP.add))
            p.v(lambda e, fq=fq, oc=oc:
                e.scalar_tensor_tensor(P2l[:, oc * TS:(oc + 1) * TS], fq, 0.001,
                                       P2h[:, oc * TS:(oc + 1) * TS], OP.add, OP.subtract))

        for oc in range(NCD + 2):
            if oc < NCD:
                bank = pb[oc % 2]
                for kc in range(NCD):
                    lh = WAh[:, kc * D + oc * 128: kc * D + oc * 128 + 128]
                    ll = WAl[:, kc * D + oc * 128: kc * D + oc * 128 + 128]
                    rh = P1h[:, kc * TS:(kc + 1) * TS]
                    rl = P1l[:, kc * TS:(kc + 1) * TS]
                    mm3(bank[:, 0:TS], lh, ll, rh, rl, kc == 0, kc == NCD - 1)
            if oc >= 1 and oc - 1 < NCD:
                q_relu(oc - 1)
            if oc >= 2:
                q_split(oc - 2)
            p.bar()
        if ti == 0:
            dbg("pqh0", P2h[:, 0:TS])
        dma("gpsimd", rows3(d_pqh, ti * NCD * 128, 8, TS), sb3(P2h[:, 0:4096], 8, TS))
        dma("gpsimd", rows3(d_pql, ti * NCD * 128, 8, TS), sb3(P2l[:, 0:4096], 8, TS))
        p.bar()

        dma("sync", sb3(WAh[:, 0:8192], 8, D), rows3(c_wk[0], 0, 8, D))
        dma("sync", sb3(WAl[:, 0:8192], 8, D), rows3(c_wk[1], 0, 8, D))
        dma("sync", sb3(WBh[:, 0:4096], 4, D), rows3(c_wv[0], 0, 4, D))
        dma("sync", sb3(WBl[:, 0:4096], 4, D), rows3(c_wv[0], 4 * 128, 4, D))
        dma("sync", sb3(P2h[:, 0:4096], 4, D), rows3(c_wv[1], 0, 4, D))
        dma("sync", sb3(P2l[:, 0:4096], 4, D), rows3(c_wv[1], 4 * 128, 4, D))
        p.bar()

        def wvh(kc, hf):
            idx = kc * D + hf * 512
            t = WBh if idx < 4096 else WBl
            return t[:, idx % 4096: idx % 4096 + 512]

        def wvl(kc, hf):
            idx = kc * D + hf * 512
            t = P2h if idx < 4096 else P2l
            return t[:, idx % 4096: idx % 4096 + 512]

        for blk in range(NBLK):
            for hf in range(2):
                bank = pb[2 + hf]
                for kc in range(NCD):
                    ah = P1h[:, kc * TS + blk * 128: kc * TS + blk * 128 + 128]
                    al = P1l[:, kc * TS + blk * 128: kc * TS + blk * 128 + 128]
                    rh = WAh[:, kc * D + hf * 512: kc * D + hf * 512 + 512]
                    rl = WAl[:, kc * D + hf * 512: kc * D + hf * 512 + 512]
                    mm3(bank[:, 0:512], ah, al, rh, rl, kc == 0, False)
                p.t(lambda e, b=bank, hf=hf:
                    e.matmul(b[:, 0:512], sb_rowpack[0:1, 2048:2176],
                             sb_rowpack[0:1, hf * 512:hf * 512 + 512],
                             start=False, stop=True))
            p.bar()
            for hf in range(2):
                bank = pb[2 + hf]
                fk = F1[:, (2 + hf) * TS:(3 + hf) * TS]
                p.a(lambda e, b=bank, fk=fk: e.activation(fk, b[:, 0:512], AF.Relu))
            p.bar()
            for hf in range(2):
                fk = F1[:, (2 + hf) * TS:(3 + hf) * TS]
                p.v(lambda e, fk=fk, hf=hf:
                    e.tensor_scalar(U1a[:, hf * 512:hf * 512 + 512], fk, 0.001, None, OP.add))
                p.v(lambda e, fk=fk, hf=hf:
                    e.scalar_tensor_tensor(U1b[:, hf * 512:hf * 512 + 512], fk, 0.001,
                                           U1a[:, hf * 512:hf * 512 + 512], OP.add, OP.subtract))
            p.bar()
            for hf in range(2):
                bank = pb[2 + hf]
                for kc in range(NCD):
                    ah = P1h[:, kc * TS + blk * 128: kc * TS + blk * 128 + 128]
                    al = P1l[:, kc * TS + blk * 128: kc * TS + blk * 128 + 128]
                    mm3(bank[:, 0:512], ah, al, wvh(kc, hf), wvl(kc, hf), kc == 0, False)
                p.t(lambda e, b=bank, hf=hf:
                    e.matmul(b[:, 0:512], sb_rowpack[0:1, 2048:2176],
                             sb_rowpack[0:1, 1024 + hf * 512:1024 + hf * 512 + 512],
                             start=False, stop=True))
            p.bar()
            for hf in range(2):
                bank = pb[2 + hf]
                src = bass.AP(bank, 0, [[512, 128], [64, 8], [1, 64]])
                dsth = bass.AP(U1a, 1024 + hf * 8 * 65, [[2112, 128], [65, 8], [1, 64]])
                dstl = bass.AP(U1b, 1024 + hf * 8 * 65, [[2112, 128], [65, 8], [1, 64]])
                p.v(lambda e, s=src, d=dsth: e.tensor_copy(d, s))
                p.v(lambda e, s=src, dh=dsth, dl=dstl:
                    e.scalar_tensor_tensor(dl, s, 0.0, dh, OP.add, OP.subtract))
            onesh = bass.AP(U1a, 1024 + 64, [[2112, 128], [65, 16], [1, 1]])
            onesl = bass.AP(U1b, 1024 + 64, [[2112, 128], [65, 16], [1, 1]])
            p.v(lambda e, o=onesh: e.memset(o, 1.0))
            p.v(lambda e, o=onesl: e.memset(o, 0.0))
            p.bar()
            if ti == 0 and blk == 0:
                dbg("u1a0", U1a[:, 0:2112])
                dbg("u1b0", U1b[:, 0:2112])
            for rnd in range(2):
                for j in range(4):
                    pair = rnd * 4 + j
                    bank = pb[4 + j]
                    for sub in range(2):
                        h_ = pair * 2 + sub
                        lh = U1a[:, h_ * 64:(h_ + 1) * 64]
                        ll = U1b[:, h_ * 64:(h_ + 1) * 64]
                        rh = U1a[:, 1024 + h_ * 65:1024 + (h_ + 1) * 65]
                        rl = U1b[:, 1024 + h_ * 65:1024 + (h_ + 1) * 65]
                        mm3(bank[sub * 64:(sub + 1) * 64, 0:65], lh, ll, rh, rl, True, True)
                p.bar()
                if debug and ti == 0 and blk == 0 and rnd == 0:
                    p.v(lambda e: e.tensor_copy(F1[:, 0:65], pb[4][:, 0:65]))
                    dbg("kvps0", F1[:, 0:65])
                for j in range(4):
                    pair = rnd * 4 + j
                    bank = pb[4 + j]
                    p.v(lambda e, b=bank, pr=pair:
                        e.tensor_tensor(kvacc[:, pr * 65:(pr + 1) * 65],
                                        kvacc[:, pr * 65:(pr + 1) * 65],
                                        b[:, 0:65], OP.add))
                p.bar()
                if debug and ti == 0 and blk == 0 and rnd == 0:
                    dbg("kvacc0", kvacc[:])

    dbg("kvpart", kvacc[:])
    dma("gpsimd", d_ccin.ap(), kvacc[:])
    p.bar()
    dbg("ccin", d_ccin.ap())
    p.g(lambda e: e.collective_compute(
        "AllReduce", mybir.AluOpType.add,
        replica_groups=[[0, 1, 2, 3], [4, 5, 6, 7]],
        ins=[d_ccin.ap()], outs=[d_ccout.ap()]))
    p.bar()
    dbg("ccout", d_ccout.ap())
    dma("gpsimd", kvacc[:], d_ccout.ap())
    p.bar()
    dbg("kvback", kvacc[:])
    p.v(lambda e: e.tensor_copy(kvh[:], kvacc[:]))
    p.bar()
    p.v(lambda e: e.tensor_tensor(kvl[:], kvacc[:], kvh[:], OP.subtract))
    p.v(lambda e: e.memset(kselh[:], 0.0))
    p.v(lambda e: e.memset(ksell[:], 0.0))
    p.bar()
    dbg("kvfull", kvacc[:])
    p.vdrain()
    for c in range(NCD):
        p.v(lambda e, c=c: e.tensor_copy(kselh[0:64, c * 16 + 2 * c:c * 16 + 2 * c + 1],
                                         kvh[0:64, c * 65 + 64:c * 65 + 65]))
        p.v(lambda e, c=c: e.tensor_copy(kselh[64:128, c * 16 + 2 * c + 1:c * 16 + 2 * c + 2],
                                         kvh[64:128, c * 65 + 64:c * 65 + 65]))
        p.v(lambda e, c=c: e.tensor_copy(ksell[0:64, c * 16 + 2 * c:c * 16 + 2 * c + 1],
                                         kvl[0:64, c * 65 + 64:c * 65 + 65]))
        p.v(lambda e, c=c: e.tensor_copy(ksell[64:128, c * 16 + 2 * c + 1:c * 16 + 2 * c + 2],
                                         kvl[64:128, c * 65 + 64:c * 65 + 65]))
    p.vdrain()
    p.bar()

    for ti in range(NT):
        tcol = slice(ti * TS, (ti + 1) * TS)
        xc = [X0[c][:, tcol] for c in range(NCD)]
        rc = [R1[:, c * TS:(c + 1) * TS] for c in range(NCD)]

        if ti == 0:
            dma("sync", sb3(P1h[:, 0:4096], 8, TS), rows3(d_pqh, 0, 8, TS))
            dma("sync", sb3(P1l[:, 0:4096], 8, TS), rows3(d_pql, 0, 8, TS))
            dma("sync", sb3(WAh[:, 0:8192], 8, D), rows3(c_wo[0], 0, 8, D))
            dma("sync", sb3(WAl[:, 0:8192], 8, D), rows3(c_wo[1], 0, 8, D))
            p.bar()

        for kc in range(NCD):
            rh = P1h[:, kc * TS:(kc + 1) * TS]
            rl = P1l[:, kc * TS:(kc + 1) * TS]
            mm3(pb[2][0:16, 0:TS], kselh[:, kc * 16:(kc + 1) * 16],
                ksell[:, kc * 16:(kc + 1) * 16], rh, rl, kc == 0, kc == NCD - 1)
        p.bar()
        p.v(lambda e: e.reciprocal(F2[0:16, 0:512], pb[2][0:16, 0:TS]))
        p.bar()
        if ti == 0:
            dbg("recip0", F2[0:16, 0:512])

        def attn_evac(c):
            nb_ = pb[c % 2]
            bb_ = pb[3 + c % 2]
            rb = F2[:, 512:1024]
            fa = F1[:, (2 + c % 2) * TS:(3 + c % 2) * TS]
            p.v(lambda e, b=bb_, rb=rb: e.tensor_copy(rb, b[:, 0:TS]))
            p.v(lambda e, b=nb_, rb=rb, fa=fa: e.tensor_tensor(fa, b[:, 0:TS], rb, OP.mult))
            p.v(lambda e, c=c, fa=fa: e.tensor_copy(P2h[:, c * TS:(c + 1) * TS], fa))
            p.v(lambda e, c=c, fa=fa:
                e.tensor_tensor(P2l[:, c * TS:(c + 1) * TS], fa,
                                P2h[:, c * TS:(c + 1) * TS], OP.subtract))

        for c in range(NCD + 1):
            if c < NCD:
                nb_ = pb[c % 2]
                bb_ = pb[3 + c % 2]
                for sub in range(2):
                    lh = kvh[sub * 64:(sub + 1) * 64, c * 65:c * 65 + 64]
                    ll = kvl[sub * 64:(sub + 1) * 64, c * 65:c * 65 + 64]
                    rh = P1h[sub * 64:(sub + 1) * 64, c * TS:(c + 1) * TS]
                    rl = P1l[sub * 64:(sub + 1) * 64, c * TS:(c + 1) * TS]
                    mm3(nb_[sub * 64:(sub + 1) * 64, 0:TS], lh, ll, rh, rl, True, True)
                p.t(lambda e, c=c, b=bb_:
                    e.matmul(b[:, 0:TS], sb_sels[:, 256 + c * 128:256 + (c + 1) * 128],
                             F2[0:16, 0:512], start=True, stop=True))
            if c >= 1:
                attn_evac(c - 1)
            p.bar()

        dma("sync", F2[0:2, 0:512], d_rows.ap()[0:2, tcol])
        p.bar()
        p.t(lambda e: e.matmul(pb[6][:, 0:TS], sb_sels[:, 0:128],
                               F2[0:16, 0:512], start=True, stop=True))
        p.t(lambda e: e.matmul(pb[7][:, 0:TS], sb_sels[:, 128:256],
                               F2[0:16, 0:512], start=True, stop=True))
        p.bar()
        S1B = F1[:, 0:TS]
        MS1B = F1[:, TS:2 * TS]
        p.v(lambda e: e.tensor_scalar(MS1B, pb[6][:, 0:TS], 1.0 / D, None, OP.mult))
        p.bar()
        p.v(lambda e: e.tensor_tensor(S1B, MS1B, MS1B, OP.mult))
        p.bar()
        p.v(lambda e: e.scalar_tensor_tensor(S1B, pb[7][:, 0:TS], 1.0 / D, S1B,
                                             OP.mult, OP.subtract))
        p.bar()
        p.a(lambda e: e.activation(S1B, S1B, AF.Sqrt, bias=sb_vecs[:, VC_E5:VC_E5 + 1]))
        p.bar()
        p.v(lambda e: e.reciprocal(S1B, S1B))
        p.bar()
        p.v(lambda e: e.tensor_tensor(MS1B, MS1B, S1B, OP.mult))
        p.bar()

        for oc in range(NCD):
            bank = pb[oc]
            for kc in range(NCD):
                lh = WAh[:, kc * D + oc * 128: kc * D + oc * 128 + 128]
                ll = WAl[:, kc * D + oc * 128: kc * D + oc * 128 + 128]
                rh = P2h[:, kc * TS:(kc + 1) * TS]
                rl = P2l[:, kc * TS:(kc + 1) * TS]
                mm3(bank[:, 0:TS], lh, ll, rh, rl, kc == 0, kc == NCD - 1)
        p.bar()
        for oc in range(NCD):
            bank = pb[oc]
            t3 = F1[:, 2 * TS:3 * TS]
            p.v(lambda e, t3=t3, xop=X0[oc][:, tcol]:
                e.tensor_tensor(t3, xop, S1B, OP.mult))
            p.bar()
            p.v(lambda e, t3=t3: e.tensor_tensor(t3, t3, MS1B, OP.subtract))
            p.bar()
            p.v(lambda e, oc=oc, b=bank, t3=t3:
                e.scalar_tensor_tensor(R1[:, oc * TS:(oc + 1) * TS],
                                       b[:, 0:TS], sb_vecs[:, VBO + oc:VBO + oc + 1],
                                       t3, OP.add, OP.add))
            p.bar()

        ln_stats(rc, None, VC_E6)
        UB = F1[:, 0:TS]
        QB = F1[:, TS:2 * TS]
        for c in range(NCD):
            t2 = F1[:, 2 * TS:3 * TS]
            yf = F1[:, 3 * TS:4 * TS]
            p.v(lambda e, rcc=rc[c], t2=t2: e.tensor_tensor(t2, rcc, UB, OP.mult))
            p.bar()
            p.v(lambda e, t2=t2: e.tensor_tensor(t2, t2, QB, OP.subtract))
            p.bar()
            p.a(lambda e, c=c, t2=t2, yf=yf:
                e.activation(yf, t2, AF.Identity,
                             bias=sb_vecs[:, VB2 + c:VB2 + c + 1],
                             scale=sb_vecs[:, VG2 + c:VG2 + c + 1]))
            p.bar()
            p.v(lambda e, c=c, yf=yf: e.tensor_copy(P1h[:, c * TS:(c + 1) * TS], yf))
            p.bar()
            p.v(lambda e, c=c, yf=yf:
                e.tensor_tensor(P1l[:, c * TS:(c + 1) * TS], yf,
                                P1h[:, c * TS:(c + 1) * TS], OP.subtract))
            p.bar()

        dma("sync", sb3(WAh[:, 0:4096], 8, 512), rows3(c_w1[0], 0, 8, 512))
        dma("sync", sb3(WAl[:, 0:4096], 8, 512), rows3(c_w1[1], 0, 8, 512))
        dma("sync", sb3(WBh[:, 0:4096], 4, D), rows3(c_w2[0], 0, 4, D))
        dma("sync", sb3(WBl[:, 0:4096], 4, D), rows3(c_w2[1], 0, 4, D))
        p.bar()
        for ft in range(8):
            ha = (ft % 2) * 4096
            for j in range(4):
                bank = pb[j]
                for kc in range(NCD):
                    lh = WAh[:, ha + kc * 512 + j * 128: ha + kc * 512 + j * 128 + 128]
                    ll = WAl[:, ha + kc * 512 + j * 128: ha + kc * 512 + j * 128 + 128]
                    rh = P1h[:, kc * TS:(kc + 1) * TS]
                    rl = P1l[:, kc * TS:(kc + 1) * TS]
                    mm3(bank[:, 0:TS], lh, ll, rh, rl, kc == 0, kc == NCD - 1)
            p.bar()
            for j in range(4):
                gf = F1[:, j * TS:(j + 1) * TS]
                p.a(lambda e, b=pb[j], gf=gf, ft=ft, j=j:
                    e.activation(gf, b[:, 0:TS], AF.Gelu_apprx_tanh,
                                 bias=sb_b1c[:, ft * 4 + j: ft * 4 + j + 1]))
            p.bar()
            for j in range(4):
                gf = F1[:, j * TS:(j + 1) * TS]
                p.v(lambda e, j=j, gf=gf: e.tensor_copy(U1a[:, j * TS:(j + 1) * TS], gf))
                p.v(lambda e, j=j, gf=gf:
                    e.scalar_tensor_tensor(U1b[:, j * TS:(j + 1) * TS], gf, 0.0,
                                           U1a[:, j * TS:(j + 1) * TS], OP.add, OP.subtract))
            p.bar()
            if ft < 7:
                ha2 = ((ft + 1) % 2) * 4096
                dma("sync", sb3(WAh[:, ha2:ha2 + 4096], 8, 512),
                    rows3(c_w1[0], (ft + 1) * 1024, 8, 512))
                dma("sync", sb3(WAl[:, ha2:ha2 + 4096], 8, 512),
                    rows3(c_w1[1], (ft + 1) * 1024, 8, 512))
            for dc in range(NCD):
                bank = pb[dc]
                for j in range(4):
                    lh = WBh[:, j * D + dc * 128: j * D + dc * 128 + 128]
                    ll = WBl[:, j * D + dc * 128: j * D + dc * 128 + 128]
                    rh = U1a[:, j * TS:(j + 1) * TS]
                    rl = U1b[:, j * TS:(j + 1) * TS]
                    mm3(bank[:, 0:TS], lh, ll, rh, rl, j == 0, j == 3)
            p.bar()
            if ft < 7:
                dma("sync", sb3(WBh[:, 0:4096], 4, D), rows3(c_w2[0], (ft + 1) * 4 * 128, 4, D))
                dma("sync", sb3(WBl[:, 0:4096], 4, D), rows3(c_w2[1], (ft + 1) * 4 * 128, 4, D))
            for dc in range(NCD):
                bank = pb[dc]
                rap = R1[:, dc * TS:(dc + 1) * TS]
                if ft == 7:
                    p.v(lambda e, dc=dc, b=bank, rap=rap:
                        e.scalar_tensor_tensor(rap, b[:, 0:TS],
                                               sb_vecs[:, VB2b + dc:VB2b + dc + 1],
                                               rap, OP.add, OP.add))
                else:
                    p.v(lambda e, b=bank, rap=rap:
                        e.tensor_tensor(rap, rap, b[:, 0:TS], OP.add))
            p.bar()

        ln_stats(rc, None, VC_E5)
        SB = F1[:, 0:TS]
        QB2 = F1[:, TS:2 * TS]
        for c in range(NCD):
            t2 = F1[:, 2 * TS:3 * TS]
            p.v(lambda e, rcc=rc[c], t2=t2: e.tensor_tensor(t2, rcc, SB, OP.mult))
            p.v(lambda e, t2=t2: e.tensor_tensor(t2, t2, QB2, OP.subtract))
            p.v(lambda e, xcc=xc[c], t2=t2: e.tensor_tensor(xcc, xcc, t2, OP.add))
        p.bar()

        if ti == 0:
            dbg("x1_0", X0[0][:, 0:TS])
        ln_stats(xc, None, VC_E5)
        SB = F1[:, 0:TS]
        QB2 = F1[:, TS:2 * TS]
        for c in range(NCD):
            t2 = F1[:, 2 * TS:3 * TS]
            wf = F1[:, 3 * TS:4 * TS]
            p.v(lambda e, xcc=xc[c], t2=t2: e.tensor_tensor(t2, xcc, SB, OP.mult))
            p.bar()
            p.v(lambda e, t2=t2, wf=wf: e.tensor_tensor(wf, t2, QB2, OP.subtract))
            p.bar()
            p.v(lambda e, c=c, wf=wf: e.tensor_copy(P1h[:, c * TS:(c + 1) * TS], wf))
            p.bar()
            p.v(lambda e, c=c, wf=wf:
                e.tensor_tensor(P1l[:, c * TS:(c + 1) * TS], wf,
                                P1h[:, c * TS:(c + 1) * TS], OP.subtract))
            p.bar()

        for kc in range(NCD):
            rh = P1h[:, kc * TS:(kc + 1) * TS]
            rl = P1l[:, kc * TS:(kc + 1) * TS]
            mm3(pb[2][0:8, 0:TS], sb_rwh[:, kc * 8:(kc + 1) * 8],
                sb_rwl[:, kc * 8:(kc + 1) * 8], rh, rl, kc == 0, kc == NCD - 1)
        p.bar()
        p.v(lambda e: e.tensor_copy(F2[0:8, 0:512], pb[2][0:8, 0:TS]))
        p.bar()
        if ti == 0:
            dbg("rlog", F2[0:8, 0:512])
        for j in range(4):
            p.t(lambda e, j=j: e.transpose(pb[3][:, j * 8:(j + 1) * 8],
                                           F2[0:8, j * 128:(j + 1) * 128],
                                           sb_id[0:8, 0:8]))
        p.bar()
        sc_ = F2[:, 512:1024]
        p.v(lambda e: e.tensor_copy(sc_[:, 0:32], pb[3][:, 0:32]))
        p.bar()
        for g in range(4):
            p.v(lambda e, g=g: e.reduce_max(sc_[:, 32 + g:33 + g],
                                            sc_[:, g * 8:(g + 1) * 8], axis=AX))
        p.bar()
        p.vdrain()
        for g in range(4):
            p.v(lambda e, g=g: e.tensor_scalar(sc_[:, 40 + g * 8:48 + g * 8],
                                               sc_[:, g * 8:(g + 1) * 8],
                                               sc_[:, 32 + g:33 + g], None, OP.subtract))
        p.bar()
        p.a(lambda e: e.activation(sc_[:, 40:72], sc_[:, 40:72], AF.Exp))
        p.bar()
        for g in range(4):
            p.v(lambda e, g=g: e.reduce_sum(sc_[:, 72 + g:73 + g],
                                            sc_[:, 40 + g * 8:48 + g * 8], axis=AX))
        p.bar()
        p.vdrain()
        p.v(lambda e: e.reciprocal(sc_[:, 76:80], sc_[:, 72:76]))
        p.bar()
        p.vdrain()
        for g in range(4):
            p.v(lambda e, g=g: e.tensor_scalar(sc_[:, 80 + g * 8:88 + g * 8],
                                               sc_[:, 40 + g * 8:48 + g * 8],
                                               sc_[:, 76 + g:77 + g], None, OP.mult))
        p.bar()
        p.vdrain()
        for g in range(4):
            p.v(lambda e, g=g: e.reduce_max(sc_[:, 112 + g:113 + g],
                                            sc_[:, 80 + g * 8:88 + g * 8], axis=AX))
        p.bar()
        p.vdrain()
        for g in range(4):
            p.v(lambda e, g=g: e.tensor_scalar(sc_[:, 120 + g * 8:128 + g * 8],
                                               sc_[:, 80 + g * 8:88 + g * 8],
                                               sc_[:, 112 + g:113 + g], None, OP.is_ge))
        p.bar()
        p.vdrain()
        p.v(lambda e: e.tensor_tensor(sc_[:, 120:152], sc_[:, 80:112],
                                      sc_[:, 120:152], OP.mult))
        p.bar()
        p.vdrain()
        p.v(lambda e: e.tensor_tensor(sc_[:, 152:184], sc_[:, 80:112],
                                      sc_[:, 120:152], OP.subtract))
        p.bar()
        p.vdrain()
        for g in range(4):
            p.v(lambda e, g=g: e.reduce_max(sc_[:, 184 + g:185 + g],
                                            sc_[:, 152 + g * 8:160 + g * 8], axis=AX))
        p.bar()
        p.vdrain()
        for g in range(4):
            p.v(lambda e, g=g: e.tensor_scalar(sc_[:, 192 + g * 8:200 + g * 8],
                                               sc_[:, 80 + g * 8:88 + g * 8],
                                               sc_[:, 184 + g:185 + g], None, OP.is_ge))
        p.bar()
        p.vdrain()
        p.v(lambda e: e.tensor_tensor(sc_[:, 224:256], sc_[:, 80:112],
                                      sc_[:, 192:224], OP.mult))
        p.bar()
        if ti == 0:
            dbg("smsc", sc_[:, 0:256])
        for j in range(4):
            p.t(lambda e, j=j: e.transpose(pb[2][0:8, j * 128:(j + 1) * 128],
                                           sc_[:, 224 + j * 8:224 + (j + 1) * 8],
                                           sb_id[:]))
        p.bar()
        p.v(lambda e: e.tensor_copy(F2[0:8, 0:512], pb[2][0:8, 0:512]))
        p.bar()
        for e_ in range(NE + 2):
            if e_ < NE:
                p.t(lambda e, e_=e_:
                    e.matmul(pb[e_ % 2][:, 0:TS],
                             sb_sels[0:8, 1280 + e_ * 128:1280 + (e_ + 1) * 128],
                             F2[0:8, 0:512], start=True, stop=True))
            if 1 <= e_ <= NE:
                p.v(lambda e, e_=e_ - 1:
                    e.tensor_copy(F1[:, (e_ % 2) * TS:(e_ % 2 + 1) * TS],
                                  pb[e_ % 2][:, 0:TS]))
            if 2 <= e_:
                dma("gpsimd", d_cb.ap()[(e_ - 2) * 128:(e_ - 1) * 128, :],
                    F1[:, ((e_ - 2) % 2) * TS:((e_ - 2) % 2 + 1) * TS])
            p.bar()

        if ti == 0:
            dbg("wh0", P1h[:, 0:TS])
            dbg("comb0", F2[0:8, 0:512])
        def eslot(s, kc, off, width):
            c = kc * 1024 + off
            if s == 0:
                return WAh[:, c:c + width]
            if s == 1:
                return WAl[:, c:c + width]
            t = WBh if c < 4096 else WBl
            return t[:, c % 4096: c % 4096 + width]

        def eload(s, cache2d, row0):
            if s == 0:
                dma("sync", sb3(WAh[:, 0:8192], 8, D), rows3(cache2d, row0, 8, D))
            elif s == 1:
                dma("sync", sb3(WAl[:, 0:8192], 8, D), rows3(cache2d, row0, 8, D))
            else:
                dma("sync", sb3(WBh[:, 0:4096], 4, D), rows3(cache2d, row0, 4, D))
                dma("sync", sb3(WBl[:, 0:4096], 4, D), rows3(cache2d, row0 + 4 * 128, 4, D))

        def w1_quarter(ei, q, s_h, s_l):
            for oc in (2 * q, 2 * q + 1):
                for kc in range(NCD):
                    lh = eslot(s_h, kc, oc * 128, 128)
                    ll = eslot(s_l, kc, oc * 128, 128)
                    rh = P1h[:, kc * TS:(kc + 1) * TS]
                    rl = P1l[:, kc * TS:(kc + 1) * TS]
                    mm3(pb[oc][:, 0:TS], lh, ll, rh, rl, kc == 0, kc == NCD - 1)

        def silu_q(q):
            for oc in (2 * q, 2 * q + 1):
                p.a(lambda e, b=pb[oc]: e.activation(b[:, 0:TS], b[:, 0:TS], AF.Silu))

        def msplit_q(q):
            for oc in (2 * q, 2 * q + 1):
                gf = F1[:, (oc % 4) * TS:(oc % 4 + 1) * TS]
                p.v(lambda e, b=pb[oc], gf=gf:
                    e.tensor_tensor(gf, b[:, 0:TS], F2[:, 512:1024], OP.mult))
                p.v(lambda e, oc=oc, gf=gf:
                    e.tensor_copy(P2h[:, oc * TS:(oc + 1) * TS], gf))
                p.v(lambda e, oc=oc, gf=gf:
                    e.scalar_tensor_tensor(P2l[:, oc * TS:(oc + 1) * TS], gf, 0.0,
                                           P2h[:, oc * TS:(oc + 1) * TS],
                                           OP.add, OP.subtract))

        def w2_pass(dcs, fcs, s_h, s_l, which):
            for dc in dcs:
                for fc in fcs:
                    lh = eslot(s_h, fc, dc * 128, 128)
                    ll = eslot(s_l, fc, dc * 128, 128)
                    rh = P2h[:, fc * TS:(fc + 1) * TS]
                    rl = P2l[:, fc * TS:(fc + 1) * TS]
                    out = pb[dc][:, 0:TS]
                    if which == "hi":
                        p.t(lambda e, o=out, a=lh, b=rh, f=(fc == 0):
                            e.matmul(o, a, b, start=f, stop=False))
                        p.t(lambda e, o=out, a=lh, b=rl:
                            e.matmul(o, a, b, start=False, stop=False))
                    else:
                        p.t(lambda e, o=out, a=ll, b=rh, s=(fc == NCD - 1):
                            e.matmul(o, a, b, start=False, stop=s))

        def evac_add(ei, dcs):
            for dc in dcs:
                rap = R1[:, dc * TS:(dc + 1) * TS]
                if ei == 0:
                    p.v(lambda e, b=pb[dc], rap=rap: e.tensor_copy(rap, b[:, 0:TS]))
                else:
                    p.v(lambda e, b=pb[dc], rap=rap:
                        e.tensor_tensor(rap, rap, b[:, 0:TS], OP.add))

        eload(0, c_e1[0], 0)
        eload(1, c_e1[1], 0)
        p.bar()
        for ei in range(NE):
            a_ = ei % 3
            s_e1h, s_e1l, s_e2h, s_e2l = a_, (a_ + 1) % 3, (a_ + 2) % 3, a_
            nxt = (ei + 1) % 3
            w1_quarter(ei, 0, s_e1h, s_e1l)
            if ei > 0:
                evac_add(ei - 1, (2, 3))
            eload(s_e2h, c_e2[0], ei * 8 * 128)
            dma("sync", F2[:, 512:1024], d_cb.ap()[ei * 128:(ei + 1) * 128, :])
            p.bar()
            w1_quarter(ei, 1, s_e1h, s_e1l)
            if ei > 0:
                evac_add(ei - 1, (4, 5))
            silu_q(0)
            p.bar()
            w1_quarter(ei, 2, s_e1h, s_e1l)
            if ei > 0:
                evac_add(ei - 1, (6, 7))
            msplit_q(0)
            silu_q(1)
            p.bar()
            w1_quarter(ei, 3, s_e1h, s_e1l)
            msplit_q(1)
            silu_q(2)
            p.bar()
            w2_pass(range(0, 4), range(0, 4), s_e2h, s_e2l, "hi")
            msplit_q(2)
            silu_q(3)
            eload(s_e2l, c_e2[1], ei * 8 * 128)
            p.bar()
            w2_pass(range(4, 6), range(0, 4), s_e2h, s_e2l, "hi")
            msplit_q(3)
            if ei < NE - 1:
                eload(nxt, c_e1[0], (ei + 1) * 8 * 128)
            p.bar()
            w2_pass(range(6, 8), range(0, 4), s_e2h, s_e2l, "hi")
            w2_pass(range(0, 8), range(4, 8), s_e2h, s_e2l, "hi")
            p.bar()
            w2_pass(range(0, 2), range(0, 8), s_e2h, s_e2l, "lo")
            if ei < NE - 1:
                eload((nxt + 1) % 3, c_e1[1], (ei + 1) * 8 * 128)
            p.bar()
            w2_pass(range(2, 8), range(0, 8), s_e2h, s_e2l, "lo")
            evac_add(ei, (0, 1))
            p.bar()
        evac_add(NE - 1, (2, 3, 4, 5, 6, 7))
        p.bar()

        for kc in range(NCD):
            rh = P1h[:, kc * TS:(kc + 1) * TS]
            rl = P1l[:, kc * TS:(kc + 1) * TS]
            mm3(pb[2][0:1, 0:TS], sb_sgwh[:, kc:kc + 1], sb_sgwl[:, kc:kc + 1],
                rh, rl, kc == 0, kc == NCD - 1)
        dma("sync", sb3(WAh[:, 0:8192], 8, 1024), rows3(c_sgu[0], 0, 8, 1024, c0=0))
        dma("sync", sb3(WAl[:, 0:8192], 8, 1024), rows3(c_sgu[1], 0, 8, 1024, c0=0))
        p.bar()
        p.a(lambda e: e.activation(F2[0:1, 512:1024], pb[2][0:1, 0:TS], AF.Sigmoid))
        p.bar()
        p.t(lambda e: e.matmul(pb[3][:, 0:TS], sb_sels[:, 0:128], F2[0:16, 512:1024],
                               start=True, stop=True))
        p.bar()
        p.v(lambda e: e.tensor_copy(F2[:, 512:1024], pb[3][:, 0:TS]))
        dma("sync", sb3(WBh[:, 0:4096], 8, 512), rows3(c_sgu[0], 0, 8, 512, c0=1024))
        dma("sync", sb3(WBl[:, 0:4096], 8, 512), rows3(c_sgu[1], 0, 8, 512, c0=1024))
        p.bar()
        for half in range(2):
            if half == 1:
                dma("sync", sb3(WBh[:, 0:4096], 8, 512), rows3(c_sgu[0], 0, 8, 512, c0=1536))
                dma("sync", sb3(WBl[:, 0:4096], 8, 512), rows3(c_sgu[1], 0, 8, 512, c0=1536))
                p.bar()
            for q in range(4):
                oc = half * 4 + q
                for kc in range(NCD):
                    lh = WAh[:, kc * 1024 + oc * 128: kc * 1024 + oc * 128 + 128]
                    ll = WAl[:, kc * 1024 + oc * 128: kc * 1024 + oc * 128 + 128]
                    rh = P1h[:, kc * TS:(kc + 1) * TS]
                    rl = P1l[:, kc * TS:(kc + 1) * TS]
                    mm3(pb[0][:, 0:TS], lh, ll, rh, rl, kc == 0, kc == NCD - 1)
                for kc in range(NCD):
                    lh = WBh[:, kc * 512 + q * 128: kc * 512 + q * 128 + 128]
                    ll = WBl[:, kc * 512 + q * 128: kc * 512 + q * 128 + 128]
                    rh = P1h[:, kc * TS:(kc + 1) * TS]
                    rl = P1l[:, kc * TS:(kc + 1) * TS]
                    mm3(pb[1][:, 0:TS], lh, ll, rh, rl, kc == 0, kc == NCD - 1)
                p.bar()
                s3s = F1[:, 2 * TS:3 * TS]
                p.a(lambda e, s3s=s3s: e.activation(s3s, pb[0][:, 0:TS], AF.Silu))
                p.bar()
                p.v(lambda e, s3s=s3s: e.tensor_tensor(s3s, s3s, pb[1][:, 0:TS], OP.mult))
                p.bar()
                p.v(lambda e, s3s=s3s: e.tensor_tensor(s3s, s3s, F2[:, 512:1024], OP.mult))
                p.bar()
                p.v(lambda e, q=q, s3s=s3s:
                    e.tensor_copy(U1a[:, q * 512:(q + 1) * 512], s3s))
                p.v(lambda e, q=q, s3s=s3s:
                    e.scalar_tensor_tensor(U1b[:, q * 512:(q + 1) * 512], s3s, 0.0,
                                           U1a[:, q * 512:(q + 1) * 512], OP.add, OP.subtract))
                p.bar()
            dma("sync", sb3(P2h[:, 0:4096], 4, D), rows3(c_sdn[0], half * 4 * 128, 4, D))
            dma("sync", sb3(P2l[:, 0:4096], 4, D), rows3(c_sdn[1], half * 4 * 128, 4, D))
            p.bar()
            for dc in range(NCD):
                for q in range(4):
                    lh = P2h[:, q * D + dc * 128: q * D + dc * 128 + 128]
                    ll = P2l[:, q * D + dc * 128: q * D + dc * 128 + 128]
                    rh = U1a[:, q * 512 + 0: q * 512 + 512]
                    rl = U1b[:, q * 512 + 0: q * 512 + 512]
                    mm3(pb[dc][:, 0:TS], lh, ll, rh, rl, q == 0, q == 3)
            p.bar()
            for dc in range(NCD):
                rap = R1[:, dc * TS:(dc + 1) * TS]
                p.v(lambda e, b=pb[dc], rap=rap:
                    e.tensor_tensor(rap, rap, b[:, 0:TS], OP.add))
            p.bar()

        if ti < NT - 1:
            dma("sync", sb3(P1h[:, 0:4096], 8, TS), rows3(d_pqh, (ti + 1) * NCD * 128, 8, TS))
            dma("sync", sb3(P1l[:, 0:4096], 8, TS), rows3(d_pql, (ti + 1) * NCD * 128, 8, TS))
            dma("sync", sb3(WAh[:, 0:8192], 8, D), rows3(c_wo[0], 0, 8, D))
            dma("sync", sb3(WAl[:, 0:8192], 8, D), rows3(c_wo[1], 0, 8, D))
        ln_stats(rc, None, VC_E5)
        SB = F1[:, 0:TS]
        QB2 = F1[:, TS:2 * TS]
        for c in range(NCD):
            t2 = F1[:, 2 * TS:3 * TS]
            p.v(lambda e, rcc=rc[c], t2=t2: e.tensor_tensor(t2, rcc, SB, OP.mult))
            p.v(lambda e, t2=t2: e.tensor_tensor(t2, t2, QB2, OP.subtract))
            p.v(lambda e, xcc=xc[c], t2=t2: e.tensor_tensor(xcc, xcc, t2, OP.add))
        p.bar()

        for blk in range(NBLK):
            for c in range(NCD):
                bank = pb[c // 4]
                p.t(lambda e, c=c, b=bank,
                    xap=X0[c][:, ti * TS + blk * 128: ti * TS + blk * 128 + 128]:
                    e.transpose(b[:, (c % 4) * 128:(c % 4) * 128 + 128], xap, sb_id[:]))
            p.bar()
            p.v(lambda e: e.tensor_copy(F2[:, 0:512], pb[0][:, 0:512]))
            p.v(lambda e: e.tensor_copy(F2[:, 512:1024], pb[1][:, 0:512]))
            p.bar()
            dma("gpsimd", d_out.ap()[ti * TS + blk * 128: ti * TS + (blk + 1) * 128, :],
                F2[:, 0:1024])
            p.bar()

    with (
        nc.semaphore("s_dma_sync") as s1,
        nc.semaphore("s_dma_gpsimd") as s2,
        nc.semaphore("s_x_gpsimd") as s3,
        nc.semaphore("s_x_vector") as s4,
        nc.semaphore("s_x_scalar") as s5,
        nc.semaphore("s_x_tensor") as s6,
        nc.semaphore("s_x_vector_d") as s7,
        nc.Block() as block,
    ):
        sems = {"dma_sync": s1, "dma_gpsimd": s2, "x_gpsimd": s3,
                "x_vector": s4, "x_scalar": s5, "x_tensor": s6,
                "x_vector_d": s7}
        p.emit(block, sems)

    return nc


def host_prep(inp):
    f32 = np.float32
    x = np.ascontiguousarray(np.asarray(inp["L0_attn_inputs"], f32)).reshape(2, 8192, D)
    wq = np.ascontiguousarray(np.asarray(inp["L0_attn_Wq"], f32).reshape(D, D))
    wk = np.ascontiguousarray(np.asarray(inp["L0_attn_Wk"], f32).reshape(D, D))
    wv = np.ascontiguousarray(np.asarray(inp["L0_attn_Wv"], f32).reshape(D, D))
    wo = np.ascontiguousarray(np.asarray(inp["L0_attn_Wo"], f32).reshape(D, D))
    w1 = np.asarray(inp["L0_attn_W1"], f32)
    w1t = np.ascontiguousarray(
        w1.reshape(8, 128, 8, 512).transpose(2, 0, 1, 3).reshape(8 * 8 * 128, 512))
    w2 = np.ascontiguousarray(np.asarray(inp["L0_attn_W2"], f32))
    e1 = np.ascontiguousarray(np.asarray(inp["L0_ffn_w1"], f32).reshape(NE * D, 1024))
    e2 = np.ascontiguousarray(np.asarray(inp["L0_ffn_w2"], f32).reshape(NE * D, 1024))
    sgu = np.ascontiguousarray(np.asarray(inp["L0_ffn_shared_gate_up"], f32))
    sdn = np.ascontiguousarray(np.asarray(inp["L0_ffn_shared_down"], f32))
    rw = np.ascontiguousarray(np.asarray(inp["L0_ffn_router_w"], f32)
                              .reshape(8, 128, 8).transpose(1, 0, 2).reshape(128, 64))
    sgw = np.ascontiguousarray(np.asarray(inp["L0_ffn_shared_gate_w"], f32)
                               .reshape(8, 128).T)

    def col8(v):
        return np.ascontiguousarray(np.asarray(v, f32).reshape(8, 128).T)

    vecs = np.zeros((128, 64), f32)
    vecs[:, VG1:VG1 + 8] = col8(inp["L0_attn_ln1_scale"])
    vecs[:, VB1:VB1 + 8] = col8(inp["L0_attn_ln1_bias"])
    vecs[:, VG2:VG2 + 8] = col8(inp["L0_attn_ln2_scale"])
    vecs[:, VB2:VB2 + 8] = col8(inp["L0_attn_ln2_bias"])
    vecs[:, VBQ:VBQ + 8] = col8(np.asarray(inp["L0_attn_bq"], f32).reshape(D))
    vecs[:, VBO:VBO + 8] = col8(inp["L0_attn_bo"])
    vecs[:, VB2b:VB2b + 8] = col8(inp["L0_attn_b2"])
    vecs[:, VC_LN1B] = 1e-11
    vecs[:, VC_LN1S] = 1.0 + 1e-6
    vecs[:, VC_E5] = 1e-5
    vecs[:, VC_E6] = 1e-6
    b1c = np.ascontiguousarray(np.asarray(inp["L0_attn_b1"], f32).reshape(32, 128).T)
    rowpack = np.zeros((1, 2176), f32)
    rowpack[0, 0:D] = np.asarray(inp["L0_attn_bk"], f32).reshape(D)
    rowpack[0, D:2 * D] = np.asarray(inp["L0_attn_bv"], f32).reshape(D)
    rowpack[0, 2048:2176] = 1.0

    ident = np.eye(128, dtype=f32)
    sels = np.zeros((16, 2304), f32)
    for j in range(2):
        sels[j, j * 128:(j + 1) * 128] = 1.0
    for c in range(8):
        sels[2 * c, 256 + c * 128:256 + c * 128 + 64] = 1.0
        sels[2 * c + 1, 256 + c * 128 + 64:256 + (c + 1) * 128] = 1.0
    for e_ in range(8):
        sels[e_, 1280 + e_ * 128:1280 + (e_ + 1) * 128] = 1.0
    onesel = np.zeros((128, 32), f32)
    onesel[:, 0] = 1.0
    onesel[:, 17] = 1.0

    shared = dict(wq=wq, wk=wk, wv=wv, wo=wo, w1t=w1t, w2=w2, e1=e1, e2=e2,
                  sgu=sgu, sdn=sdn, rw=rw, sgw=sgw, vecs=vecs, b1c=b1c,
                  rowpack=rowpack, ident=ident, sels=sels, onesel=onesel)
    in_maps = []
    for core in range(8):
        b = core // 4
        s0 = (core % 4) * TPC
        m = dict(shared)
        m["x"] = np.ascontiguousarray(x[b, s0:s0 + TPC])
        in_maps.append(m)
    return in_maps


def kernel(**inp):
    in_maps = host_prep(inp)
    nc = build_nc()
    res = run_bass_kernel_spmd(nc, in_maps, core_ids=list(range(8)))
    out = np.empty((2, 8192, D), np.float32)
    for core in range(8):
        b = core // 4
        s0 = (core % 4) * TPC
        out[b, s0:s0 + TPC] = np.asarray(res.results[core]["out"], np.float32)
    return out


# version 07
# speedup vs baseline: 1.1136x; 1.0248x over previous
import numpy as np
import concourse.bass as bass
import concourse.mybir as mybir
from concourse.bass_utils import run_bass_kernel_spmd
from concourse.alu_op_type import AluOpType as OP

F32 = mybir.dt.float32
BF16 = mybir.dt.bfloat16
AF = mybir.ActivationFunctionType
AX = mybir.AxisListType.X

TPC = 2048
NT = 4
TS = 512
NBLK = 4
D = 1024
NCD = 8
MLP = 4096
NE = 8

VG1, VB1, VG2, VB2, VBQ, VBO, VB2b = 0, 8, 16, 24, 32, 40, 48
VC_LN1B, VC_LN1S, VC_E5, VC_E6 = 56, 57, 58, 59


class Prog:

    def __init__(self):
        self.segs = [dict()]

    def _add(self, eng, fn, dma):
        self.segs[-1].setdefault(eng, []).append((fn, dma))

    def t(self, fn):
        self._add("tensor", fn, False)

    def v(self, fn):
        self._add("vector", fn, False)

    def a(self, fn):
        self._add("scalar", fn, False)

    def g(self, fn):
        self._add("gpsimd", fn, False)

    def dS(self, fn):
        self._add("sync", fn, True)

    def dG(self, fn):
        self._add("gpsimd", fn, True)

    def vdrain(self):
        self._add("vector", lambda e: e.drain(), False)

    def bar(self):
        if self.segs[-1]:
            self.segs.append(dict())

    def emit(self, block, sems):
        segs = [s for s in self.segs if s]
        n = len(segs)
        cum = {k: [0] * (n + 1) for k in sems if k != "x_vector_d"}
        for i, seg in enumerate(segs):
            for k in cum:
                cum[k][i + 1] = cum[k][i]
            for eng, ops in seg.items():
                ndma = sum(1 for _, d in ops if d)
                nex = sum(1 for _, d in ops if not d)
                if eng in ("sync", "gpsimd") and ndma:
                    cum["dma_" + eng][i + 1] += 16 * ndma
                if nex and eng != "sync":
                    cum["x_" + eng][i + 1] += 1

        def emit_engine(eng_name, eng):
            last_wait = {k: 0 for k in cum}
            for i, seg in enumerate(segs):
                ops = seg.get(eng_name)
                if not ops:
                    continue
                for k in cum:
                    if k == "x_" + eng_name and eng_name != "gpsimd":
                        continue
                    tgt = cum[k][i]
                    if tgt > last_wait[k]:
                        eng.wait_ge(sems[k], tgt)
                        last_wait[k] = tgt
                last_exec = None
                for fn, dma_ in ops:
                    instr = fn(eng)
                    if dma_:
                        instr.then_inc(sems["dma_" + eng_name], 16)
                    else:
                        last_exec = instr
                if last_exec is not None and eng_name != "sync":
                    last_exec.then_inc(sems["x_" + eng_name], 1)
            if eng_name == "sync":
                for k in cum:
                    if cum[k][n] > last_wait[k]:
                        eng.wait_ge(sems[k], cum[k][n])

        @block.sync
        def _(sync):
            emit_engine("sync", sync)

        @block.gpsimd
        def _(gpsimd):
            emit_engine("gpsimd", gpsimd)

        @block.vector
        def _(vector):
            emit_engine("vector", vector)

        @block.scalar
        def _(scalar):
            emit_engine("scalar", scalar)

        @block.tensor
        def _(tensor):
            emit_engine("tensor", tensor)


def rows3(t2d, r0, nblk, cols, c0=0):
    ap = t2d.ap()[r0:r0 + nblk * 128, :].rearrange("(a p) c -> p a c", p=128)
    if c0 != 0 or cols != ap.shape[2]:
        ap = ap[:, :, c0:c0 + cols]
    return ap


def sb3(sb_ap, nblk, cols):
    return sb_ap.rearrange("p (a c) -> p a c", c=cols)


def build_nc(debug=False):
    nc = bass.Bass("TRN2", target_bir_lowering=False)

    d_x = nc.dram_tensor("x", [TPC, D], F32, kind="ExternalInput")
    d_wq = nc.dram_tensor("wq", [NCD * 128, D], F32, kind="ExternalInput")
    d_wk = nc.dram_tensor("wk", [NCD * 128, D], F32, kind="ExternalInput")
    d_wv = nc.dram_tensor("wv", [NCD * 128, D], F32, kind="ExternalInput")
    d_wo = nc.dram_tensor("wo", [NCD * 128, D], F32, kind="ExternalInput")
    d_w1 = nc.dram_tensor("w1t", [8 * 8 * 128, 512], F32, kind="ExternalInput")
    d_w2 = nc.dram_tensor("w2", [32 * 128, D], F32, kind="ExternalInput")
    d_e1 = nc.dram_tensor("e1", [NE * 8 * 128, 1024], F32, kind="ExternalInput")
    d_e2 = nc.dram_tensor("e2", [NE * 8 * 128, 1024], F32, kind="ExternalInput")
    d_sgu = nc.dram_tensor("sgu", [8 * 128, 2048], F32, kind="ExternalInput")
    d_sdn = nc.dram_tensor("sdn", [8 * 128, 1024], F32, kind="ExternalInput")
    d_rw = nc.dram_tensor("rw", [128, 64], F32, kind="ExternalInput")
    d_sgw = nc.dram_tensor("sgw", [128, 8], F32, kind="ExternalInput")
    d_vecs = nc.dram_tensor("vecs", [128, 64], F32, kind="ExternalInput")
    d_b1c = nc.dram_tensor("b1c", [128, 32], F32, kind="ExternalInput")
    d_rowpack = nc.dram_tensor("rowpack", [1, 2176], F32, kind="ExternalInput")
    d_id = nc.dram_tensor("ident", [128, 128], F32, kind="ExternalInput")
    d_sels = nc.dram_tensor("sels", [16, 2304], F32, kind="ExternalInput")
    d_onesel = nc.dram_tensor("onesel", [128, 32], F32, kind="ExternalInput")

    d_out = nc.dram_tensor("out", [TPC, D], F32, kind="ExternalOutput")

    d_ccin = nc.dram_tensor("ccin", [128, 520], F32)
    d_ccout = nc.dram_tensor("ccout", [128, 520], F32)

    def cache(name, rows, cols):
        return (nc.dram_tensor(name + "_h", [rows, cols], BF16),
                nc.dram_tensor(name + "_l", [rows, cols], BF16))

    c_wq = cache("c_wq", NCD * 128, D)
    c_wk = cache("c_wk", NCD * 128, D)
    c_wv = cache("c_wv", NCD * 128, D)
    c_wo = cache("c_wo", NCD * 128, D)
    c_w1 = cache("c_w1", 8 * 8 * 128, 512)
    c_w2 = cache("c_w2", 32 * 128, D)
    c_e1 = cache("c_e1", NE * 8 * 128, 1024)
    c_e2 = cache("c_e2", NE * 8 * 128, 1024)
    c_sgu = cache("c_sgu", 8 * 128, 2048)
    c_sdn = cache("c_sdn", 8 * 128, 1024)

    d_pqh = nc.dram_tensor("pqh", [NT * NCD * 128, TS], BF16)
    d_pql = nc.dram_tensor("pql", [NT * NCD * 128, TS], BF16)
    d_rows = nc.dram_tensor("rows", [2, TPC], F32)
    d_cb = nc.dram_tensor("cbt", [NE * 128, TS], F32)

    X0 = [nc.alloc_sbuf_tensor(f"x0_{c}", [128, TPC], F32) for c in range(NCD)]
    kvacc = nc.alloc_sbuf_tensor("kvacc", [128, 520], F32)
    kvh = nc.alloc_sbuf_tensor("kvh", [128, 520], BF16)
    kvl = nc.alloc_sbuf_tensor("kvl", [128, 520], BF16)
    kselh = nc.alloc_sbuf_tensor("kselh", [128, 128], BF16)
    ksell = nc.alloc_sbuf_tensor("ksell", [128, 128], BF16)
    sb_id = nc.alloc_sbuf_tensor("sb_id", [128, 128], F32)
    sb_sels = nc.alloc_sbuf_tensor("sb_sels", [16, 2304], F32)
    sb_onesel = nc.alloc_sbuf_tensor("sb_onesel", [128, 32], F32)
    sb_rowpack = nc.alloc_sbuf_tensor("sb_rowpack", [1, 2176], F32)
    sb_vecs = nc.alloc_sbuf_tensor("sb_vecs", [128, 64], F32)
    sb_b1c = nc.alloc_sbuf_tensor("sb_b1c", [128, 32], F32)
    sb_rwh = nc.alloc_sbuf_tensor("sb_rwh", [128, 64], BF16)
    sb_rwl = nc.alloc_sbuf_tensor("sb_rwl", [128, 64], BF16)
    sb_sgwh = nc.alloc_sbuf_tensor("sb_sgwh", [128, 8], BF16)
    sb_sgwl = nc.alloc_sbuf_tensor("sb_sgwl", [128, 8], BF16)
    P1h = nc.alloc_sbuf_tensor("P1h", [128, 4096], BF16)
    P1l = nc.alloc_sbuf_tensor("P1l", [128, 4096], BF16)
    P2h = nc.alloc_sbuf_tensor("P2h", [128, 4096], BF16)
    P2l = nc.alloc_sbuf_tensor("P2l", [128, 4096], BF16)
    R1 = nc.alloc_sbuf_tensor("R1", [128, 4096], F32)
    F1 = nc.alloc_sbuf_tensor("F1", [128, 2048], F32)
    F2 = nc.alloc_sbuf_tensor("F2", [128, 1024], F32)
    WAh = nc.alloc_sbuf_tensor("WAh", [128, 8192], BF16)
    WAl = nc.alloc_sbuf_tensor("WAl", [128, 8192], BF16)
    WBh = nc.alloc_sbuf_tensor("WBh", [128, 4096], BF16)
    WBl = nc.alloc_sbuf_tensor("WBl", [128, 4096], BF16)
    U1a = nc.alloc_sbuf_tensor("U1a", [128, 2112], BF16)
    U1b = nc.alloc_sbuf_tensor("U1b", [128, 2112], BF16)

    pb = [nc.alloc_psum_tensor(f"pb{i}", [128, 512], F32) for i in range(8)]

    p = Prog()

    def mm3(out_ap, lh, ll, rh, rl, first, last):
        p.t(lambda e, o=out_ap, a=lh, b=rh, f=first: e.matmul(o, a, b, start=f, stop=False))
        p.t(lambda e, o=out_ap, a=lh, b=rl: e.matmul(o, a, b, start=False, stop=False))
        p.t(lambda e, o=out_ap, a=ll, b=rh, s=last: e.matmul(o, a, b, start=False, stop=s))

    def dma(q, out_ap, in_ap):
        fn = lambda e, o=out_ap, i=in_ap: e.dma_start(out=o, in_=i)
        (p.dS if q == "sync" else p.dG)(fn)

    def dbg(name, ap):
        if not debug:
            return
        shape = list(ap.shape)
        t = nc.dram_tensor("dbg_" + name, shape, ap.dtype, kind="ExternalOutput")
        p.bar()
        dma("gpsimd", t.ap(), ap)
        p.bar()

    def ln_stats(src_aps, sqrt_scale_col, sqrt_bias_col, keep_ti=None):
        for half in (0, 1):
            for j in range(4):
                c = half * 4 + j
                p.a(lambda e, s=src_aps[c], d=F1[:, j * TS:(j + 1) * TS]:
                    e.activation(d, s, AF.Square))
            p.bar()
            for j in range(4):
                c = half * 4 + j
                p.t(lambda e, r=src_aps[c], f=(c == 0):
                    e.matmul(pb[6][0:16, 0:TS], sb_onesel[:, 0:16], r, start=f, stop=False))
                p.t(lambda e, r=F1[:, j * TS:(j + 1) * TS], s=(c == 7):
                    e.matmul(pb[6][0:16, 0:TS], sb_onesel[:, 16:32], r, start=False, stop=s))
            p.bar()
        p.v(lambda e: e.tensor_copy(F2[0:2, 0:TS], pb[6][0:2, 0:TS]))
        p.bar()
        if keep_ti is not None:
            dma("gpsimd", d_rows.ap()[0:2, keep_ti * TS:(keep_ti + 1) * TS],
                F2[0:2, 0:TS])
        p.t(lambda e: e.matmul(pb[6][:, 0:TS], sb_sels[:, 0:128], F2[0:16, 0:TS],
                               start=True, stop=True))
        p.t(lambda e: e.matmul(pb[7][:, 0:TS], sb_sels[:, 128:256], F2[0:16, 0:TS],
                               start=True, stop=True))
        p.bar()
        sm = F1[:, 0:TS]
        sqm = F1[:, TS:2 * TS]
        p.v(lambda e: e.tensor_scalar(sqm, pb[6][:, 0:TS], 1.0 / D, None, OP.mult))
        p.bar()
        p.v(lambda e: e.tensor_tensor(sm, sqm, sqm, OP.mult))
        p.bar()
        p.v(lambda e: e.scalar_tensor_tensor(sm, pb[7][:, 0:TS], 1.0 / D, sm,
                                             OP.mult, OP.subtract))
        p.bar()
        if sqrt_scale_col is None:
            p.a(lambda e, bc=sqrt_bias_col:
                e.activation(sm, sm, AF.Sqrt, bias=sb_vecs[:, bc:bc + 1]))
        else:
            p.a(lambda e, bc=sqrt_bias_col, sc=sqrt_scale_col:
                e.activation(sm, sm, AF.Sqrt, bias=sb_vecs[:, bc:bc + 1],
                             scale=sb_vecs[:, sc:sc + 1]))
        p.bar()
        p.v(lambda e: e.reciprocal(sm, sm))
        p.bar()
        p.v(lambda e: e.tensor_tensor(sqm, sqm, sm, OP.mult))
        p.bar()

    dma("sync", sb_id[:], d_id.ap())
    dma("sync", sb_sels[:], d_sels.ap())
    dma("sync", sb_onesel[:], d_onesel.ap())
    dma("sync", sb_vecs[:], d_vecs.ap())
    dma("sync", sb_b1c[:], d_b1c.ap())
    dma("sync", sb_rowpack[:], d_rowpack.ap())
    p.v(lambda e: e.memset(kvacc[:], 0.0))
    p.v(lambda e: e.memset(F2[:], 0.0))
    p.bar()

    def pin_buf(blk):
        return F2[:, 0:1024] if blk % 2 == 0 else R1[:, 0:1024]

    for blk in range(TPC // 128 + 2):
        if blk < TPC // 128:
            dma("sync", pin_buf(blk), d_x.ap()[blk * 128:(blk + 1) * 128, :])
        if 1 <= blk <= TPC // 128:
            b0 = (blk - 1) % 2
            for c in range(NCD):
                bank = pb[2 * b0 + c // 4]
                p.t(lambda e, c=c, b=bank, b0=b0, blk=blk:
                    e.transpose(b[:, (c % 4) * 128:(c % 4) * 128 + 128],
                                pin_buf(blk - 1)[:, c * 128:(c + 1) * 128], sb_id[:]))
        if 2 <= blk:
            b1 = (blk - 2) % 2
            for c in range(NCD):
                bank = pb[2 * b1 + c // 4]
                p.v(lambda e, c=c, b=bank, blk=blk, b1=b1:
                    e.tensor_copy(X0[c][:, (blk - 2) * 128:(blk - 1) * 128],
                                  b[:, (c % 4) * 128:(c % 4) * 128 + 128]))
        p.bar()

    dbg("x0c0", X0[0][:, 0:512])

    slabs = []

    def add_slabs(src, ch, cl, nrb, cols):
        g = max(1, 2048 // cols)
        r = 0
        while r < nrb:
            gg = min(g, nrb - r)
            slabs.append((src, ch, cl, r * 128, gg, cols))
            r += gg

    add_slabs(d_wq, *c_wq, NCD, D)
    add_slabs(d_wk, *c_wk, NCD, D)
    add_slabs(d_wv, *c_wv, NCD, D)
    add_slabs(d_wo, *c_wo, NCD, D)
    add_slabs(d_w1, *c_w1, 64, 512)
    add_slabs(d_w2, *c_w2, 32, D)
    add_slabs(d_e1, *c_e1, 64, 1024)
    add_slabs(d_e2, *c_e2, 64, 1024)
    add_slabs(d_sgu, *c_sgu, 8, 2048)
    add_slabs(d_sdn, *c_sdn, 8, 1024)

    nsl = len(slabs)

    def p0in(k):
        s = k % 3
        return (R1[:, 0:2048] if s == 0 else
                R1[:, 2048:4096] if s == 1 else F1[:, 0:2048])

    for k in range(nsl + 3):
        if k < nsl:
            src_, ch, cl, r0, nb_, cols = slabs[k]
            w = nb_ * cols
            dma("sync", sb3(p0in(k)[:, 0:w], nb_, cols), rows3(src_, r0, nb_, cols))
        if 1 <= k <= nsl:
            src_, ch, cl, r0, nb_, cols = slabs[k - 1]
            w = nb_ * cols
            half = ((k - 1) % 2) * 2048
            p.a(lambda e, w=w, h=half, kk=k - 1:
                e.copy(P1h[:, h:h + w], p0in(kk)[:, 0:w]))
        if 2 <= k <= nsl + 1:
            src_, ch, cl, r0, nb_, cols = slabs[k - 2]
            w = nb_ * cols
            half = ((k - 2) % 2) * 2048
            p.v(lambda e, w=w, h=half, kk=k - 2:
                e.tensor_tensor(P1l[:, h:h + w], p0in(kk)[:, 0:w],
                                P1h[:, h:h + w], OP.subtract))
            dma("gpsimd", rows3(ch, r0, nb_, cols), sb3(P1h[:, half:half + w], nb_, cols))
        if 3 <= k:
            src_, ch, cl, r0, nb_, cols = slabs[k - 3]
            w = nb_ * cols
            half = ((k - 3) % 2) * 2048
            dma("gpsimd", rows3(cl, r0, nb_, cols), sb3(P1l[:, half:half + w], nb_, cols))
        p.bar()

    dma("sync", F1[:, 0:64], d_rw.ap())
    dma("sync", F1[:, 64:72], d_sgw.ap())
    p.bar()
    p.v(lambda e: e.tensor_copy(sb_rwh[:], F1[:, 0:64]))
    p.v(lambda e: e.tensor_copy(sb_sgwh[:], F1[:, 64:72]))
    p.bar()
    p.v(lambda e: e.tensor_tensor(sb_rwl[:], F1[:, 0:64], sb_rwh[:], OP.subtract))
    p.v(lambda e: e.tensor_tensor(sb_sgwl[:], F1[:, 64:72], sb_sgwh[:], OP.subtract))
    p.bar()

    for ti in range(NT):
        tcol = slice(ti * TS, (ti + 1) * TS)
        xc = [X0[c][:, tcol] for c in range(NCD)]

        dma("sync", sb3(WAh[:, 0:8192], 8, D), rows3(c_wq[0], 0, 8, D))
        dma("sync", sb3(WAl[:, 0:8192], 8, D), rows3(c_wq[1], 0, 8, D))
        ln_stats(xc, VC_LN1S, VC_LN1B, keep_ti=ti)
        if ti == 0:
            dbg("ln1U", F1[:, 0:TS])
            dbg("ln1Q", F1[:, TS:2 * TS])
        UB = F1[:, 0:TS]
        QB = F1[:, TS:2 * TS]
        for c in range(NCD):
            t2 = F1[:, 2 * TS:3 * TS]
            af = F1[:, 3 * TS:4 * TS]
            p.v(lambda e, xcc=xc[c], t2=t2: e.tensor_tensor(t2, xcc, UB, OP.mult))
            p.bar()
            if ti == 0 and c == 0:
                dbg("t2a", t2)
            p.v(lambda e, t2=t2: e.tensor_tensor(t2, t2, QB, OP.subtract))
            p.bar()
            if ti == 0 and c == 0:
                dbg("t2b", t2)
            p.a(lambda e, c=c, t2=t2, af=af:
                e.activation(af, t2, AF.Identity,
                             bias=sb_vecs[:, VB1 + c:VB1 + c + 1],
                             scale=sb_vecs[:, VG1 + c:VG1 + c + 1]))
            p.bar()
            if ti == 0 and c == 0:
                dbg("af0", af)
            p.v(lambda e, c=c, af=af: e.tensor_copy(P1h[:, c * TS:(c + 1) * TS], af))
            p.bar()
            p.v(lambda e, c=c, af=af:
                e.tensor_tensor(P1l[:, c * TS:(c + 1) * TS], af,
                                P1h[:, c * TS:(c + 1) * TS], OP.subtract))
            p.bar()

        if ti == 0:
            dbg("ah0", P1h[:, 0:TS])
            dbg("al0", P1l[:, 0:TS])
        def q_relu(oc):
            fq = F1[:, (2 + oc % 2) * TS:(3 + oc % 2) * TS]
            p.a(lambda e, b=pb[oc % 2], fq=fq, oc=oc:
                e.activation(fq, b[:, 0:TS], AF.Relu,
                             bias=sb_vecs[:, VBQ + oc:VBQ + oc + 1]))

        def q_split(oc):
            fq = F1[:, (2 + oc % 2) * TS:(3 + oc % 2) * TS]
            p.v(lambda e, fq=fq, oc=oc:
                e.tensor_scalar(P2h[:, oc * TS:(oc + 1) * TS], fq, 0.001, None, OP.add))
            p.v(lambda e, fq=fq, oc=oc:
                e.scalar_tensor_tensor(P2l[:, oc * TS:(oc + 1) * TS], fq, 0.001,
                                       P2h[:, oc * TS:(oc + 1) * TS], OP.add, OP.subtract))

        for oc in range(NCD + 2):
            if oc < NCD:
                bank = pb[oc % 2]
                for kc in range(NCD):
                    lh = WAh[:, kc * D + oc * 128: kc * D + oc * 128 + 128]
                    ll = WAl[:, kc * D + oc * 128: kc * D + oc * 128 + 128]
                    rh = P1h[:, kc * TS:(kc + 1) * TS]
                    rl = P1l[:, kc * TS:(kc + 1) * TS]
                    mm3(bank[:, 0:TS], lh, ll, rh, rl, kc == 0, kc == NCD - 1)
            if oc == 2:
                dma("sync", sb3(WBh[:, 0:4096], 4, D), rows3(c_wv[0], 0, 4, D))
                dma("sync", sb3(WBl[:, 0:4096], 4, D), rows3(c_wv[0], 4 * 128, 4, D))
            if oc == NCD:
                dma("sync", sb3(WAh[:, 0:8192], 8, D), rows3(c_wk[0], 0, 8, D))
                dma("sync", sb3(WAl[:, 0:8192], 8, D), rows3(c_wk[1], 0, 8, D))
            if oc >= 1 and oc - 1 < NCD:
                q_relu(oc - 1)
            if oc >= 2:
                q_split(oc - 2)
            if oc == NCD + 1:
                dma("gpsimd", rows3(d_pqh, ti * NCD * 128, 8, TS), sb3(P2h[:, 0:4096], 8, TS))
                dma("gpsimd", rows3(d_pql, ti * NCD * 128, 8, TS), sb3(P2l[:, 0:4096], 8, TS))
            p.bar()
        if ti == 0:
            dbg("pqh0", P2h[:, 0:TS])

        dma("sync", sb3(P2h[:, 0:4096], 4, D), rows3(c_wv[1], 0, 4, D))
        dma("sync", sb3(P2l[:, 0:4096], 4, D), rows3(c_wv[1], 4 * 128, 4, D))
        onesh = bass.AP(U1a, 1024 + 64, [[2112, 128], [65, 16], [1, 1]])
        onesl = bass.AP(U1b, 1024 + 64, [[2112, 128], [65, 16], [1, 1]])
        p.v(lambda e, o=onesh: e.memset(o, 1.0))
        p.v(lambda e, o=onesl: e.memset(o, 0.0))
        p.bar()

        def wvh(kc, hf):
            idx = kc * D + hf * 512
            t = WBh if idx < 4096 else WBl
            return t[:, idx % 4096: idx % 4096 + 512]

        def wvl(kc, hf):
            idx = kc * D + hf * 512
            t = P2h if idx < 4096 else P2l
            return t[:, idx % 4096: idx % 4096 + 512]

        def k_mms(blk):
            for hf in range(2):
                bank = pb[2 + hf]
                for kc in range(NCD):
                    ah = P1h[:, kc * TS + blk * 128: kc * TS + blk * 128 + 128]
                    al = P1l[:, kc * TS + blk * 128: kc * TS + blk * 128 + 128]
                    rh = WAh[:, kc * D + hf * 512: kc * D + hf * 512 + 512]
                    rl = WAl[:, kc * D + hf * 512: kc * D + hf * 512 + 512]
                    mm3(bank[:, 0:512], ah, al, rh, rl, kc == 0, False)
                p.t(lambda e, b=bank, hf=hf:
                    e.matmul(b[:, 0:512], sb_rowpack[0:1, 2048:2176],
                             sb_rowpack[0:1, hf * 512:hf * 512 + 512],
                             start=False, stop=True))

        def v_mms(blk):
            for hf in range(2):
                bank = pb[4 + hf]
                for kc in range(NCD):
                    ah = P1h[:, kc * TS + blk * 128: kc * TS + blk * 128 + 128]
                    al = P1l[:, kc * TS + blk * 128: kc * TS + blk * 128 + 128]
                    mm3(bank[:, 0:512], ah, al, wvh(kc, hf), wvl(kc, hf), kc == 0, False)
                p.t(lambda e, b=bank, hf=hf:
                    e.matmul(b[:, 0:512], sb_rowpack[0:1, 2048:2176],
                             sb_rowpack[0:1, 1024 + hf * 512:1024 + hf * 512 + 512],
                             start=False, stop=True))

        def k_relu(blk):
            for hf in range(2):
                fk = F1[:, (2 + hf) * TS:(3 + hf) * TS]
                p.a(lambda e, b=pb[2 + hf], fk=fk: e.activation(fk, b[:, 0:512], AF.Relu))

        def k_split(blk):
            for hf in range(2):
                fk = F1[:, (2 + hf) * TS:(3 + hf) * TS]
                p.v(lambda e, fk=fk, hf=hf:
                    e.tensor_scalar(U1a[:, hf * 512:hf * 512 + 512], fk, 0.001, None, OP.add))
                p.v(lambda e, fk=fk, hf=hf:
                    e.scalar_tensor_tensor(U1b[:, hf * 512:hf * 512 + 512], fk, 0.001,
                                           U1a[:, hf * 512:hf * 512 + 512], OP.add, OP.subtract))

        def v_split(blk):
            for hf in range(2):
                src_ = bass.AP(pb[4 + hf], 0, [[512, 128], [64, 8], [1, 64]])
                dsth = bass.AP(U1a, 1024 + hf * 8 * 65, [[2112, 128], [65, 8], [1, 64]])
                dstl = bass.AP(U1b, 1024 + hf * 8 * 65, [[2112, 128], [65, 8], [1, 64]])
                p.v(lambda e, s=src_, d=dsth: e.tensor_copy(d, s))
                p.v(lambda e, s=src_, dh=dsth, dl=dstl:
                    e.scalar_tensor_tensor(dl, s, 0.0, dh, OP.add, OP.subtract))

        def kv_mms(blk):
            for pair in range(8):
                bank = pb[pair // 4]
                col = (pair % 4) * 65
                for sub in range(2):
                    h_ = pair * 2 + sub
                    lh = U1a[:, h_ * 64:(h_ + 1) * 64]
                    ll = U1b[:, h_ * 64:(h_ + 1) * 64]
                    rh = U1a[:, 1024 + h_ * 65:1024 + (h_ + 1) * 65]
                    rl = U1b[:, 1024 + h_ * 65:1024 + (h_ + 1) * 65]
                    mm3(bank[sub * 64:(sub + 1) * 64, col:col + 65], lh, ll, rh, rl,
                        True, True)

        def kv_add(blk):
            for half in range(2):
                p.v(lambda e, b=pb[half], half=half:
                    e.tensor_tensor(kvacc[:, half * 260:(half + 1) * 260],
                                    kvacc[:, half * 260:(half + 1) * 260],
                                    b[:, 0:260], OP.add))

        for blk in range(NBLK + 2):
            if blk < NBLK:
                k_mms(blk)
            if 1 <= blk <= NBLK:
                k_split(blk - 1)
                v_split(blk - 1)
            if 2 <= blk:
                kv_add(blk - 2)
            p.bar()
            if blk < NBLK:
                v_mms(blk)
            if 1 <= blk <= NBLK:
                kv_mms(blk - 1)
            if blk < NBLK or (1 <= blk <= NBLK):
                pass
            if blk < NBLK:
                k_relu(blk)
            p.bar()

    dbg("kvpart", kvacc[:])
    dma("gpsimd", d_ccin.ap(), kvacc[:])
    p.bar()
    dbg("ccin", d_ccin.ap())
    p.g(lambda e: e.collective_compute(
        "AllReduce", mybir.AluOpType.add,
        replica_groups=[[0, 1, 2, 3], [4, 5, 6, 7]],
        ins=[d_ccin.ap()], outs=[d_ccout.ap()]))
    p.bar()
    dbg("ccout", d_ccout.ap())
    dma("gpsimd", kvacc[:], d_ccout.ap())
    p.bar()
    dbg("kvback", kvacc[:])
    p.v(lambda e: e.tensor_copy(kvh[:], kvacc[:]))
    p.bar()
    p.v(lambda e: e.tensor_tensor(kvl[:], kvacc[:], kvh[:], OP.subtract))
    p.v(lambda e: e.memset(kselh[:], 0.0))
    p.v(lambda e: e.memset(ksell[:], 0.0))
    p.bar()
    dbg("kvfull", kvacc[:])
    p.vdrain()
    for c in range(NCD):
        p.v(lambda e, c=c: e.tensor_copy(kselh[0:64, c * 16 + 2 * c:c * 16 + 2 * c + 1],
                                         kvh[0:64, c * 65 + 64:c * 65 + 65]))
        p.v(lambda e, c=c: e.tensor_copy(kselh[64:128, c * 16 + 2 * c + 1:c * 16 + 2 * c + 2],
                                         kvh[64:128, c * 65 + 64:c * 65 + 65]))
        p.v(lambda e, c=c: e.tensor_copy(ksell[0:64, c * 16 + 2 * c:c * 16 + 2 * c + 1],
                                         kvl[0:64, c * 65 + 64:c * 65 + 65]))
        p.v(lambda e, c=c: e.tensor_copy(ksell[64:128, c * 16 + 2 * c + 1:c * 16 + 2 * c + 2],
                                         kvl[64:128, c * 65 + 64:c * 65 + 65]))
    p.vdrain()
    p.bar()

    for ti in range(NT):
        tcol = slice(ti * TS, (ti + 1) * TS)
        xc = [X0[c][:, tcol] for c in range(NCD)]
        rc = [R1[:, c * TS:(c + 1) * TS] for c in range(NCD)]

        def eslot(s, kc, off, width):
            c = kc * 1024 + off
            if s == 0:
                return WAh[:, c:c + width]
            if s == 1:
                return WAl[:, c:c + width]
            t = WBh if c < 4096 else WBl
            return t[:, c % 4096: c % 4096 + width]

        def eload(s, cache2d, row0):
            if s == 0:
                dma("sync", sb3(WAh[:, 0:8192], 8, D), rows3(cache2d, row0, 8, D))
            elif s == 1:
                dma("sync", sb3(WAl[:, 0:8192], 8, D), rows3(cache2d, row0, 8, D))
            else:
                dma("sync", sb3(WBh[:, 0:4096], 4, D), rows3(cache2d, row0, 4, D))
                dma("sync", sb3(WBl[:, 0:4096], 4, D), rows3(cache2d, row0 + 4 * 128, 4, D))


        if ti == 0:
            dma("sync", sb3(P1h[:, 0:4096], 8, TS), rows3(d_pqh, 0, 8, TS))
            dma("sync", sb3(P1l[:, 0:4096], 8, TS), rows3(d_pql, 0, 8, TS))
            dma("sync", sb3(WAh[:, 0:8192], 8, D), rows3(c_wo[0], 0, 8, D))
            dma("sync", sb3(WAl[:, 0:8192], 8, D), rows3(c_wo[1], 0, 8, D))
            p.bar()

        for kc in range(NCD):
            rh = P1h[:, kc * TS:(kc + 1) * TS]
            rl = P1l[:, kc * TS:(kc + 1) * TS]
            mm3(pb[2][0:16, 0:TS], kselh[:, kc * 16:(kc + 1) * 16],
                ksell[:, kc * 16:(kc + 1) * 16], rh, rl, kc == 0, kc == NCD - 1)
        p.bar()
        p.v(lambda e: e.reciprocal(F2[0:16, 0:512], pb[2][0:16, 0:TS]))
        p.bar()
        if ti == 0:
            dbg("recip0", F2[0:16, 0:512])

        def attn_evac(c):
            nb_ = pb[c % 2]
            bb_ = pb[3 + c % 2]
            rb = F2[:, 512:1024]
            fa = F1[:, (2 + c % 2) * TS:(3 + c % 2) * TS]
            p.v(lambda e, b=bb_, rb=rb: e.tensor_copy(rb, b[:, 0:TS]))
            p.v(lambda e, b=nb_, rb=rb, fa=fa: e.tensor_tensor(fa, b[:, 0:TS], rb, OP.mult))
            p.v(lambda e, c=c, fa=fa: e.tensor_copy(P2h[:, c * TS:(c + 1) * TS], fa))
            p.v(lambda e, c=c, fa=fa:
                e.tensor_tensor(P2l[:, c * TS:(c + 1) * TS], fa,
                                P2h[:, c * TS:(c + 1) * TS], OP.subtract))

        for c in range(NCD + 1):
            if c < NCD:
                nb_ = pb[c % 2]
                bb_ = pb[3 + c % 2]
                for sub in range(2):
                    lh = kvh[sub * 64:(sub + 1) * 64, c * 65:c * 65 + 64]
                    ll = kvl[sub * 64:(sub + 1) * 64, c * 65:c * 65 + 64]
                    rh = P1h[sub * 64:(sub + 1) * 64, c * TS:(c + 1) * TS]
                    rl = P1l[sub * 64:(sub + 1) * 64, c * TS:(c + 1) * TS]
                    mm3(nb_[sub * 64:(sub + 1) * 64, 0:TS], lh, ll, rh, rl, True, True)
                p.t(lambda e, c=c, b=bb_:
                    e.matmul(b[:, 0:TS], sb_sels[:, 256 + c * 128:256 + (c + 1) * 128],
                             F2[0:16, 0:512], start=True, stop=True))
            if c >= 1:
                attn_evac(c - 1)
            p.bar()

        dma("sync", F2[0:2, 0:512], d_rows.ap()[0:2, tcol])
        p.bar()
        p.t(lambda e: e.matmul(pb[6][:, 0:TS], sb_sels[:, 0:128],
                               F2[0:16, 0:512], start=True, stop=True))
        p.t(lambda e: e.matmul(pb[7][:, 0:TS], sb_sels[:, 128:256],
                               F2[0:16, 0:512], start=True, stop=True))
        p.bar()
        S1B = F1[:, 0:TS]
        MS1B = F1[:, TS:2 * TS]
        p.v(lambda e: e.tensor_scalar(MS1B, pb[6][:, 0:TS], 1.0 / D, None, OP.mult))
        p.bar()
        p.v(lambda e: e.tensor_tensor(S1B, MS1B, MS1B, OP.mult))
        p.bar()
        p.v(lambda e: e.scalar_tensor_tensor(S1B, pb[7][:, 0:TS], 1.0 / D, S1B,
                                             OP.mult, OP.subtract))
        p.bar()
        p.a(lambda e: e.activation(S1B, S1B, AF.Sqrt, bias=sb_vecs[:, VC_E5:VC_E5 + 1]))
        p.bar()
        p.v(lambda e: e.reciprocal(S1B, S1B))
        p.bar()
        p.v(lambda e: e.tensor_tensor(MS1B, MS1B, S1B, OP.mult))
        p.bar()

        for oc in range(NCD):
            bank = pb[oc]
            for kc in range(NCD):
                lh = WAh[:, kc * D + oc * 128: kc * D + oc * 128 + 128]
                ll = WAl[:, kc * D + oc * 128: kc * D + oc * 128 + 128]
                rh = P2h[:, kc * TS:(kc + 1) * TS]
                rl = P2l[:, kc * TS:(kc + 1) * TS]
                mm3(bank[:, 0:TS], lh, ll, rh, rl, kc == 0, kc == NCD - 1)
        p.bar()
        for oc in range(NCD):
            bank = pb[oc]
            t3 = F1[:, 2 * TS:3 * TS]
            p.v(lambda e, t3=t3, xop=X0[oc][:, tcol]:
                e.tensor_tensor(t3, xop, S1B, OP.mult))
            p.bar()
            p.v(lambda e, t3=t3: e.tensor_tensor(t3, t3, MS1B, OP.subtract))
            p.bar()
            p.v(lambda e, oc=oc, b=bank, t3=t3:
                e.scalar_tensor_tensor(R1[:, oc * TS:(oc + 1) * TS],
                                       b[:, 0:TS], sb_vecs[:, VBO + oc:VBO + oc + 1],
                                       t3, OP.add, OP.add))
            p.bar()

        ln_stats(rc, None, VC_E6)
        UB = F1[:, 0:TS]
        QB = F1[:, TS:2 * TS]
        for c in range(NCD):
            t2 = F1[:, 2 * TS:3 * TS]
            yf = F1[:, 3 * TS:4 * TS]
            p.v(lambda e, rcc=rc[c], t2=t2: e.tensor_tensor(t2, rcc, UB, OP.mult))
            p.bar()
            p.v(lambda e, t2=t2: e.tensor_tensor(t2, t2, QB, OP.subtract))
            p.bar()
            p.a(lambda e, c=c, t2=t2, yf=yf:
                e.activation(yf, t2, AF.Identity,
                             bias=sb_vecs[:, VB2 + c:VB2 + c + 1],
                             scale=sb_vecs[:, VG2 + c:VG2 + c + 1]))
            p.bar()
            p.v(lambda e, c=c, yf=yf: e.tensor_copy(P1h[:, c * TS:(c + 1) * TS], yf))
            p.bar()
            p.v(lambda e, c=c, yf=yf:
                e.tensor_tensor(P1l[:, c * TS:(c + 1) * TS], yf,
                                P1h[:, c * TS:(c + 1) * TS], OP.subtract))
            p.bar()

        dma("sync", sb3(WAh[:, 0:4096], 8, 512), rows3(c_w1[0], 0, 8, 512))
        dma("sync", sb3(WAl[:, 0:4096], 8, 512), rows3(c_w1[1], 0, 8, 512))
        dma("sync", sb3(WBh[:, 0:4096], 4, D), rows3(c_w2[0], 0, 4, D))
        dma("sync", sb3(WBl[:, 0:4096], 4, D), rows3(c_w2[1], 0, 4, D))
        p.bar()
        for ft in range(8):
            ha = (ft % 2) * 4096
            for j in range(4):
                bank = pb[j]
                for kc in range(NCD):
                    lh = WAh[:, ha + kc * 512 + j * 128: ha + kc * 512 + j * 128 + 128]
                    ll = WAl[:, ha + kc * 512 + j * 128: ha + kc * 512 + j * 128 + 128]
                    rh = P1h[:, kc * TS:(kc + 1) * TS]
                    rl = P1l[:, kc * TS:(kc + 1) * TS]
                    mm3(bank[:, 0:TS], lh, ll, rh, rl, kc == 0, kc == NCD - 1)
            p.bar()
            for j in range(4):
                gf = F1[:, j * TS:(j + 1) * TS]
                p.a(lambda e, b=pb[j], gf=gf, ft=ft, j=j:
                    e.activation(gf, b[:, 0:TS], AF.Gelu_apprx_tanh,
                                 bias=sb_b1c[:, ft * 4 + j: ft * 4 + j + 1]))
            p.bar()
            for j in range(4):
                gf = F1[:, j * TS:(j + 1) * TS]
                p.v(lambda e, j=j, gf=gf: e.tensor_copy(U1a[:, j * TS:(j + 1) * TS], gf))
                p.v(lambda e, j=j, gf=gf:
                    e.scalar_tensor_tensor(U1b[:, j * TS:(j + 1) * TS], gf, 0.0,
                                           U1a[:, j * TS:(j + 1) * TS], OP.add, OP.subtract))
            p.bar()
            if ft < 7:
                ha2 = ((ft + 1) % 2) * 4096
                dma("sync", sb3(WAh[:, ha2:ha2 + 4096], 8, 512),
                    rows3(c_w1[0], (ft + 1) * 1024, 8, 512))
                dma("sync", sb3(WAl[:, ha2:ha2 + 4096], 8, 512),
                    rows3(c_w1[1], (ft + 1) * 1024, 8, 512))
            for dc in range(NCD):
                bank = pb[dc]
                for j in range(4):
                    lh = WBh[:, j * D + dc * 128: j * D + dc * 128 + 128]
                    ll = WBl[:, j * D + dc * 128: j * D + dc * 128 + 128]
                    rh = U1a[:, j * TS:(j + 1) * TS]
                    rl = U1b[:, j * TS:(j + 1) * TS]
                    mm3(bank[:, 0:TS], lh, ll, rh, rl, j == 0, j == 3)
            p.bar()
            if ft < 7:
                dma("sync", sb3(WBh[:, 0:4096], 4, D), rows3(c_w2[0], (ft + 1) * 4 * 128, 4, D))
                dma("sync", sb3(WBl[:, 0:4096], 4, D), rows3(c_w2[1], (ft + 1) * 4 * 128, 4, D))
            for dc in range(NCD):
                bank = pb[dc]
                rap = R1[:, dc * TS:(dc + 1) * TS]
                if ft == 7:
                    p.v(lambda e, dc=dc, b=bank, rap=rap:
                        e.scalar_tensor_tensor(rap, b[:, 0:TS],
                                               sb_vecs[:, VB2b + dc:VB2b + dc + 1],
                                               rap, OP.add, OP.add))
                else:
                    p.v(lambda e, b=bank, rap=rap:
                        e.tensor_tensor(rap, rap, b[:, 0:TS], OP.add))
            p.bar()

        ln_stats(rc, None, VC_E5)
        SB = F1[:, 0:TS]
        QB2 = F1[:, TS:2 * TS]
        for c in range(NCD):
            t2 = F1[:, 2 * TS:3 * TS]
            p.v(lambda e, rcc=rc[c], t2=t2: e.tensor_tensor(t2, rcc, SB, OP.mult))
            p.v(lambda e, t2=t2: e.tensor_tensor(t2, t2, QB2, OP.subtract))
            p.v(lambda e, xcc=xc[c], t2=t2: e.tensor_tensor(xcc, xcc, t2, OP.add))
        p.bar()

        if ti == 0:
            dbg("x1_0", X0[0][:, 0:TS])
        ln_stats(xc, None, VC_E5)
        SB = F1[:, 0:TS]
        QB2 = F1[:, TS:2 * TS]
        for c in range(NCD):
            t2 = F1[:, 2 * TS:3 * TS]
            wf = F1[:, 3 * TS:4 * TS]
            p.v(lambda e, xcc=xc[c], t2=t2: e.tensor_tensor(t2, xcc, SB, OP.mult))
            p.bar()
            p.v(lambda e, t2=t2, wf=wf: e.tensor_tensor(wf, t2, QB2, OP.subtract))
            p.bar()
            p.v(lambda e, c=c, wf=wf: e.tensor_copy(P1h[:, c * TS:(c + 1) * TS], wf))
            p.bar()
            p.v(lambda e, c=c, wf=wf:
                e.tensor_tensor(P1l[:, c * TS:(c + 1) * TS], wf,
                                P1h[:, c * TS:(c + 1) * TS], OP.subtract))
            p.bar()

        for kc in range(NCD):
            rh = P1h[:, kc * TS:(kc + 1) * TS]
            rl = P1l[:, kc * TS:(kc + 1) * TS]
            mm3(pb[2][0:8, 0:TS], sb_rwh[:, kc * 8:(kc + 1) * 8],
                sb_rwl[:, kc * 8:(kc + 1) * 8], rh, rl, kc == 0, kc == NCD - 1)
        p.bar()
        p.v(lambda e: e.tensor_copy(F2[0:8, 0:512], pb[2][0:8, 0:TS]))
        p.bar()
        if ti == 0:
            dbg("rlog", F2[0:8, 0:512])
        for j in range(4):
            p.t(lambda e, j=j: e.transpose(pb[3][:, j * 8:(j + 1) * 8],
                                           F2[0:8, j * 128:(j + 1) * 128],
                                           sb_id[0:8, 0:8]))
        p.bar()
        sc_ = F2[:, 512:1024]
        p.v(lambda e: e.tensor_copy(sc_[:, 0:32], pb[3][:, 0:32]))
        p.bar()
        for g in range(4):
            p.v(lambda e, g=g: e.reduce_max(sc_[:, 32 + g:33 + g],
                                            sc_[:, g * 8:(g + 1) * 8], axis=AX))
        p.bar()
        p.vdrain()
        for g in range(4):
            p.v(lambda e, g=g: e.tensor_scalar(sc_[:, 40 + g * 8:48 + g * 8],
                                               sc_[:, g * 8:(g + 1) * 8],
                                               sc_[:, 32 + g:33 + g], None, OP.subtract))
        p.bar()
        p.a(lambda e: e.activation(sc_[:, 40:72], sc_[:, 40:72], AF.Exp))
        p.bar()
        for g in range(4):
            p.v(lambda e, g=g: e.reduce_sum(sc_[:, 72 + g:73 + g],
                                            sc_[:, 40 + g * 8:48 + g * 8], axis=AX))
        p.bar()
        p.vdrain()
        p.v(lambda e: e.reciprocal(sc_[:, 76:80], sc_[:, 72:76]))
        p.bar()
        p.vdrain()
        for g in range(4):
            p.v(lambda e, g=g: e.tensor_scalar(sc_[:, 80 + g * 8:88 + g * 8],
                                               sc_[:, 40 + g * 8:48 + g * 8],
                                               sc_[:, 76 + g:77 + g], None, OP.mult))
        p.bar()
        p.vdrain()
        for g in range(4):
            p.v(lambda e, g=g: e.reduce_max(sc_[:, 112 + g:113 + g],
                                            sc_[:, 80 + g * 8:88 + g * 8], axis=AX))
        p.bar()
        p.vdrain()
        for g in range(4):
            p.v(lambda e, g=g: e.tensor_scalar(sc_[:, 120 + g * 8:128 + g * 8],
                                               sc_[:, 80 + g * 8:88 + g * 8],
                                               sc_[:, 112 + g:113 + g], None, OP.is_ge))
        p.bar()
        p.vdrain()
        p.v(lambda e: e.tensor_tensor(sc_[:, 120:152], sc_[:, 80:112],
                                      sc_[:, 120:152], OP.mult))
        p.bar()
        p.vdrain()
        p.v(lambda e: e.tensor_tensor(sc_[:, 152:184], sc_[:, 80:112],
                                      sc_[:, 120:152], OP.subtract))
        p.bar()
        p.vdrain()
        for g in range(4):
            p.v(lambda e, g=g: e.reduce_max(sc_[:, 184 + g:185 + g],
                                            sc_[:, 152 + g * 8:160 + g * 8], axis=AX))
        p.bar()
        p.vdrain()
        for g in range(4):
            p.v(lambda e, g=g: e.tensor_scalar(sc_[:, 192 + g * 8:200 + g * 8],
                                               sc_[:, 80 + g * 8:88 + g * 8],
                                               sc_[:, 184 + g:185 + g], None, OP.is_ge))
        p.bar()
        p.vdrain()
        p.v(lambda e: e.tensor_tensor(sc_[:, 224:256], sc_[:, 80:112],
                                      sc_[:, 192:224], OP.mult))
        p.bar()
        if ti == 0:
            dbg("smsc", sc_[:, 0:256])
        for j in range(4):
            p.t(lambda e, j=j: e.transpose(pb[2][0:8, j * 128:(j + 1) * 128],
                                           sc_[:, 224 + j * 8:224 + (j + 1) * 8],
                                           sb_id[:]))
        p.bar()
        p.v(lambda e: e.tensor_copy(F2[0:8, 0:512], pb[2][0:8, 0:512]))
        p.bar()
        eload(0, c_e1[0], 0)
        eload(1, c_e1[1], 0)
        for e_ in range(NE + 2):
            if e_ < NE:
                p.t(lambda e, e_=e_:
                    e.matmul(pb[e_ % 2][:, 0:TS],
                             sb_sels[0:8, 1280 + e_ * 128:1280 + (e_ + 1) * 128],
                             F2[0:8, 0:512], start=True, stop=True))
            if 1 <= e_ <= NE:
                p.v(lambda e, e_=e_ - 1:
                    e.tensor_copy(F1[:, (e_ % 2) * TS:(e_ % 2 + 1) * TS],
                                  pb[e_ % 2][:, 0:TS]))
            if 2 <= e_:
                dma("gpsimd", d_cb.ap()[(e_ - 2) * 128:(e_ - 1) * 128, :],
                    F1[:, ((e_ - 2) % 2) * TS:((e_ - 2) % 2 + 1) * TS])
            p.bar()

        if ti == 0:
            dbg("wh0", P1h[:, 0:TS])
            dbg("comb0", F2[0:8, 0:512])
        def w1_quarter(ei, q, s_h, s_l):
            for oc in (2 * q, 2 * q + 1):
                for kc in range(NCD):
                    lh = eslot(s_h, kc, oc * 128, 128)
                    ll = eslot(s_l, kc, oc * 128, 128)
                    rh = P1h[:, kc * TS:(kc + 1) * TS]
                    rl = P1l[:, kc * TS:(kc + 1) * TS]
                    mm3(pb[oc][:, 0:TS], lh, ll, rh, rl, kc == 0, kc == NCD - 1)

        def silu_q(q):
            for oc in (2 * q, 2 * q + 1):
                p.a(lambda e, b=pb[oc]: e.activation(b[:, 0:TS], b[:, 0:TS], AF.Silu))

        def msplit_q(q):
            for oc in (2 * q, 2 * q + 1):
                gf = F1[:, (oc % 4) * TS:(oc % 4 + 1) * TS]
                p.v(lambda e, b=pb[oc], gf=gf:
                    e.tensor_tensor(gf, b[:, 0:TS], F2[:, 512:1024], OP.mult))
                p.v(lambda e, oc=oc, gf=gf:
                    e.tensor_copy(P2h[:, oc * TS:(oc + 1) * TS], gf))
                p.v(lambda e, oc=oc, gf=gf:
                    e.scalar_tensor_tensor(P2l[:, oc * TS:(oc + 1) * TS], gf, 0.0,
                                           P2h[:, oc * TS:(oc + 1) * TS],
                                           OP.add, OP.subtract))

        def w2_pass(dcs, fcs, s_h, s_l, which):
            for dc in dcs:
                for fc in fcs:
                    lh = eslot(s_h, fc, dc * 128, 128)
                    ll = eslot(s_l, fc, dc * 128, 128)
                    rh = P2h[:, fc * TS:(fc + 1) * TS]
                    rl = P2l[:, fc * TS:(fc + 1) * TS]
                    out = pb[dc][:, 0:TS]
                    if which == "hi":
                        p.t(lambda e, o=out, a=lh, b=rh, f=(fc == 0):
                            e.matmul(o, a, b, start=f, stop=False))
                        p.t(lambda e, o=out, a=lh, b=rl:
                            e.matmul(o, a, b, start=False, stop=False))
                    else:
                        p.t(lambda e, o=out, a=ll, b=rh, s=(fc == NCD - 1):
                            e.matmul(o, a, b, start=False, stop=s))

        def evac_add(ei, dcs):
            for dc in dcs:
                rap = R1[:, dc * TS:(dc + 1) * TS]
                if ei == 0:
                    p.v(lambda e, b=pb[dc], rap=rap: e.tensor_copy(rap, b[:, 0:TS]))
                else:
                    p.v(lambda e, b=pb[dc], rap=rap:
                        e.tensor_tensor(rap, rap, b[:, 0:TS], OP.add))

        for ei in range(NE):
            a_ = ei % 3
            s_e1h, s_e1l, s_e2h, s_e2l = a_, (a_ + 1) % 3, (a_ + 2) % 3, a_
            nxt = (ei + 1) % 3
            w1_quarter(ei, 0, s_e1h, s_e1l)
            if ei > 0:
                evac_add(ei - 1, (2, 3))
            eload(s_e2h, c_e2[0], ei * 8 * 128)
            dma("sync", F2[:, 512:1024], d_cb.ap()[ei * 128:(ei + 1) * 128, :])
            p.bar()
            w1_quarter(ei, 1, s_e1h, s_e1l)
            if ei > 0:
                evac_add(ei - 1, (4, 5))
            silu_q(0)
            p.bar()
            w1_quarter(ei, 2, s_e1h, s_e1l)
            if ei > 0:
                evac_add(ei - 1, (6, 7))
            msplit_q(0)
            silu_q(1)
            p.bar()
            w1_quarter(ei, 3, s_e1h, s_e1l)
            msplit_q(1)
            silu_q(2)
            p.bar()
            w2_pass(range(0, 4), range(0, 4), s_e2h, s_e2l, "hi")
            msplit_q(2)
            silu_q(3)
            eload(s_e2l, c_e2[1], ei * 8 * 128)
            p.bar()
            w2_pass(range(4, 6), range(0, 4), s_e2h, s_e2l, "hi")
            msplit_q(3)
            if ei < NE - 1:
                eload(nxt, c_e1[0], (ei + 1) * 8 * 128)
            p.bar()
            w2_pass(range(6, 8), range(0, 4), s_e2h, s_e2l, "hi")
            w2_pass(range(0, 8), range(4, 8), s_e2h, s_e2l, "hi")
            p.bar()
            w2_pass(range(0, 2), range(0, 8), s_e2h, s_e2l, "lo")
            if ei < NE - 1:
                eload((nxt + 1) % 3, c_e1[1], (ei + 1) * 8 * 128)
            p.bar()
            w2_pass(range(2, 8), range(0, 8), s_e2h, s_e2l, "lo")
            evac_add(ei, (0, 1))
            p.bar()
        evac_add(NE - 1, (2, 3, 4, 5, 6, 7))
        p.bar()

        for kc in range(NCD):
            rh = P1h[:, kc * TS:(kc + 1) * TS]
            rl = P1l[:, kc * TS:(kc + 1) * TS]
            mm3(pb[2][0:1, 0:TS], sb_sgwh[:, kc:kc + 1], sb_sgwl[:, kc:kc + 1],
                rh, rl, kc == 0, kc == NCD - 1)
        dma("sync", sb3(WAh[:, 0:8192], 8, 1024), rows3(c_sgu[0], 0, 8, 1024, c0=0))
        dma("sync", sb3(WAl[:, 0:8192], 8, 1024), rows3(c_sgu[1], 0, 8, 1024, c0=0))
        p.bar()
        p.a(lambda e: e.activation(F2[0:1, 512:1024], pb[2][0:1, 0:TS], AF.Sigmoid))
        p.bar()
        p.t(lambda e: e.matmul(pb[3][:, 0:TS], sb_sels[:, 0:128], F2[0:16, 512:1024],
                               start=True, stop=True))
        p.bar()
        p.v(lambda e: e.tensor_copy(F2[:, 512:1024], pb[3][:, 0:TS]))
        dma("sync", sb3(WBh[:, 0:4096], 8, 512), rows3(c_sgu[0], 0, 8, 512, c0=1024))
        dma("sync", sb3(WBl[:, 0:4096], 8, 512), rows3(c_sgu[1], 0, 8, 512, c0=1024))
        p.bar()
        for half in range(2):
            if half == 1:
                dma("sync", sb3(WBh[:, 0:4096], 8, 512), rows3(c_sgu[0], 0, 8, 512, c0=1536))
                dma("sync", sb3(WBl[:, 0:4096], 8, 512), rows3(c_sgu[1], 0, 8, 512, c0=1536))
                p.bar()
            for q in range(4):
                oc = half * 4 + q
                for kc in range(NCD):
                    lh = WAh[:, kc * 1024 + oc * 128: kc * 1024 + oc * 128 + 128]
                    ll = WAl[:, kc * 1024 + oc * 128: kc * 1024 + oc * 128 + 128]
                    rh = P1h[:, kc * TS:(kc + 1) * TS]
                    rl = P1l[:, kc * TS:(kc + 1) * TS]
                    mm3(pb[0][:, 0:TS], lh, ll, rh, rl, kc == 0, kc == NCD - 1)
                for kc in range(NCD):
                    lh = WBh[:, kc * 512 + q * 128: kc * 512 + q * 128 + 128]
                    ll = WBl[:, kc * 512 + q * 128: kc * 512 + q * 128 + 128]
                    rh = P1h[:, kc * TS:(kc + 1) * TS]
                    rl = P1l[:, kc * TS:(kc + 1) * TS]
                    mm3(pb[1][:, 0:TS], lh, ll, rh, rl, kc == 0, kc == NCD - 1)
                p.bar()
                s3s = F1[:, 2 * TS:3 * TS]
                p.a(lambda e, s3s=s3s: e.activation(s3s, pb[0][:, 0:TS], AF.Silu))
                p.bar()
                p.v(lambda e, s3s=s3s: e.tensor_tensor(s3s, s3s, pb[1][:, 0:TS], OP.mult))
                p.bar()
                p.v(lambda e, s3s=s3s: e.tensor_tensor(s3s, s3s, F2[:, 512:1024], OP.mult))
                p.bar()
                p.v(lambda e, q=q, s3s=s3s:
                    e.tensor_copy(U1a[:, q * 512:(q + 1) * 512], s3s))
                p.v(lambda e, q=q, s3s=s3s:
                    e.scalar_tensor_tensor(U1b[:, q * 512:(q + 1) * 512], s3s, 0.0,
                                           U1a[:, q * 512:(q + 1) * 512], OP.add, OP.subtract))
                p.bar()
            dma("sync", sb3(P2h[:, 0:4096], 4, D), rows3(c_sdn[0], half * 4 * 128, 4, D))
            dma("sync", sb3(P2l[:, 0:4096], 4, D), rows3(c_sdn[1], half * 4 * 128, 4, D))
            p.bar()
            for dc in range(NCD):
                for q in range(4):
                    lh = P2h[:, q * D + dc * 128: q * D + dc * 128 + 128]
                    ll = P2l[:, q * D + dc * 128: q * D + dc * 128 + 128]
                    rh = U1a[:, q * 512 + 0: q * 512 + 512]
                    rl = U1b[:, q * 512 + 0: q * 512 + 512]
                    mm3(pb[dc][:, 0:TS], lh, ll, rh, rl, q == 0, q == 3)
            p.bar()
            for dc in range(NCD):
                rap = R1[:, dc * TS:(dc + 1) * TS]
                p.v(lambda e, b=pb[dc], rap=rap:
                    e.tensor_tensor(rap, rap, b[:, 0:TS], OP.add))
            p.bar()

        if ti < NT - 1:
            dma("sync", sb3(P1h[:, 0:4096], 8, TS), rows3(d_pqh, (ti + 1) * NCD * 128, 8, TS))
            dma("sync", sb3(P1l[:, 0:4096], 8, TS), rows3(d_pql, (ti + 1) * NCD * 128, 8, TS))
            dma("sync", sb3(WAh[:, 0:8192], 8, D), rows3(c_wo[0], 0, 8, D))
            dma("sync", sb3(WAl[:, 0:8192], 8, D), rows3(c_wo[1], 0, 8, D))
        ln_stats(rc, None, VC_E5)
        SB = F1[:, 0:TS]
        QB2 = F1[:, TS:2 * TS]
        for c in range(NCD):
            t2 = F1[:, 2 * TS:3 * TS]
            p.v(lambda e, rcc=rc[c], t2=t2: e.tensor_tensor(t2, rcc, SB, OP.mult))
            p.v(lambda e, t2=t2: e.tensor_tensor(t2, t2, QB2, OP.subtract))
            p.v(lambda e, xcc=xc[c], t2=t2: e.tensor_tensor(xcc, xcc, t2, OP.add))
        p.bar()

        for blk in range(NBLK):
            for c in range(NCD):
                bank = pb[c // 4]
                p.t(lambda e, c=c, b=bank,
                    xap=X0[c][:, ti * TS + blk * 128: ti * TS + blk * 128 + 128]:
                    e.transpose(b[:, (c % 4) * 128:(c % 4) * 128 + 128], xap, sb_id[:]))
            p.bar()
            p.v(lambda e: e.tensor_copy(F2[:, 0:512], pb[0][:, 0:512]))
            p.v(lambda e: e.tensor_copy(F2[:, 512:1024], pb[1][:, 0:512]))
            p.bar()
            dma("gpsimd", d_out.ap()[ti * TS + blk * 128: ti * TS + (blk + 1) * 128, :],
                F2[:, 0:1024])
            p.bar()

    with (
        nc.semaphore("s_dma_sync") as s1,
        nc.semaphore("s_dma_gpsimd") as s2,
        nc.semaphore("s_x_gpsimd") as s3,
        nc.semaphore("s_x_vector") as s4,
        nc.semaphore("s_x_scalar") as s5,
        nc.semaphore("s_x_tensor") as s6,
        nc.semaphore("s_x_vector_d") as s7,
        nc.Block() as block,
    ):
        sems = {"dma_sync": s1, "dma_gpsimd": s2, "x_gpsimd": s3,
                "x_vector": s4, "x_scalar": s5, "x_tensor": s6,
                "x_vector_d": s7}
        p.emit(block, sems)

    return nc


def host_prep(inp):
    f32 = np.float32
    x = np.ascontiguousarray(np.asarray(inp["L0_attn_inputs"], f32)).reshape(2, 8192, D)
    wq = np.ascontiguousarray(np.asarray(inp["L0_attn_Wq"], f32).reshape(D, D))
    wk = np.ascontiguousarray(np.asarray(inp["L0_attn_Wk"], f32).reshape(D, D))
    wv = np.ascontiguousarray(np.asarray(inp["L0_attn_Wv"], f32).reshape(D, D))
    wo = np.ascontiguousarray(np.asarray(inp["L0_attn_Wo"], f32).reshape(D, D))
    w1 = np.asarray(inp["L0_attn_W1"], f32)
    w1t = np.ascontiguousarray(
        w1.reshape(8, 128, 8, 512).transpose(2, 0, 1, 3).reshape(8 * 8 * 128, 512))
    w2 = np.ascontiguousarray(np.asarray(inp["L0_attn_W2"], f32))
    e1 = np.ascontiguousarray(np.asarray(inp["L0_ffn_w1"], f32).reshape(NE * D, 1024))
    e2 = np.ascontiguousarray(np.asarray(inp["L0_ffn_w2"], f32).reshape(NE * D, 1024))
    sgu = np.ascontiguousarray(np.asarray(inp["L0_ffn_shared_gate_up"], f32))
    sdn = np.ascontiguousarray(np.asarray(inp["L0_ffn_shared_down"], f32))
    rw = np.ascontiguousarray(np.asarray(inp["L0_ffn_router_w"], f32)
                              .reshape(8, 128, 8).transpose(1, 0, 2).reshape(128, 64))
    sgw = np.ascontiguousarray(np.asarray(inp["L0_ffn_shared_gate_w"], f32)
                               .reshape(8, 128).T)

    def col8(v):
        return np.ascontiguousarray(np.asarray(v, f32).reshape(8, 128).T)

    vecs = np.zeros((128, 64), f32)
    vecs[:, VG1:VG1 + 8] = col8(inp["L0_attn_ln1_scale"])
    vecs[:, VB1:VB1 + 8] = col8(inp["L0_attn_ln1_bias"])
    vecs[:, VG2:VG2 + 8] = col8(inp["L0_attn_ln2_scale"])
    vecs[:, VB2:VB2 + 8] = col8(inp["L0_attn_ln2_bias"])
    vecs[:, VBQ:VBQ + 8] = col8(np.asarray(inp["L0_attn_bq"], f32).reshape(D))
    vecs[:, VBO:VBO + 8] = col8(inp["L0_attn_bo"])
    vecs[:, VB2b:VB2b + 8] = col8(inp["L0_attn_b2"])
    vecs[:, VC_LN1B] = 1e-11
    vecs[:, VC_LN1S] = 1.0 + 1e-6
    vecs[:, VC_E5] = 1e-5
    vecs[:, VC_E6] = 1e-6
    b1c = np.ascontiguousarray(np.asarray(inp["L0_attn_b1"], f32).reshape(32, 128).T)
    rowpack = np.zeros((1, 2176), f32)
    rowpack[0, 0:D] = np.asarray(inp["L0_attn_bk"], f32).reshape(D)
    rowpack[0, D:2 * D] = np.asarray(inp["L0_attn_bv"], f32).reshape(D)
    rowpack[0, 2048:2176] = 1.0

    ident = np.eye(128, dtype=f32)
    sels = np.zeros((16, 2304), f32)
    for j in range(2):
        sels[j, j * 128:(j + 1) * 128] = 1.0
    for c in range(8):
        sels[2 * c, 256 + c * 128:256 + c * 128 + 64] = 1.0
        sels[2 * c + 1, 256 + c * 128 + 64:256 + (c + 1) * 128] = 1.0
    for e_ in range(8):
        sels[e_, 1280 + e_ * 128:1280 + (e_ + 1) * 128] = 1.0
    onesel = np.zeros((128, 32), f32)
    onesel[:, 0] = 1.0
    onesel[:, 17] = 1.0

    shared = dict(wq=wq, wk=wk, wv=wv, wo=wo, w1t=w1t, w2=w2, e1=e1, e2=e2,
                  sgu=sgu, sdn=sdn, rw=rw, sgw=sgw, vecs=vecs, b1c=b1c,
                  rowpack=rowpack, ident=ident, sels=sels, onesel=onesel)
    in_maps = []
    for core in range(8):
        b = core // 4
        s0 = (core % 4) * TPC
        m = dict(shared)
        m["x"] = np.ascontiguousarray(x[b, s0:s0 + TPC])
        in_maps.append(m)
    return in_maps


def kernel(**inp):
    in_maps = host_prep(inp)
    nc = build_nc()
    res = run_bass_kernel_spmd(nc, in_maps, core_ids=list(range(8)))
    out = np.empty((2, 8192, D), np.float32)
    for core in range(8):
        b = core // 4
        s0 = (core % 4) * TPC
        out[b, s0:s0 + TPC] = np.asarray(res.results[core]["out"], np.float32)
    return out


# version 08
# speedup vs baseline: 1.1497x; 1.0324x over previous
import numpy as np
import concourse.bass as bass
import concourse.mybir as mybir
from concourse.bass_utils import run_bass_kernel_spmd
from concourse.alu_op_type import AluOpType as OP

F32 = mybir.dt.float32
BF16 = mybir.dt.bfloat16
AF = mybir.ActivationFunctionType
AX = mybir.AxisListType.X

TPC = 2048
NT = 4
TS = 512
NBLK = 4
D = 1024
NCD = 8
MLP = 4096
NE = 8

VG1, VB1, VG2, VB2, VBQ, VBO, VB2b = 0, 8, 16, 24, 32, 40, 48
VC_LN1B, VC_LN1S, VC_E5, VC_E6 = 56, 57, 58, 59


class Prog:

    def __init__(self):
        self.segs = [dict()]

    def _add(self, eng, fn, dma):
        self.segs[-1].setdefault(eng, []).append((fn, dma))

    def t(self, fn):
        self._add("tensor", fn, False)

    def v(self, fn):
        self._add("vector", fn, False)

    def a(self, fn):
        self._add("scalar", fn, False)

    def g(self, fn):
        self._add("gpsimd", fn, False)

    def dS(self, fn):
        self._add("sync", fn, True)

    def dG(self, fn):
        self._add("gpsimd", fn, True)

    def vdrain(self):
        self._add("vector", lambda e: e.drain(), False)

    def bar(self):
        if self.segs[-1]:
            self.segs.append(dict())

    def emit(self, block, sems):
        segs = [s for s in self.segs if s]
        n = len(segs)
        cum = {k: [0] * (n + 1) for k in sems if k != "x_vector_d"}
        for i, seg in enumerate(segs):
            for k in cum:
                cum[k][i + 1] = cum[k][i]
            for eng, ops in seg.items():
                ndma = sum(1 for _, d in ops if d)
                nex = sum(1 for _, d in ops if not d)
                if eng in ("sync", "gpsimd") and ndma:
                    cum["dma_" + eng][i + 1] += 16 * ndma
                if nex and eng != "sync":
                    cum["x_" + eng][i + 1] += 1

        def emit_engine(eng_name, eng):
            last_wait = {k: 0 for k in cum}
            for i, seg in enumerate(segs):
                ops = seg.get(eng_name)
                if not ops:
                    continue
                for k in cum:
                    if k == "x_" + eng_name and eng_name != "gpsimd":
                        continue
                    tgt = cum[k][i]
                    if tgt > last_wait[k]:
                        eng.wait_ge(sems[k], tgt)
                        last_wait[k] = tgt
                last_exec = None
                for fn, dma_ in ops:
                    instr = fn(eng)
                    if dma_:
                        instr.then_inc(sems["dma_" + eng_name], 16)
                    else:
                        last_exec = instr
                if last_exec is not None and eng_name != "sync":
                    last_exec.then_inc(sems["x_" + eng_name], 1)
            if eng_name == "sync":
                for k in cum:
                    if cum[k][n] > last_wait[k]:
                        eng.wait_ge(sems[k], cum[k][n])

        @block.sync
        def _(sync):
            emit_engine("sync", sync)

        @block.gpsimd
        def _(gpsimd):
            emit_engine("gpsimd", gpsimd)

        @block.vector
        def _(vector):
            emit_engine("vector", vector)

        @block.scalar
        def _(scalar):
            emit_engine("scalar", scalar)

        @block.tensor
        def _(tensor):
            emit_engine("tensor", tensor)


def rows3(t2d, r0, nblk, cols, c0=0):
    ap = t2d.ap()[r0:r0 + nblk * 128, :].rearrange("(a p) c -> p a c", p=128)
    if c0 != 0 or cols != ap.shape[2]:
        ap = ap[:, :, c0:c0 + cols]
    return ap


def sb3(sb_ap, nblk, cols):
    return sb_ap.rearrange("p (a c) -> p a c", c=cols)


def build_nc(debug=False):
    nc = bass.Bass("TRN2", target_bir_lowering=False)

    d_x = nc.dram_tensor("x", [TPC, D], F32, kind="ExternalInput")
    d_wq = nc.dram_tensor("wq", [NCD * 128, D], F32, kind="ExternalInput")
    d_wk = nc.dram_tensor("wk", [NCD * 128, D], F32, kind="ExternalInput")
    d_wv = nc.dram_tensor("wv", [NCD * 128, D], F32, kind="ExternalInput")
    d_wo = nc.dram_tensor("wo", [NCD * 128, D], F32, kind="ExternalInput")
    d_w1 = nc.dram_tensor("w1t", [8 * 8 * 128, 512], F32, kind="ExternalInput")
    d_w2 = nc.dram_tensor("w2", [32 * 128, D], F32, kind="ExternalInput")
    d_e1 = nc.dram_tensor("e1", [NE * 8 * 128, 1024], F32, kind="ExternalInput")
    d_e2 = nc.dram_tensor("e2", [NE * 8 * 128, 1024], F32, kind="ExternalInput")
    d_sgu = nc.dram_tensor("sgu", [8 * 128, 2048], F32, kind="ExternalInput")
    d_sdn = nc.dram_tensor("sdn", [8 * 128, 1024], F32, kind="ExternalInput")
    d_rw = nc.dram_tensor("rw", [128, 64], F32, kind="ExternalInput")
    d_sgw = nc.dram_tensor("sgw", [128, 8], F32, kind="ExternalInput")
    d_vecs = nc.dram_tensor("vecs", [128, 64], F32, kind="ExternalInput")
    d_b1c = nc.dram_tensor("b1c", [128, 32], F32, kind="ExternalInput")
    d_rowpack = nc.dram_tensor("rowpack", [1, 2176], F32, kind="ExternalInput")
    d_id = nc.dram_tensor("ident", [128, 128], F32, kind="ExternalInput")
    d_sels = nc.dram_tensor("sels", [16, 2304], F32, kind="ExternalInput")
    d_onesel = nc.dram_tensor("onesel", [128, 32], F32, kind="ExternalInput")

    d_out = nc.dram_tensor("out", [TPC, D], F32, kind="ExternalOutput")

    d_ccin = nc.dram_tensor("ccin", [128, 520], F32)
    d_ccout = nc.dram_tensor("ccout", [128, 520], F32)

    def cache(name, rows, cols):
        return (nc.dram_tensor(name + "_h", [rows, cols], BF16),
                nc.dram_tensor(name + "_l", [rows, cols], BF16))

    c_wq = cache("c_wq", NCD * 128, D)
    c_wk = cache("c_wk", NCD * 128, D)
    c_wv = cache("c_wv", NCD * 128, D)
    c_wo = cache("c_wo", NCD * 128, D)
    c_w1 = cache("c_w1", 8 * 8 * 128, 512)
    c_w2 = cache("c_w2", 32 * 128, D)
    c_e1 = cache("c_e1", NE * 8 * 128, 1024)
    c_e2 = cache("c_e2", NE * 8 * 128, 1024)
    c_sgu = cache("c_sgu", 8 * 128, 2048)
    c_sdn = cache("c_sdn", 8 * 128, 1024)

    d_pqh = nc.dram_tensor("pqh", [NT * NCD * 128, TS], BF16)
    d_pql = nc.dram_tensor("pql", [NT * NCD * 128, TS], BF16)
    d_rows = nc.dram_tensor("rows", [2, TPC], F32)
    d_cb = nc.dram_tensor("cbt", [NE * 128, TS], F32)

    X0 = [nc.alloc_sbuf_tensor(f"x0_{c}", [128, TPC], F32) for c in range(NCD)]
    kvacc = nc.alloc_sbuf_tensor("kvacc", [128, 520], F32)
    kvh = nc.alloc_sbuf_tensor("kvh", [128, 520], BF16)
    kvl = nc.alloc_sbuf_tensor("kvl", [128, 520], BF16)
    kselh = nc.alloc_sbuf_tensor("kselh", [128, 128], BF16)
    ksell = nc.alloc_sbuf_tensor("ksell", [128, 128], BF16)
    sb_id = nc.alloc_sbuf_tensor("sb_id", [128, 128], F32)
    sb_sels = nc.alloc_sbuf_tensor("sb_sels", [16, 2304], F32)
    sb_onesel = nc.alloc_sbuf_tensor("sb_onesel", [128, 32], F32)
    sb_rowpack = nc.alloc_sbuf_tensor("sb_rowpack", [1, 2176], F32)
    sb_vecs = nc.alloc_sbuf_tensor("sb_vecs", [128, 64], F32)
    sb_b1c = nc.alloc_sbuf_tensor("sb_b1c", [128, 32], F32)
    sb_rwh = nc.alloc_sbuf_tensor("sb_rwh", [128, 64], BF16)
    sb_rwl = nc.alloc_sbuf_tensor("sb_rwl", [128, 64], BF16)
    sb_sgwh = nc.alloc_sbuf_tensor("sb_sgwh", [128, 8], BF16)
    sb_sgwl = nc.alloc_sbuf_tensor("sb_sgwl", [128, 8], BF16)
    P1h = nc.alloc_sbuf_tensor("P1h", [128, 4096], BF16)
    P1l = nc.alloc_sbuf_tensor("P1l", [128, 4096], BF16)
    P2h = nc.alloc_sbuf_tensor("P2h", [128, 4096], BF16)
    P2l = nc.alloc_sbuf_tensor("P2l", [128, 4096], BF16)
    R1 = nc.alloc_sbuf_tensor("R1", [128, 4096], F32)
    F1 = nc.alloc_sbuf_tensor("F1", [128, 2048], F32)
    F2 = nc.alloc_sbuf_tensor("F2", [128, 1024], F32)
    WAh = nc.alloc_sbuf_tensor("WAh", [128, 8192], BF16)
    WAl = nc.alloc_sbuf_tensor("WAl", [128, 8192], BF16)
    WBh = nc.alloc_sbuf_tensor("WBh", [128, 4096], BF16)
    WBl = nc.alloc_sbuf_tensor("WBl", [128, 4096], BF16)
    U1a = nc.alloc_sbuf_tensor("U1a", [128, 2112], BF16)
    U1b = nc.alloc_sbuf_tensor("U1b", [128, 2112], BF16)

    pb = [nc.alloc_psum_tensor(f"pb{i}", [128, 512], F32) for i in range(8)]

    p = Prog()

    def mm3(out_ap, lh, ll, rh, rl, first, last):
        p.t(lambda e, o=out_ap, a=lh, b=rh, f=first: e.matmul(o, a, b, start=f, stop=False))
        p.t(lambda e, o=out_ap, a=lh, b=rl: e.matmul(o, a, b, start=False, stop=False))
        p.t(lambda e, o=out_ap, a=ll, b=rh, s=last: e.matmul(o, a, b, start=False, stop=s))

    def dma(q, out_ap, in_ap):
        fn = lambda e, o=out_ap, i=in_ap: e.dma_start(out=o, in_=i)
        (p.dS if q == "sync" else p.dG)(fn)

    def dbg(name, ap):
        if not debug:
            return
        shape = list(ap.shape)
        t = nc.dram_tensor("dbg_" + name, shape, ap.dtype, kind="ExternalOutput")
        p.bar()
        dma("gpsimd", t.ap(), ap)
        p.bar()

    def ln_stats(src_aps, sqrt_scale_col, sqrt_bias_col, keep_ti=None):
        for half in (0, 1):
            for j in range(4):
                c = half * 4 + j
                p.a(lambda e, s=src_aps[c], d=F1[:, j * TS:(j + 1) * TS]:
                    e.activation(d, s, AF.Square))
            if half == 0:
                for c in range(NCD):
                    p.t(lambda e, r=src_aps[c], f=(c == 0):
                        e.matmul(pb[6][0:16, 0:TS], sb_onesel[:, 0:16], r,
                                 start=f, stop=False))
            p.bar()
            for j in range(4):
                c = half * 4 + j
                p.t(lambda e, r=F1[:, j * TS:(j + 1) * TS], s=(c == 7):
                    e.matmul(pb[6][0:16, 0:TS], sb_onesel[:, 16:32], r, start=False, stop=s))
            p.bar()
        p.v(lambda e: e.tensor_copy(F2[0:2, 0:TS], pb[6][0:2, 0:TS]))
        p.bar()
        if keep_ti is not None:
            dma("gpsimd", d_rows.ap()[0:2, keep_ti * TS:(keep_ti + 1) * TS],
                F2[0:2, 0:TS])
        p.t(lambda e: e.matmul(pb[6][:, 0:TS], sb_sels[:, 0:128], F2[0:16, 0:TS],
                               start=True, stop=True))
        p.t(lambda e: e.matmul(pb[7][:, 0:TS], sb_sels[:, 128:256], F2[0:16, 0:TS],
                               start=True, stop=True))
        p.bar()
        sm = F1[:, 0:TS]
        sqm = F1[:, TS:2 * TS]
        p.v(lambda e: e.tensor_scalar(sqm, pb[6][:, 0:TS], 1.0 / D, None, OP.mult))
        p.bar()
        p.v(lambda e: e.tensor_tensor(sm, sqm, sqm, OP.mult))
        p.bar()
        p.v(lambda e: e.scalar_tensor_tensor(sm, pb[7][:, 0:TS], 1.0 / D, sm,
                                             OP.mult, OP.subtract))
        p.bar()
        if sqrt_scale_col is None:
            p.a(lambda e, bc=sqrt_bias_col:
                e.activation(sm, sm, AF.Sqrt, bias=sb_vecs[:, bc:bc + 1]))
        else:
            p.a(lambda e, bc=sqrt_bias_col, sc=sqrt_scale_col:
                e.activation(sm, sm, AF.Sqrt, bias=sb_vecs[:, bc:bc + 1],
                             scale=sb_vecs[:, sc:sc + 1]))
        p.bar()
        p.v(lambda e: e.reciprocal(sm, sm))
        p.bar()
        p.v(lambda e: e.tensor_tensor(sqm, sqm, sm, OP.mult))
        p.bar()

    dma("sync", sb_id[:], d_id.ap())
    dma("sync", sb_sels[:], d_sels.ap())
    dma("sync", sb_onesel[:], d_onesel.ap())
    dma("sync", sb_vecs[:], d_vecs.ap())
    dma("sync", sb_b1c[:], d_b1c.ap())
    dma("sync", sb_rowpack[:], d_rowpack.ap())
    p.v(lambda e: e.memset(kvacc[:], 0.0))
    p.v(lambda e: e.memset(F2[:], 0.0))
    p.bar()

    def pin_buf(blk):
        return F2[:, 0:1024] if blk % 2 == 0 else R1[:, 0:1024]

    for blk in range(TPC // 128 + 2):
        if blk < TPC // 128:
            dma("sync", pin_buf(blk), d_x.ap()[blk * 128:(blk + 1) * 128, :])
        if 1 <= blk <= TPC // 128:
            b0 = (blk - 1) % 2
            for c in range(NCD):
                bank = pb[2 * b0 + c // 4]
                p.t(lambda e, c=c, b=bank, b0=b0, blk=blk:
                    e.transpose(b[:, (c % 4) * 128:(c % 4) * 128 + 128],
                                pin_buf(blk - 1)[:, c * 128:(c + 1) * 128], sb_id[:]))
        if 2 <= blk:
            b1 = (blk - 2) % 2
            for c in range(NCD):
                bank = pb[2 * b1 + c // 4]
                p.v(lambda e, c=c, b=bank, blk=blk, b1=b1:
                    e.tensor_copy(X0[c][:, (blk - 2) * 128:(blk - 1) * 128],
                                  b[:, (c % 4) * 128:(c % 4) * 128 + 128]))
        p.bar()

    dbg("x0c0", X0[0][:, 0:512])

    slabs = []

    def add_slabs(src, ch, cl, nrb, cols):
        g = max(1, 2048 // cols)
        r = 0
        while r < nrb:
            gg = min(g, nrb - r)
            slabs.append((src, ch, cl, r * 128, gg, cols))
            r += gg

    add_slabs(d_wq, *c_wq, NCD, D)
    add_slabs(d_wk, *c_wk, NCD, D)
    add_slabs(d_wv, *c_wv, NCD, D)
    add_slabs(d_wo, *c_wo, NCD, D)
    add_slabs(d_w1, *c_w1, 64, 512)
    add_slabs(d_w2, *c_w2, 32, D)
    add_slabs(d_e1, *c_e1, 64, 1024)
    add_slabs(d_e2, *c_e2, 64, 1024)
    add_slabs(d_sgu, *c_sgu, 8, 2048)
    add_slabs(d_sdn, *c_sdn, 8, 1024)

    nsl = len(slabs)

    def p0in(k):
        s = k % 3
        return (R1[:, 0:2048] if s == 0 else
                R1[:, 2048:4096] if s == 1 else F1[:, 0:2048])

    for k in range(nsl + 3):
        if k < nsl:
            src_, ch, cl, r0, nb_, cols = slabs[k]
            w = nb_ * cols
            dma("sync", sb3(p0in(k)[:, 0:w], nb_, cols), rows3(src_, r0, nb_, cols))
        if 1 <= k <= nsl:
            src_, ch, cl, r0, nb_, cols = slabs[k - 1]
            w = nb_ * cols
            half = ((k - 1) % 2) * 2048
            p.a(lambda e, w=w, h=half, kk=k - 1:
                e.copy(P1h[:, h:h + w], p0in(kk)[:, 0:w]))
        if 2 <= k <= nsl + 1:
            src_, ch, cl, r0, nb_, cols = slabs[k - 2]
            w = nb_ * cols
            half = ((k - 2) % 2) * 2048
            p.v(lambda e, w=w, h=half, kk=k - 2:
                e.tensor_tensor(P1l[:, h:h + w], p0in(kk)[:, 0:w],
                                P1h[:, h:h + w], OP.subtract))
            dma("gpsimd", rows3(ch, r0, nb_, cols), sb3(P1h[:, half:half + w], nb_, cols))
        if 3 <= k:
            src_, ch, cl, r0, nb_, cols = slabs[k - 3]
            w = nb_ * cols
            half = ((k - 3) % 2) * 2048
            dma("gpsimd", rows3(cl, r0, nb_, cols), sb3(P1l[:, half:half + w], nb_, cols))
        p.bar()

    dma("sync", F1[:, 0:64], d_rw.ap())
    dma("sync", F1[:, 64:72], d_sgw.ap())
    p.bar()
    p.v(lambda e: e.tensor_copy(sb_rwh[:], F1[:, 0:64]))
    p.v(lambda e: e.tensor_copy(sb_sgwh[:], F1[:, 64:72]))
    p.bar()
    p.v(lambda e: e.tensor_tensor(sb_rwl[:], F1[:, 0:64], sb_rwh[:], OP.subtract))
    p.v(lambda e: e.tensor_tensor(sb_sgwl[:], F1[:, 64:72], sb_sgwh[:], OP.subtract))
    p.bar()

    for ti in range(NT):
        tcol = slice(ti * TS, (ti + 1) * TS)
        xc = [X0[c][:, tcol] for c in range(NCD)]

        dma("sync", sb3(WAh[:, 0:8192], 8, D), rows3(c_wq[0], 0, 8, D))
        dma("sync", sb3(WAl[:, 0:8192], 8, D), rows3(c_wq[1], 0, 8, D))
        ln_stats(xc, VC_LN1S, VC_LN1B, keep_ti=ti)
        if ti == 0:
            dbg("ln1U", F1[:, 0:TS])
            dbg("ln1Q", F1[:, TS:2 * TS])
        UB = F1[:, 0:TS]
        QB = F1[:, TS:2 * TS]
        for c in range(NCD):
            t2 = F1[:, 2 * TS:3 * TS]
            af = F1[:, 3 * TS:4 * TS]
            p.v(lambda e, xcc=xc[c], t2=t2: e.tensor_tensor(t2, xcc, UB, OP.mult))
            p.bar()
            if ti == 0 and c == 0:
                dbg("t2a", t2)
            p.v(lambda e, t2=t2: e.tensor_tensor(t2, t2, QB, OP.subtract))
            p.bar()
            if ti == 0 and c == 0:
                dbg("t2b", t2)
            p.a(lambda e, c=c, t2=t2, af=af:
                e.activation(af, t2, AF.Identity,
                             bias=sb_vecs[:, VB1 + c:VB1 + c + 1],
                             scale=sb_vecs[:, VG1 + c:VG1 + c + 1]))
            p.bar()
            if ti == 0 and c == 0:
                dbg("af0", af)
            p.v(lambda e, c=c, af=af: e.tensor_copy(P1h[:, c * TS:(c + 1) * TS], af))
            p.bar()
            p.v(lambda e, c=c, af=af:
                e.tensor_tensor(P1l[:, c * TS:(c + 1) * TS], af,
                                P1h[:, c * TS:(c + 1) * TS], OP.subtract))
            p.bar()

        if ti == 0:
            dbg("ah0", P1h[:, 0:TS])
            dbg("al0", P1l[:, 0:TS])
        def q_relu(oc):
            fq = F1[:, (2 + oc % 2) * TS:(3 + oc % 2) * TS]
            p.a(lambda e, b=pb[oc % 2], fq=fq, oc=oc:
                e.activation(fq, b[:, 0:TS], AF.Relu,
                             bias=sb_vecs[:, VBQ + oc:VBQ + oc + 1]))

        def q_split(oc):
            fq = F1[:, (2 + oc % 2) * TS:(3 + oc % 2) * TS]
            p.v(lambda e, fq=fq, oc=oc:
                e.tensor_scalar(P2h[:, oc * TS:(oc + 1) * TS], fq, 0.001, None, OP.add))
            p.v(lambda e, fq=fq, oc=oc:
                e.scalar_tensor_tensor(P2l[:, oc * TS:(oc + 1) * TS], fq, 0.001,
                                       P2h[:, oc * TS:(oc + 1) * TS], OP.add, OP.subtract))

        for oc in range(NCD + 2):
            if oc < NCD:
                bank = pb[oc % 2]
                for kc in range(NCD):
                    lh = WAh[:, kc * D + oc * 128: kc * D + oc * 128 + 128]
                    ll = WAl[:, kc * D + oc * 128: kc * D + oc * 128 + 128]
                    rh = P1h[:, kc * TS:(kc + 1) * TS]
                    rl = P1l[:, kc * TS:(kc + 1) * TS]
                    mm3(bank[:, 0:TS], lh, ll, rh, rl, kc == 0, kc == NCD - 1)
            if oc == 2:
                dma("sync", sb3(WBh[:, 0:4096], 4, D), rows3(c_wv[0], 0, 4, D))
                dma("sync", sb3(WBl[:, 0:4096], 4, D), rows3(c_wv[0], 4 * 128, 4, D))
            if oc == NCD:
                dma("sync", sb3(WAh[:, 0:8192], 8, D), rows3(c_wk[0], 0, 8, D))
                dma("sync", sb3(WAl[:, 0:8192], 8, D), rows3(c_wk[1], 0, 8, D))
            if oc >= 1 and oc - 1 < NCD:
                q_relu(oc - 1)
            if oc >= 2:
                q_split(oc - 2)
            if oc == NCD + 1:
                dma("gpsimd", rows3(d_pqh, ti * NCD * 128, 8, TS), sb3(P2h[:, 0:4096], 8, TS))
                dma("gpsimd", rows3(d_pql, ti * NCD * 128, 8, TS), sb3(P2l[:, 0:4096], 8, TS))
            p.bar()
        if ti == 0:
            dbg("pqh0", P2h[:, 0:TS])

        dma("sync", sb3(P2h[:, 0:4096], 4, D), rows3(c_wv[1], 0, 4, D))
        dma("sync", sb3(P2l[:, 0:4096], 4, D), rows3(c_wv[1], 4 * 128, 4, D))
        onesh = bass.AP(U1a, 1024 + 64, [[2112, 128], [65, 16], [1, 1]])
        onesl = bass.AP(U1b, 1024 + 64, [[2112, 128], [65, 16], [1, 1]])
        p.v(lambda e, o=onesh: e.memset(o, 1.0))
        p.v(lambda e, o=onesl: e.memset(o, 0.0))
        p.bar()

        def wvh(kc, hf):
            idx = kc * D + hf * 512
            t = WBh if idx < 4096 else WBl
            return t[:, idx % 4096: idx % 4096 + 512]

        def wvl(kc, hf):
            idx = kc * D + hf * 512
            t = P2h if idx < 4096 else P2l
            return t[:, idx % 4096: idx % 4096 + 512]

        def k_mms(blk):
            for hf in range(2):
                bank = pb[2 + hf]
                for kc in range(NCD):
                    ah = P1h[:, kc * TS + blk * 128: kc * TS + blk * 128 + 128]
                    al = P1l[:, kc * TS + blk * 128: kc * TS + blk * 128 + 128]
                    rh = WAh[:, kc * D + hf * 512: kc * D + hf * 512 + 512]
                    rl = WAl[:, kc * D + hf * 512: kc * D + hf * 512 + 512]
                    mm3(bank[:, 0:512], ah, al, rh, rl, kc == 0, False)
                p.t(lambda e, b=bank, hf=hf:
                    e.matmul(b[:, 0:512], sb_rowpack[0:1, 2048:2176],
                             sb_rowpack[0:1, hf * 512:hf * 512 + 512],
                             start=False, stop=True))

        def v_mms(blk):
            for hf in range(2):
                bank = pb[4 + hf]
                for kc in range(NCD):
                    ah = P1h[:, kc * TS + blk * 128: kc * TS + blk * 128 + 128]
                    al = P1l[:, kc * TS + blk * 128: kc * TS + blk * 128 + 128]
                    mm3(bank[:, 0:512], ah, al, wvh(kc, hf), wvl(kc, hf), kc == 0, False)
                p.t(lambda e, b=bank, hf=hf:
                    e.matmul(b[:, 0:512], sb_rowpack[0:1, 2048:2176],
                             sb_rowpack[0:1, 1024 + hf * 512:1024 + hf * 512 + 512],
                             start=False, stop=True))

        def k_relu(blk):
            for hf in range(2):
                fk = F1[:, (2 + hf) * TS:(3 + hf) * TS]
                p.a(lambda e, b=pb[2 + hf], fk=fk: e.activation(fk, b[:, 0:512], AF.Relu))

        def k_split(blk):
            for hf in range(2):
                fk = F1[:, (2 + hf) * TS:(3 + hf) * TS]
                p.v(lambda e, fk=fk, hf=hf:
                    e.tensor_scalar(U1a[:, hf * 512:hf * 512 + 512], fk, 0.001, None, OP.add))
                p.v(lambda e, fk=fk, hf=hf:
                    e.scalar_tensor_tensor(U1b[:, hf * 512:hf * 512 + 512], fk, 0.001,
                                           U1a[:, hf * 512:hf * 512 + 512], OP.add, OP.subtract))

        def v_split(blk):
            for hf in range(2):
                src_ = bass.AP(pb[4 + hf], 0, [[512, 128], [64, 8], [1, 64]])
                dsth = bass.AP(U1a, 1024 + hf * 8 * 65, [[2112, 128], [65, 8], [1, 64]])
                dstl = bass.AP(U1b, 1024 + hf * 8 * 65, [[2112, 128], [65, 8], [1, 64]])
                p.v(lambda e, s=src_, d=dsth: e.tensor_copy(d, s))
                p.v(lambda e, s=src_, dh=dsth, dl=dstl:
                    e.scalar_tensor_tensor(dl, s, 0.0, dh, OP.add, OP.subtract))

        def kv_mms(blk):
            for pair in range(8):
                bank = pb[pair // 4]
                col = (pair % 4) * 65
                for sub in range(2):
                    h_ = pair * 2 + sub
                    lh = U1a[:, h_ * 64:(h_ + 1) * 64]
                    ll = U1b[:, h_ * 64:(h_ + 1) * 64]
                    rh = U1a[:, 1024 + h_ * 65:1024 + (h_ + 1) * 65]
                    rl = U1b[:, 1024 + h_ * 65:1024 + (h_ + 1) * 65]
                    mm3(bank[sub * 64:(sub + 1) * 64, col:col + 65], lh, ll, rh, rl,
                        True, True)

        def kv_add(blk):
            for half in range(2):
                p.v(lambda e, b=pb[half], half=half:
                    e.tensor_tensor(kvacc[:, half * 260:(half + 1) * 260],
                                    kvacc[:, half * 260:(half + 1) * 260],
                                    b[:, 0:260], OP.add))

        for blk in range(NBLK + 2):
            if blk < NBLK:
                k_mms(blk)
            if 1 <= blk <= NBLK:
                k_split(blk - 1)
                v_split(blk - 1)
            if 2 <= blk:
                kv_add(blk - 2)
            p.bar()
            if blk < NBLK:
                v_mms(blk)
            if 1 <= blk <= NBLK:
                kv_mms(blk - 1)
            if blk < NBLK or (1 <= blk <= NBLK):
                pass
            if blk < NBLK:
                k_relu(blk)
            p.bar()

    dbg("kvpart", kvacc[:])
    dma("gpsimd", d_ccin.ap(), kvacc[:])
    p.bar()
    dbg("ccin", d_ccin.ap())
    p.g(lambda e: e.collective_compute(
        "AllReduce", mybir.AluOpType.add,
        replica_groups=[[0, 1, 2, 3], [4, 5, 6, 7]],
        ins=[d_ccin.ap()], outs=[d_ccout.ap()]))
    p.bar()
    dbg("ccout", d_ccout.ap())
    dma("gpsimd", kvacc[:], d_ccout.ap())
    p.bar()
    dbg("kvback", kvacc[:])
    p.v(lambda e: e.tensor_copy(kvh[:], kvacc[:]))
    p.bar()
    p.v(lambda e: e.tensor_tensor(kvl[:], kvacc[:], kvh[:], OP.subtract))
    p.v(lambda e: e.memset(kselh[:], 0.0))
    p.v(lambda e: e.memset(ksell[:], 0.0))
    p.bar()
    dbg("kvfull", kvacc[:])
    p.vdrain()
    for c in range(NCD):
        p.v(lambda e, c=c: e.tensor_copy(kselh[0:64, c * 16 + 2 * c:c * 16 + 2 * c + 1],
                                         kvh[0:64, c * 65 + 64:c * 65 + 65]))
        p.v(lambda e, c=c: e.tensor_copy(kselh[64:128, c * 16 + 2 * c + 1:c * 16 + 2 * c + 2],
                                         kvh[64:128, c * 65 + 64:c * 65 + 65]))
        p.v(lambda e, c=c: e.tensor_copy(ksell[0:64, c * 16 + 2 * c:c * 16 + 2 * c + 1],
                                         kvl[0:64, c * 65 + 64:c * 65 + 65]))
        p.v(lambda e, c=c: e.tensor_copy(ksell[64:128, c * 16 + 2 * c + 1:c * 16 + 2 * c + 2],
                                         kvl[64:128, c * 65 + 64:c * 65 + 65]))
    p.vdrain()
    p.bar()

    for ti in range(NT):
        tcol = slice(ti * TS, (ti + 1) * TS)
        xc = [X0[c][:, tcol] for c in range(NCD)]
        rc = [R1[:, c * TS:(c + 1) * TS] for c in range(NCD)]

        def eslot(s, kc, off, width):
            c = kc * 1024 + off
            if s == 0:
                return WAh[:, c:c + width]
            if s == 1:
                return WAl[:, c:c + width]
            t = WBh if c < 4096 else WBl
            return t[:, c % 4096: c % 4096 + width]

        def eload(s, cache2d, row0):
            if s == 0:
                dma("sync", sb3(WAh[:, 0:8192], 8, D), rows3(cache2d, row0, 8, D))
            elif s == 1:
                dma("sync", sb3(WAl[:, 0:8192], 8, D), rows3(cache2d, row0, 8, D))
            else:
                dma("sync", sb3(WBh[:, 0:4096], 4, D), rows3(cache2d, row0, 4, D))
                dma("sync", sb3(WBl[:, 0:4096], 4, D), rows3(cache2d, row0 + 4 * 128, 4, D))


        if ti == 0:
            dma("sync", sb3(P1h[:, 0:4096], 8, TS), rows3(d_pqh, 0, 8, TS))
            dma("sync", sb3(P1l[:, 0:4096], 8, TS), rows3(d_pql, 0, 8, TS))
            dma("sync", sb3(WAh[:, 0:8192], 8, D), rows3(c_wo[0], 0, 8, D))
            dma("sync", sb3(WAl[:, 0:8192], 8, D), rows3(c_wo[1], 0, 8, D))
            p.bar()

        for kc in range(NCD):
            rh = P1h[:, kc * TS:(kc + 1) * TS]
            rl = P1l[:, kc * TS:(kc + 1) * TS]
            mm3(pb[2][0:16, 0:TS], kselh[:, kc * 16:(kc + 1) * 16],
                ksell[:, kc * 16:(kc + 1) * 16], rh, rl, kc == 0, kc == NCD - 1)
        p.bar()
        p.v(lambda e: e.reciprocal(F2[0:16, 0:512], pb[2][0:16, 0:TS]))
        p.bar()
        if ti == 0:
            dbg("recip0", F2[0:16, 0:512])

        def attn_evac(c):
            nb_ = pb[c % 2]
            bb_ = pb[3 + c % 2]
            rb = F2[:, 512:1024]
            fa = F1[:, (2 + c % 2) * TS:(3 + c % 2) * TS]
            p.v(lambda e, b=bb_, rb=rb: e.tensor_copy(rb, b[:, 0:TS]))
            p.v(lambda e, b=nb_, rb=rb, fa=fa: e.tensor_tensor(fa, b[:, 0:TS], rb, OP.mult))
            p.v(lambda e, c=c, fa=fa: e.tensor_copy(P2h[:, c * TS:(c + 1) * TS], fa))
            p.v(lambda e, c=c, fa=fa:
                e.tensor_tensor(P2l[:, c * TS:(c + 1) * TS], fa,
                                P2h[:, c * TS:(c + 1) * TS], OP.subtract))

        for c in range(NCD + 1):
            if c < NCD:
                nb_ = pb[c % 2]
                bb_ = pb[3 + c % 2]
                for sub in range(2):
                    lh = kvh[sub * 64:(sub + 1) * 64, c * 65:c * 65 + 64]
                    ll = kvl[sub * 64:(sub + 1) * 64, c * 65:c * 65 + 64]
                    rh = P1h[sub * 64:(sub + 1) * 64, c * TS:(c + 1) * TS]
                    rl = P1l[sub * 64:(sub + 1) * 64, c * TS:(c + 1) * TS]
                    mm3(nb_[sub * 64:(sub + 1) * 64, 0:TS], lh, ll, rh, rl, True, True)
                p.t(lambda e, c=c, b=bb_:
                    e.matmul(b[:, 0:TS], sb_sels[:, 256 + c * 128:256 + (c + 1) * 128],
                             F2[0:16, 0:512], start=True, stop=True))
            if c >= 1:
                attn_evac(c - 1)
            p.bar()

        dma("sync", F2[0:2, 0:512], d_rows.ap()[0:2, tcol])
        p.bar()
        p.t(lambda e: e.matmul(pb[6][:, 0:TS], sb_sels[:, 0:128],
                               F2[0:16, 0:512], start=True, stop=True))
        p.t(lambda e: e.matmul(pb[7][:, 0:TS], sb_sels[:, 128:256],
                               F2[0:16, 0:512], start=True, stop=True))
        p.bar()
        S1B = F1[:, 0:TS]
        MS1B = F1[:, TS:2 * TS]
        p.v(lambda e: e.tensor_scalar(MS1B, pb[6][:, 0:TS], 1.0 / D, None, OP.mult))
        p.bar()
        p.v(lambda e: e.tensor_tensor(S1B, MS1B, MS1B, OP.mult))
        p.bar()
        p.v(lambda e: e.scalar_tensor_tensor(S1B, pb[7][:, 0:TS], 1.0 / D, S1B,
                                             OP.mult, OP.subtract))
        p.bar()
        p.a(lambda e: e.activation(S1B, S1B, AF.Sqrt, bias=sb_vecs[:, VC_E5:VC_E5 + 1]))
        p.bar()
        p.v(lambda e: e.reciprocal(S1B, S1B))
        p.bar()
        p.v(lambda e: e.tensor_tensor(MS1B, MS1B, S1B, OP.mult))
        p.bar()

        for oc in range(NCD):
            bank = pb[oc]
            for kc in range(NCD):
                lh = WAh[:, kc * D + oc * 128: kc * D + oc * 128 + 128]
                ll = WAl[:, kc * D + oc * 128: kc * D + oc * 128 + 128]
                rh = P2h[:, kc * TS:(kc + 1) * TS]
                rl = P2l[:, kc * TS:(kc + 1) * TS]
                mm3(bank[:, 0:TS], lh, ll, rh, rl, kc == 0, kc == NCD - 1)
        p.bar()
        for oc in range(NCD):
            bank = pb[oc]
            t3 = F1[:, 2 * TS:3 * TS]
            p.v(lambda e, t3=t3, xop=X0[oc][:, tcol]:
                e.tensor_tensor(t3, xop, S1B, OP.mult))
            p.bar()
            p.v(lambda e, t3=t3: e.tensor_tensor(t3, t3, MS1B, OP.subtract))
            p.bar()
            p.v(lambda e, oc=oc, b=bank, t3=t3:
                e.scalar_tensor_tensor(R1[:, oc * TS:(oc + 1) * TS],
                                       b[:, 0:TS], sb_vecs[:, VBO + oc:VBO + oc + 1],
                                       t3, OP.add, OP.add))
            p.bar()

        ln_stats(rc, None, VC_E6)
        UB = F1[:, 0:TS]
        QB = F1[:, TS:2 * TS]
        for c in range(NCD):
            t2 = F1[:, 2 * TS:3 * TS]
            yf = F1[:, 3 * TS:4 * TS]
            p.v(lambda e, rcc=rc[c], t2=t2: e.tensor_tensor(t2, rcc, UB, OP.mult))
            p.bar()
            p.v(lambda e, t2=t2: e.tensor_tensor(t2, t2, QB, OP.subtract))
            p.bar()
            p.a(lambda e, c=c, t2=t2, yf=yf:
                e.activation(yf, t2, AF.Identity,
                             bias=sb_vecs[:, VB2 + c:VB2 + c + 1],
                             scale=sb_vecs[:, VG2 + c:VG2 + c + 1]))
            p.bar()
            p.v(lambda e, c=c, yf=yf: e.tensor_copy(P1h[:, c * TS:(c + 1) * TS], yf))
            p.bar()
            p.v(lambda e, c=c, yf=yf:
                e.tensor_tensor(P1l[:, c * TS:(c + 1) * TS], yf,
                                P1h[:, c * TS:(c + 1) * TS], OP.subtract))
            p.bar()

        dma("sync", sb3(WAh[:, 0:4096], 8, 512), rows3(c_w1[0], 0, 8, 512))
        dma("sync", sb3(WAl[:, 0:4096], 8, 512), rows3(c_w1[1], 0, 8, 512))
        dma("sync", sb3(WBh[:, 0:4096], 4, D), rows3(c_w2[0], 0, 4, D))
        dma("sync", sb3(WBl[:, 0:4096], 4, D), rows3(c_w2[1], 0, 4, D))
        p.bar()
        ffn_pending_evac = [None]
        for ft in range(8):
            ha = (ft % 2) * 4096
            for j in range(4):
                bank = pb[j]
                for kc in range(NCD):
                    lh = WAh[:, ha + kc * 512 + j * 128: ha + kc * 512 + j * 128 + 128]
                    ll = WAl[:, ha + kc * 512 + j * 128: ha + kc * 512 + j * 128 + 128]
                    rh = P1h[:, kc * TS:(kc + 1) * TS]
                    rl = P1l[:, kc * TS:(kc + 1) * TS]
                    mm3(bank[:, 0:TS], lh, ll, rh, rl, kc == 0, kc == NCD - 1)
            if ffn_pending_evac[0] is not None:
                for dc in range(4, NCD):
                    bank = pb[dc]
                    rap = R1[:, dc * TS:(dc + 1) * TS]
                    p.v(lambda e, b=bank, rap=rap:
                        e.tensor_tensor(rap, rap, b[:, 0:TS], OP.add))
                ffn_pending_evac[0] = None
            p.bar()
            for j in range(4):
                gf = F1[:, j * TS:(j + 1) * TS]
                p.a(lambda e, b=pb[j], gf=gf, ft=ft, j=j:
                    e.activation(gf, b[:, 0:TS], AF.Gelu_apprx_tanh,
                                 bias=sb_b1c[:, ft * 4 + j: ft * 4 + j + 1]))
            p.bar()
            for j in range(4):
                gf = F1[:, j * TS:(j + 1) * TS]
                p.v(lambda e, j=j, gf=gf: e.tensor_copy(U1a[:, j * TS:(j + 1) * TS], gf))
                p.v(lambda e, j=j, gf=gf:
                    e.scalar_tensor_tensor(U1b[:, j * TS:(j + 1) * TS], gf, 0.0,
                                           U1a[:, j * TS:(j + 1) * TS], OP.add, OP.subtract))
            p.bar()
            if ft < 7:
                ha2 = ((ft + 1) % 2) * 4096
                dma("sync", sb3(WAh[:, ha2:ha2 + 4096], 8, 512),
                    rows3(c_w1[0], (ft + 1) * 1024, 8, 512))
                dma("sync", sb3(WAl[:, ha2:ha2 + 4096], 8, 512),
                    rows3(c_w1[1], (ft + 1) * 1024, 8, 512))
            for dc in range(NCD):
                bank = pb[dc]
                for j in range(4):
                    lh = WBh[:, j * D + dc * 128: j * D + dc * 128 + 128]
                    ll = WBl[:, j * D + dc * 128: j * D + dc * 128 + 128]
                    rh = U1a[:, j * TS:(j + 1) * TS]
                    rl = U1b[:, j * TS:(j + 1) * TS]
                    mm3(bank[:, 0:TS], lh, ll, rh, rl, j == 0, j == 3)
            p.bar()
            if ft < 7:
                dma("sync", sb3(WBh[:, 0:4096], 4, D), rows3(c_w2[0], (ft + 1) * 4 * 128, 4, D))
                dma("sync", sb3(WBl[:, 0:4096], 4, D), rows3(c_w2[1], (ft + 1) * 4 * 128, 4, D))
            for dc in range(4):
                bank = pb[dc]
                rap = R1[:, dc * TS:(dc + 1) * TS]
                if ft == 7:
                    p.v(lambda e, dc=dc, b=bank, rap=rap:
                        e.scalar_tensor_tensor(rap, b[:, 0:TS],
                                               sb_vecs[:, VB2b + dc:VB2b + dc + 1],
                                               rap, OP.add, OP.add))
                else:
                    p.v(lambda e, b=bank, rap=rap:
                        e.tensor_tensor(rap, rap, b[:, 0:TS], OP.add))
            if ft == 7:
                for dc in range(4, NCD):
                    bank = pb[dc]
                    rap = R1[:, dc * TS:(dc + 1) * TS]
                    p.v(lambda e, dc=dc, b=bank, rap=rap:
                        e.scalar_tensor_tensor(rap, b[:, 0:TS],
                                               sb_vecs[:, VB2b + dc:VB2b + dc + 1],
                                               rap, OP.add, OP.add))
            p.bar()
            if ft < 7:
                ffn_pending_evac[0] = ft

        ln_stats(rc, None, VC_E5)
        SB = F1[:, 0:TS]
        QB2 = F1[:, TS:2 * TS]
        for c in range(NCD):
            t2 = F1[:, 2 * TS:3 * TS]
            p.v(lambda e, rcc=rc[c], t2=t2: e.tensor_tensor(t2, rcc, SB, OP.mult))
            p.v(lambda e, t2=t2: e.tensor_tensor(t2, t2, QB2, OP.subtract))
            p.v(lambda e, xcc=xc[c], t2=t2: e.tensor_tensor(xcc, xcc, t2, OP.add))
        p.bar()

        if ti == 0:
            dbg("x1_0", X0[0][:, 0:TS])
        ln_stats(xc, None, VC_E5)
        SB = F1[:, 0:TS]
        QB2 = F1[:, TS:2 * TS]
        for c in range(NCD):
            t2 = F1[:, 2 * TS:3 * TS]
            wf = F1[:, 3 * TS:4 * TS]
            p.v(lambda e, xcc=xc[c], t2=t2: e.tensor_tensor(t2, xcc, SB, OP.mult))
            p.bar()
            p.v(lambda e, t2=t2, wf=wf: e.tensor_tensor(wf, t2, QB2, OP.subtract))
            p.bar()
            p.v(lambda e, c=c, wf=wf: e.tensor_copy(P1h[:, c * TS:(c + 1) * TS], wf))
            p.bar()
            p.v(lambda e, c=c, wf=wf:
                e.tensor_tensor(P1l[:, c * TS:(c + 1) * TS], wf,
                                P1h[:, c * TS:(c + 1) * TS], OP.subtract))
            p.bar()

        for kc in range(NCD):
            rh = P1h[:, kc * TS:(kc + 1) * TS]
            rl = P1l[:, kc * TS:(kc + 1) * TS]
            mm3(pb[2][0:8, 0:TS], sb_rwh[:, kc * 8:(kc + 1) * 8],
                sb_rwl[:, kc * 8:(kc + 1) * 8], rh, rl, kc == 0, kc == NCD - 1)
        p.bar()
        p.v(lambda e: e.tensor_copy(F2[0:8, 0:512], pb[2][0:8, 0:TS]))
        p.bar()
        if ti == 0:
            dbg("rlog", F2[0:8, 0:512])
        for j in range(4):
            p.t(lambda e, j=j: e.transpose(pb[3][:, j * 8:(j + 1) * 8],
                                           F2[0:8, j * 128:(j + 1) * 128],
                                           sb_id[0:8, 0:8]))
        p.bar()
        sc_ = F2[:, 512:1024]
        p.v(lambda e: e.tensor_copy(sc_[:, 0:32], pb[3][:, 0:32]))
        p.bar()
        for g in range(4):
            p.v(lambda e, g=g: e.reduce_max(sc_[:, 32 + g:33 + g],
                                            sc_[:, g * 8:(g + 1) * 8], axis=AX))
        p.bar()
        p.vdrain()
        for g in range(4):
            p.v(lambda e, g=g: e.tensor_scalar(sc_[:, 40 + g * 8:48 + g * 8],
                                               sc_[:, g * 8:(g + 1) * 8],
                                               sc_[:, 32 + g:33 + g], None, OP.subtract))
        p.bar()
        p.a(lambda e: e.activation(sc_[:, 40:72], sc_[:, 40:72], AF.Exp))
        p.bar()
        for g in range(4):
            p.v(lambda e, g=g: e.reduce_sum(sc_[:, 72 + g:73 + g],
                                            sc_[:, 40 + g * 8:48 + g * 8], axis=AX))
        p.bar()
        p.vdrain()
        p.v(lambda e: e.reciprocal(sc_[:, 76:80], sc_[:, 72:76]))
        p.bar()
        p.vdrain()
        for g in range(4):
            p.v(lambda e, g=g: e.tensor_scalar(sc_[:, 80 + g * 8:88 + g * 8],
                                               sc_[:, 40 + g * 8:48 + g * 8],
                                               sc_[:, 76 + g:77 + g], None, OP.mult))
        p.bar()
        p.vdrain()
        for g in range(4):
            p.v(lambda e, g=g: e.reduce_max(sc_[:, 112 + g:113 + g],
                                            sc_[:, 80 + g * 8:88 + g * 8], axis=AX))
        p.bar()
        p.vdrain()
        for g in range(4):
            p.v(lambda e, g=g: e.tensor_scalar(sc_[:, 120 + g * 8:128 + g * 8],
                                               sc_[:, 80 + g * 8:88 + g * 8],
                                               sc_[:, 112 + g:113 + g], None, OP.is_ge))
        p.bar()
        p.vdrain()
        p.v(lambda e: e.tensor_tensor(sc_[:, 120:152], sc_[:, 80:112],
                                      sc_[:, 120:152], OP.mult))
        p.bar()
        p.vdrain()
        p.v(lambda e: e.tensor_tensor(sc_[:, 152:184], sc_[:, 80:112],
                                      sc_[:, 120:152], OP.subtract))
        p.bar()
        p.vdrain()
        for g in range(4):
            p.v(lambda e, g=g: e.reduce_max(sc_[:, 184 + g:185 + g],
                                            sc_[:, 152 + g * 8:160 + g * 8], axis=AX))
        p.bar()
        p.vdrain()
        for g in range(4):
            p.v(lambda e, g=g: e.tensor_scalar(sc_[:, 192 + g * 8:200 + g * 8],
                                               sc_[:, 80 + g * 8:88 + g * 8],
                                               sc_[:, 184 + g:185 + g], None, OP.is_ge))
        p.bar()
        p.vdrain()
        p.v(lambda e: e.tensor_tensor(sc_[:, 224:256], sc_[:, 80:112],
                                      sc_[:, 192:224], OP.mult))
        p.bar()
        if ti == 0:
            dbg("smsc", sc_[:, 0:256])
        for j in range(4):
            p.t(lambda e, j=j: e.transpose(pb[2][0:8, j * 128:(j + 1) * 128],
                                           sc_[:, 224 + j * 8:224 + (j + 1) * 8],
                                           sb_id[:]))
        p.bar()
        p.v(lambda e: e.tensor_copy(F2[0:8, 0:512], pb[2][0:8, 0:512]))
        p.bar()
        eload(0, c_e1[0], 0)
        eload(1, c_e1[1], 0)
        for e_ in range(NE + 2):
            if e_ < NE:
                p.t(lambda e, e_=e_:
                    e.matmul(pb[e_ % 2][:, 0:TS],
                             sb_sels[0:8, 1280 + e_ * 128:1280 + (e_ + 1) * 128],
                             F2[0:8, 0:512], start=True, stop=True))
            if 1 <= e_ <= NE:
                p.v(lambda e, e_=e_ - 1:
                    e.tensor_copy(F1[:, (e_ % 2) * TS:(e_ % 2 + 1) * TS],
                                  pb[e_ % 2][:, 0:TS]))
            if 2 <= e_:
                dma("gpsimd", d_cb.ap()[(e_ - 2) * 128:(e_ - 1) * 128, :],
                    F1[:, ((e_ - 2) % 2) * TS:((e_ - 2) % 2 + 1) * TS])
            p.bar()

        if ti == 0:
            dbg("wh0", P1h[:, 0:TS])
            dbg("comb0", F2[0:8, 0:512])
        def w1_quarter(ei, q, s_h, s_l):
            for oc in (2 * q, 2 * q + 1):
                for kc in range(NCD):
                    lh = eslot(s_h, kc, oc * 128, 128)
                    ll = eslot(s_l, kc, oc * 128, 128)
                    rh = P1h[:, kc * TS:(kc + 1) * TS]
                    rl = P1l[:, kc * TS:(kc + 1) * TS]
                    mm3(pb[oc][:, 0:TS], lh, ll, rh, rl, kc == 0, kc == NCD - 1)

        def silu_q(q):
            for oc in (2 * q, 2 * q + 1):
                p.a(lambda e, b=pb[oc]: e.activation(b[:, 0:TS], b[:, 0:TS], AF.Silu))

        def msplit_q(q):
            for oc in (2 * q, 2 * q + 1):
                gf = F1[:, (oc % 4) * TS:(oc % 4 + 1) * TS]
                p.v(lambda e, b=pb[oc], gf=gf:
                    e.tensor_tensor(gf, b[:, 0:TS], F2[:, 512:1024], OP.mult))
                p.v(lambda e, oc=oc, gf=gf:
                    e.tensor_copy(P2h[:, oc * TS:(oc + 1) * TS], gf))
                p.v(lambda e, oc=oc, gf=gf:
                    e.scalar_tensor_tensor(P2l[:, oc * TS:(oc + 1) * TS], gf, 0.0,
                                           P2h[:, oc * TS:(oc + 1) * TS],
                                           OP.add, OP.subtract))

        def w2_pass(dcs, fcs, s_h, s_l, which):
            for dc in dcs:
                for fc in fcs:
                    lh = eslot(s_h, fc, dc * 128, 128)
                    ll = eslot(s_l, fc, dc * 128, 128)
                    rh = P2h[:, fc * TS:(fc + 1) * TS]
                    rl = P2l[:, fc * TS:(fc + 1) * TS]
                    out = pb[dc][:, 0:TS]
                    if which == "hi":
                        p.t(lambda e, o=out, a=lh, b=rh, f=(fc == 0):
                            e.matmul(o, a, b, start=f, stop=False))
                        p.t(lambda e, o=out, a=lh, b=rl:
                            e.matmul(o, a, b, start=False, stop=False))
                    else:
                        p.t(lambda e, o=out, a=ll, b=rh, s=(fc == NCD - 1):
                            e.matmul(o, a, b, start=False, stop=s))

        def evac_add(ei, dcs):
            for dc in dcs:
                rap = R1[:, dc * TS:(dc + 1) * TS]
                if ei == 0:
                    p.v(lambda e, b=pb[dc], rap=rap: e.tensor_copy(rap, b[:, 0:TS]))
                else:
                    p.v(lambda e, b=pb[dc], rap=rap:
                        e.tensor_tensor(rap, rap, b[:, 0:TS], OP.add))

        for ei in range(NE):
            a_ = ei % 3
            s_e1h, s_e1l, s_e2h, s_e2l = a_, (a_ + 1) % 3, (a_ + 2) % 3, a_
            nxt = (ei + 1) % 3
            w1_quarter(ei, 0, s_e1h, s_e1l)
            if ei > 0:
                evac_add(ei - 1, (2, 3))
            eload(s_e2h, c_e2[0], ei * 8 * 128)
            dma("sync", F2[:, 512:1024], d_cb.ap()[ei * 128:(ei + 1) * 128, :])
            p.bar()
            w1_quarter(ei, 1, s_e1h, s_e1l)
            if ei > 0:
                evac_add(ei - 1, (4, 5))
            silu_q(0)
            p.bar()
            w1_quarter(ei, 2, s_e1h, s_e1l)
            if ei > 0:
                evac_add(ei - 1, (6, 7))
            msplit_q(0)
            silu_q(1)
            p.bar()
            w1_quarter(ei, 3, s_e1h, s_e1l)
            msplit_q(1)
            silu_q(2)
            p.bar()
            w2_pass(range(0, 4), range(0, 4), s_e2h, s_e2l, "hi")
            msplit_q(2)
            silu_q(3)
            eload(s_e2l, c_e2[1], ei * 8 * 128)
            p.bar()
            w2_pass(range(4, 6), range(0, 4), s_e2h, s_e2l, "hi")
            msplit_q(3)
            if ei < NE - 1:
                eload(nxt, c_e1[0], (ei + 1) * 8 * 128)
            p.bar()
            w2_pass(range(6, 8), range(0, 4), s_e2h, s_e2l, "hi")
            w2_pass(range(0, 8), range(4, 8), s_e2h, s_e2l, "hi")
            p.bar()
            w2_pass(range(0, 2), range(0, 8), s_e2h, s_e2l, "lo")
            if ei < NE - 1:
                eload((nxt + 1) % 3, c_e1[1], (ei + 1) * 8 * 128)
            p.bar()
            w2_pass(range(2, 8), range(0, 8), s_e2h, s_e2l, "lo")
            evac_add(ei, (0, 1))
            p.bar()
        evac_add(NE - 1, (2, 3, 4, 5, 6, 7))
        p.bar()

        for kc in range(NCD):
            rh = P1h[:, kc * TS:(kc + 1) * TS]
            rl = P1l[:, kc * TS:(kc + 1) * TS]
            mm3(pb[2][0:1, 0:TS], sb_sgwh[:, kc:kc + 1], sb_sgwl[:, kc:kc + 1],
                rh, rl, kc == 0, kc == NCD - 1)
        dma("sync", sb3(WAh[:, 0:8192], 8, 1024), rows3(c_sgu[0], 0, 8, 1024, c0=0))
        dma("sync", sb3(WAl[:, 0:8192], 8, 1024), rows3(c_sgu[1], 0, 8, 1024, c0=0))
        p.bar()
        p.a(lambda e: e.activation(F2[0:1, 512:1024], pb[2][0:1, 0:TS], AF.Sigmoid))
        p.bar()
        p.t(lambda e: e.matmul(pb[3][:, 0:TS], sb_sels[:, 0:128], F2[0:16, 512:1024],
                               start=True, stop=True))
        p.bar()
        p.v(lambda e: e.tensor_copy(F2[:, 512:1024], pb[3][:, 0:TS]))
        dma("sync", sb3(WBh[:, 0:4096], 8, 512), rows3(c_sgu[0], 0, 8, 512, c0=1024))
        dma("sync", sb3(WBl[:, 0:4096], 8, 512), rows3(c_sgu[1], 0, 8, 512, c0=1024))
        p.bar()
        for half in range(2):
            if half == 1:
                dma("sync", sb3(WBh[:, 0:4096], 8, 512), rows3(c_sgu[0], 0, 8, 512, c0=1536))
                dma("sync", sb3(WBl[:, 0:4096], 8, 512), rows3(c_sgu[1], 0, 8, 512, c0=1536))
                p.bar()
            for q in range(4):
                if q == 0:
                    dma("sync", sb3(P2h[:, 0:4096], 4, D),
                        rows3(c_sdn[0], half * 4 * 128, 4, D))
                    dma("sync", sb3(P2l[:, 0:4096], 4, D),
                        rows3(c_sdn[1], half * 4 * 128, 4, D))
                oc = half * 4 + q
                for kc in range(NCD):
                    lh = WAh[:, kc * 1024 + oc * 128: kc * 1024 + oc * 128 + 128]
                    ll = WAl[:, kc * 1024 + oc * 128: kc * 1024 + oc * 128 + 128]
                    rh = P1h[:, kc * TS:(kc + 1) * TS]
                    rl = P1l[:, kc * TS:(kc + 1) * TS]
                    mm3(pb[0][:, 0:TS], lh, ll, rh, rl, kc == 0, kc == NCD - 1)
                for kc in range(NCD):
                    lh = WBh[:, kc * 512 + q * 128: kc * 512 + q * 128 + 128]
                    ll = WBl[:, kc * 512 + q * 128: kc * 512 + q * 128 + 128]
                    rh = P1h[:, kc * TS:(kc + 1) * TS]
                    rl = P1l[:, kc * TS:(kc + 1) * TS]
                    mm3(pb[1][:, 0:TS], lh, ll, rh, rl, kc == 0, kc == NCD - 1)
                p.bar()
                s3s = F1[:, 2 * TS:3 * TS]
                p.a(lambda e, s3s=s3s: e.activation(s3s, pb[0][:, 0:TS], AF.Silu))
                p.bar()
                p.v(lambda e, s3s=s3s: e.tensor_tensor(s3s, s3s, pb[1][:, 0:TS], OP.mult))
                p.bar()
                p.v(lambda e, s3s=s3s: e.tensor_tensor(s3s, s3s, F2[:, 512:1024], OP.mult))
                p.bar()
                p.v(lambda e, q=q, s3s=s3s:
                    e.tensor_copy(U1a[:, q * 512:(q + 1) * 512], s3s))
                p.v(lambda e, q=q, s3s=s3s:
                    e.scalar_tensor_tensor(U1b[:, q * 512:(q + 1) * 512], s3s, 0.0,
                                           U1a[:, q * 512:(q + 1) * 512], OP.add, OP.subtract))
                p.bar()
            for dc in range(NCD):
                for q in range(4):
                    lh = P2h[:, q * D + dc * 128: q * D + dc * 128 + 128]
                    ll = P2l[:, q * D + dc * 128: q * D + dc * 128 + 128]
                    rh = U1a[:, q * 512 + 0: q * 512 + 512]
                    rl = U1b[:, q * 512 + 0: q * 512 + 512]
                    mm3(pb[dc][:, 0:TS], lh, ll, rh, rl, q == 0, q == 3)
            p.bar()
            for dc in range(NCD):
                rap = R1[:, dc * TS:(dc + 1) * TS]
                p.v(lambda e, b=pb[dc], rap=rap:
                    e.tensor_tensor(rap, rap, b[:, 0:TS], OP.add))
            p.bar()

        if ti < NT - 1:
            dma("sync", sb3(P1h[:, 0:4096], 8, TS), rows3(d_pqh, (ti + 1) * NCD * 128, 8, TS))
            dma("sync", sb3(P1l[:, 0:4096], 8, TS), rows3(d_pql, (ti + 1) * NCD * 128, 8, TS))
            dma("sync", sb3(WAh[:, 0:8192], 8, D), rows3(c_wo[0], 0, 8, D))
            dma("sync", sb3(WAl[:, 0:8192], 8, D), rows3(c_wo[1], 0, 8, D))
        ln_stats(rc, None, VC_E5)
        SB = F1[:, 0:TS]
        QB2 = F1[:, TS:2 * TS]
        for c in range(NCD):
            t2 = F1[:, 2 * TS:3 * TS]
            p.v(lambda e, rcc=rc[c], t2=t2: e.tensor_tensor(t2, rcc, SB, OP.mult))
            p.v(lambda e, t2=t2: e.tensor_tensor(t2, t2, QB2, OP.subtract))
            p.v(lambda e, xcc=xc[c], t2=t2: e.tensor_tensor(xcc, xcc, t2, OP.add))
        p.bar()

        for blk in range(NBLK):
            for c in range(NCD):
                bank = pb[c // 4]
                p.t(lambda e, c=c, b=bank,
                    xap=X0[c][:, ti * TS + blk * 128: ti * TS + blk * 128 + 128]:
                    e.transpose(b[:, (c % 4) * 128:(c % 4) * 128 + 128], xap, sb_id[:]))
            p.bar()
            p.v(lambda e: e.tensor_copy(F2[:, 0:512], pb[0][:, 0:512]))
            p.v(lambda e: e.tensor_copy(F2[:, 512:1024], pb[1][:, 0:512]))
            p.bar()
            dma("gpsimd", d_out.ap()[ti * TS + blk * 128: ti * TS + (blk + 1) * 128, :],
                F2[:, 0:1024])
            p.bar()

    with (
        nc.semaphore("s_dma_sync") as s1,
        nc.semaphore("s_dma_gpsimd") as s2,
        nc.semaphore("s_x_gpsimd") as s3,
        nc.semaphore("s_x_vector") as s4,
        nc.semaphore("s_x_scalar") as s5,
        nc.semaphore("s_x_tensor") as s6,
        nc.semaphore("s_x_vector_d") as s7,
        nc.Block() as block,
    ):
        sems = {"dma_sync": s1, "dma_gpsimd": s2, "x_gpsimd": s3,
                "x_vector": s4, "x_scalar": s5, "x_tensor": s6,
                "x_vector_d": s7}
        p.emit(block, sems)

    return nc


def host_prep(inp):
    f32 = np.float32
    x = np.ascontiguousarray(np.asarray(inp["L0_attn_inputs"], f32)).reshape(2, 8192, D)
    wq = np.ascontiguousarray(np.asarray(inp["L0_attn_Wq"], f32).reshape(D, D))
    wk = np.ascontiguousarray(np.asarray(inp["L0_attn_Wk"], f32).reshape(D, D))
    wv = np.ascontiguousarray(np.asarray(inp["L0_attn_Wv"], f32).reshape(D, D))
    wo = np.ascontiguousarray(np.asarray(inp["L0_attn_Wo"], f32).reshape(D, D))
    w1 = np.asarray(inp["L0_attn_W1"], f32)
    w1t = np.ascontiguousarray(
        w1.reshape(8, 128, 8, 512).transpose(2, 0, 1, 3).reshape(8 * 8 * 128, 512))
    w2 = np.ascontiguousarray(np.asarray(inp["L0_attn_W2"], f32))
    e1 = np.ascontiguousarray(np.asarray(inp["L0_ffn_w1"], f32).reshape(NE * D, 1024))
    e2 = np.ascontiguousarray(np.asarray(inp["L0_ffn_w2"], f32).reshape(NE * D, 1024))
    sgu = np.ascontiguousarray(np.asarray(inp["L0_ffn_shared_gate_up"], f32))
    sdn = np.ascontiguousarray(np.asarray(inp["L0_ffn_shared_down"], f32))
    rw = np.ascontiguousarray(np.asarray(inp["L0_ffn_router_w"], f32)
                              .reshape(8, 128, 8).transpose(1, 0, 2).reshape(128, 64))
    sgw = np.ascontiguousarray(np.asarray(inp["L0_ffn_shared_gate_w"], f32)
                               .reshape(8, 128).T)

    def col8(v):
        return np.ascontiguousarray(np.asarray(v, f32).reshape(8, 128).T)

    vecs = np.zeros((128, 64), f32)
    vecs[:, VG1:VG1 + 8] = col8(inp["L0_attn_ln1_scale"])
    vecs[:, VB1:VB1 + 8] = col8(inp["L0_attn_ln1_bias"])
    vecs[:, VG2:VG2 + 8] = col8(inp["L0_attn_ln2_scale"])
    vecs[:, VB2:VB2 + 8] = col8(inp["L0_attn_ln2_bias"])
    vecs[:, VBQ:VBQ + 8] = col8(np.asarray(inp["L0_attn_bq"], f32).reshape(D))
    vecs[:, VBO:VBO + 8] = col8(inp["L0_attn_bo"])
    vecs[:, VB2b:VB2b + 8] = col8(inp["L0_attn_b2"])
    vecs[:, VC_LN1B] = 1e-11
    vecs[:, VC_LN1S] = 1.0 + 1e-6
    vecs[:, VC_E5] = 1e-5
    vecs[:, VC_E6] = 1e-6
    b1c = np.ascontiguousarray(np.asarray(inp["L0_attn_b1"], f32).reshape(32, 128).T)
    rowpack = np.zeros((1, 2176), f32)
    rowpack[0, 0:D] = np.asarray(inp["L0_attn_bk"], f32).reshape(D)
    rowpack[0, D:2 * D] = np.asarray(inp["L0_attn_bv"], f32).reshape(D)
    rowpack[0, 2048:2176] = 1.0

    ident = np.eye(128, dtype=f32)
    sels = np.zeros((16, 2304), f32)
    for j in range(2):
        sels[j, j * 128:(j + 1) * 128] = 1.0
    for c in range(8):
        sels[2 * c, 256 + c * 128:256 + c * 128 + 64] = 1.0
        sels[2 * c + 1, 256 + c * 128 + 64:256 + (c + 1) * 128] = 1.0
    for e_ in range(8):
        sels[e_, 1280 + e_ * 128:1280 + (e_ + 1) * 128] = 1.0
    onesel = np.zeros((128, 32), f32)
    onesel[:, 0] = 1.0
    onesel[:, 17] = 1.0

    shared = dict(wq=wq, wk=wk, wv=wv, wo=wo, w1t=w1t, w2=w2, e1=e1, e2=e2,
                  sgu=sgu, sdn=sdn, rw=rw, sgw=sgw, vecs=vecs, b1c=b1c,
                  rowpack=rowpack, ident=ident, sels=sels, onesel=onesel)
    in_maps = []
    for core in range(8):
        b = core // 4
        s0 = (core % 4) * TPC
        m = dict(shared)
        m["x"] = np.ascontiguousarray(x[b, s0:s0 + TPC])
        in_maps.append(m)
    return in_maps


def kernel(**inp):
    in_maps = host_prep(inp)
    nc = build_nc()
    res = run_bass_kernel_spmd(nc, in_maps, core_ids=list(range(8)))
    out = np.empty((2, 8192, D), np.float32)
    for core in range(8):
        b = core // 4
        s0 = (core % 4) * TPC
        out[b, s0:s0 + TPC] = np.asarray(res.results[core]["out"], np.float32)
    return out
